# version 03
# baseline (speedup 1.0000x reference)
import numpy as np
import ml_dtypes
from contextlib import ExitStack
import concourse.bass as bass
import concourse.mybir as mybir
from concourse.bass_utils import run_bass_kernel_spmd

F32 = mybir.dt.float32
BF16 = mybir.dt.bfloat16
AF = mybir.ActivationFunctionType
ALU = mybir.AluOpType
AX = mybir.AxisListType

TOK = 2048
NT = TOK // 128
DM = 1024
NC8 = 8
NS = 4

ISR128 = float(1.0 / np.sqrt(128.0))
ISR32 = float(1.0 / np.sqrt(32.0))
ISR64 = float(1.0 / np.sqrt(64.0))
EPS_LN = 1e-5
_DBG_MODE = "full"


class Sem:
    __slots__ = ("h", "n")

    def __init__(self, handle):
        self.h = handle
        self.n = 0

    def inc(self, instr, k=1):
        instr.then_inc(self.h, k)
        self.n += k
        return self.n


class CE:

    _RAW = {"wait_ge", "dma_start", "nop"}

    def __init__(self, eng, chain):
        self._e = eng
        self._c = chain

    @property
    def n(self):
        return self._c.n

    def w(self, sem, val):
        self._e.wait_ge(sem.h, val)

    def raw(self):
        return self._e

    def __getattr__(self, name):
        f = getattr(self._e, name)
        if name in CE._RAW:
            return f

        def g(*a, **k):
            self._e.wait_ge(self._c.h, self._c.n)
            ins = f(*a, **k)
            return self._c.inc(ins)
        return g


class P:
    def __init__(self, nc, ctx):
        self.nc = nc
        self.ctx = ctx
        self._sems = {}

    def sem(self, name):
        if name not in self._sems:
            self._sems[name] = Sem(self.ctx.enter_context(self.nc.semaphore(name)))
        return self._sems[name]


def fence(blk, waits, engines=("tensor", "scalar", "vector")):
    def body(e):
        for sem, val in waits:
            e.wait_ge(sem.h, val)
    for name in engines:
        getattr(blk, name)(body)


def plan_stream(state, nt=NT, store=True):
    XI, XO = state["XI"], state["XO"]
    xi_n = [s.n for s in XI]
    xi_val, gate = [], []
    prev = [list(fr) for fr in state["slot_free"]]
    xo_val = [None] * nt
    if store:
        xo_n = [s.n for s in XO]
    for t in range(nt):
        s = t % NS
        gate.append(list(prev[s]))
        xi_n[s] += 16
        xi_val.append(xi_n[s])
        if store:
            xo_n[s] += 16
            xo_val[t] = xo_n[s]
            prev[s] = [(XO[s], xo_n[s])]
        else:
            prev[s] = []
    if store:
        state["slot_free"] = prev
    return xi_val, xo_val, gate


def build_program(upto=4):
    nc = bass.Bass("TRN2", target_bir_lowering=False)
    ctx = ExitStack()
    p = P(nc, ctx)

    dr = {}
    def din(name, shape):
        dr[name] = nc.dram_tensor(name, list(shape), F32, kind="ExternalInput").ap()

    din("x", (TOK, DM))
    din("aw_inproj", (3072, DM)); din("aw_inb", (1, 3072))
    din("aw_outw", (DM, DM)); din("aw_outb", (1, DM))
    din("bw_q", (DM, DM)); din("bw_k", (DM, DM)); din("bw_v", (DM, DM)); din("bw_o", (DM, DM))
    din("bb_q", (1, DM)); din("bb_k", (1, DM)); din("bb_v", (1, DM)); din("bb_o", (1, DM))
    din("fw_q", (DM, DM)); din("fw_k", (DM, 512)); din("fw_v", (DM, 512)); din("fw_o", (DM, DM))
    din("fb_q", (1, DM)); din("fb_k", (1, 512)); din("fb_v", (1, 512)); din("fb_o", (1, DM))
    din("fw_r", (DM, 8)); din("fb_r", (1, 8))
    din("fw_1", (8, DM, 2048)); din("fb_1", (8, 2048))
    din("fw_2", (8, 2048, DM)); din("fb_2", (8, DM))
    din("c_ident", (128, 128))
    din("c_cos", (128, TOK)); din("c_sinm", (128, TOK))
    din("c_mask", (4, 128, 512)); din("c_sel4", (4, 128))
    out_hbm = nc.dram_tensor("out", [TOK, DM], F32, kind="ExternalOutput").ap()
    xsA = nc.dram_tensor("xsA", [TOK, DM], F32, kind="Internal").ap()
    xsB = nc.dram_tensor("xsB", [TOK, DM], F32, kind="Internal").ap()

    with ctx:
        emit(p, dr, out_hbm, xsA, xsB, upto)
    return nc


def emit(p, dr, out_hbm, xsA, xsB, upto):
    nc = p.nc
    blk = p.ctx.enter_context(nc.Block())
    alloc = lambda n, sh, dt=F32: p.ctx.enter_context(nc.sbuf_tensor(n, sh, dt))

    xsl = alloc("xsl", [128, NS, DM], F32)
    identb = alloc("identb", [128, 128], BF16)
    identf = alloc("identf", [128, 128], F32)
    onesf = alloc("onesf", [1, 128], F32)
    onesf2 = alloc("onesf2", [128, 64], F32)
    onesb = alloc("onesb", [128, 1], BF16)

    CD = p.sem("CD")
    SC = p.sem("SC"); VC = p.sem("VC"); GC = p.sem("GC")
    TM = p.sem("TM")
    XI = [p.sem(f"XI{s}") for s in range(NS)]
    XO = [p.sem(f"XO{s}") for s in range(NS)]
    WID = p.sem("WID")

    @blk.sync
    def _(y):
        CD.inc(y.dma_start(out=identf[:, :], in_=dr["c_ident"]), 16)

    ge_ones = [None]

    @blk.gpsimd
    def _(g_):
        g = CE(g_, GC)
        WID.inc(g_.dma_start(out=identb[:, :], in_=dr["c_ident"]), 16)
        g.memset(onesf[:, :], 1.0)
        g.memset(onesf2[:, :], 1.0)
        ge_ones[0] = g.memset(onesb[:, :], 1.0)
    ge_ones = ge_ones[0]

    bcol_a = alloc("bcol_a", [128, 16], F32)
    bcol_fo = alloc("bcol_fo", [128, 8], F32)
    bcol_fq = alloc("bcol_fq", [128, 4], F32)
    bcol_fk = alloc("bcol_fk", [128, 4], F32)
    bcol_fv2 = alloc("bcol_fv2", [128, 4], F32)
    bcol_bq = alloc("bcol_bq", [128, 8], F32)
    bcol_bk = alloc("bcol_bk", [128, 8], F32)
    bcol_f1 = alloc("bcol_f1", [128, 8, 16], F32)
    bcol_bve = alloc("bcol_bve", [64, 8], F32)
    bcol_bvo = alloc("bcol_bvo", [64, 8], F32)
    sel4f = alloc("sel4f", [128, 128], F32)
    borep_b = alloc("borep_b", [128, DM], F32)
    rbrep = alloc("rbrep", [128, 8], F32)
    fb2n = alloc("fb2n", [8, DM], BF16)
    arep = ExitStack()
    bvrep_a = arep.enter_context(nc.sbuf_tensor("bvrep_a", [128, DM], F32))
    borep_a2 = arep.enter_context(nc.sbuf_tensor("borep_a2", [128, DM], F32))

    with ExitStack() as pc:
        brows = [("aw_inb", 3072), ("aw_outb", DM), ("bb_q", DM), ("bb_k", DM),
                 ("bb_v", DM), ("bb_o", DM), ("fb_q", DM), ("fb_k", 512),
                 ("fb_v", 512), ("fb_o", DM), ("fb_r", 8)]
        brow = {}
        for name, n in brows:
            brow[name] = pc.enter_context(nc.sbuf_tensor("brow_" + name, [1, n], F32))
        fb1row = pc.enter_context(nc.sbuf_tensor("fb1row", [1, 16384], F32))
        fb2n32 = pc.enter_context(nc.sbuf_tensor("fb2n32", [8, DM], F32))
        qbrow = pc.enter_context(nc.sbuf_tensor("qbrow", [1, 512], F32))
        pbias = pc.enter_context(nc.psum_tensor("pbias", [128, 2, 512], F32))
        prep = pc.enter_context(nc.psum_tensor("prep", [128, 2, 512], F32))

        @blk.sync
        def _(y):
            for name, n in brows:
                CD.inc(y.dma_start(out=brow[name][:, :], in_=dr[name]), 16)
            CD.inc(y.dma_start(out=fb1row[:, :],
                               in_=dr["fb_1"].rearrange("e f -> (e f)")
                               .rearrange("(o f) -> o f", o=1)), 16)
            CD.inc(y.dma_start(out=fb2n32[:, :], in_=dr["fb_2"]), 16)
            CD.inc(y.dma_start(out=sel4f[0:4, :], in_=dr["c_sel4"]), 16)

        ve_pro = [None]

        @blk.vector
        def _(v_):
            v = CE(v_, VC)
            v.w(CD, CD.n)
            v.tensor_copy(fb2n[:, :], fb2n32[:, :])
            ve_pro[0] = v.tensor_copy(
                qbrow[:, :].rearrange("o (h s) -> o h s", s=32),
                brow["fb_q"][:, :].rearrange("o (h s) -> o h s", s=64)[:, :, 0:32])
        ve_pro = ve_pro[0]

        jobs = []
        for c in range(16):
            jobs.append((bcol_a[:, c:c + 1], brow["aw_inb"][0:1, 128 * c:128 * (c + 1)], 128))
        for c in range(8):
            jobs.append((bcol_fo[:, c:c + 1], brow["fb_o"][0:1, 128 * c:128 * (c + 1)], 128))
        for cc in range(4):
            jobs.append((bcol_fq[0:64, cc:cc + 1], qbrow[0:1, 128 * cc:128 * cc + 64], 64))
            jobs.append((bcol_fq[64:128, cc:cc + 1], qbrow[0:1, 128 * cc + 64:128 * (cc + 1)], 64))
            jobs.append((bcol_fk[0:64, cc:cc + 1], brow["fb_k"][0:1, 128 * cc:128 * cc + 64], 64))
            jobs.append((bcol_fk[64:128, cc:cc + 1], brow["fb_k"][0:1, 128 * cc + 64:128 * (cc + 1)], 64))
            jobs.append((bcol_fv2[0:64, cc:cc + 1], brow["fb_v"][0:1, 128 * cc:128 * cc + 64], 64))
            jobs.append((bcol_fv2[64:128, cc:cc + 1], brow["fb_v"][0:1, 128 * cc + 64:128 * (cc + 1)], 64))
        for c in range(8):
            jobs.append((bcol_bq[:, c:c + 1], brow["bb_q"][0:1, 128 * c:128 * (c + 1)], 128))
            jobs.append((bcol_bk[:, c:c + 1], brow["bb_k"][0:1, 128 * c:128 * (c + 1)], 128))
        for g in range(8):
            jobs.append((bcol_bve[0:64, g:g + 1], brow["bb_v"][0:1, 128 * g:128 * g + 64], 64))
            jobs.append((bcol_bvo[0:64, g:g + 1], brow["bb_v"][0:1, 128 * g + 64:128 * (g + 1)], 64))
        for e in range(8):
            for c in range(16):
                jobs.append((bcol_f1[:, e, c:c + 1],
                             fb1row[0:1, 2048 * e + 128 * c:2048 * e + 128 * (c + 1)], 128))

        tm0 = TM.n
        sc0 = SC.n
        if _DBG_MODE == "nobias":
            jobs = []
        elif _DBG_MODE.startswith("jobs:"):
            a, b = _DBG_MODE[5:].split(":")
            jobs = jobs[int(a):int(b)]

        @blk.tensor
        def _(t_):
            t_.wait_ge(CD.h, CD.n)
            t_.wait_ge(GC.h, ge_ones)
            t_.wait_ge(VC.h, ve_pro)
            for i, (dst, src, m) in enumerate(jobs):
                g = i // 16
                if g >= 2 and i % 16 == 0:
                    t_.wait_ge(SC.h, sc0 + 16 * (g - 1))
                TM.inc(t_.matmul(pbias[0:m, g % 2, (i % 16):(i % 16) + 1],
                                 src, onesf[0:1, 0:1], start=True, stop=True))

        @blk.scalar
        def _(s_):
            s = CE(s_, SC)
            for i, (dst, src, m) in enumerate(jobs):
                g = i // 16
                s.w(TM, tm0 + min(len(jobs), (g + 1) * 16))
                s.copy(dst, pbias[0:m, g % 2, (i % 16):(i % 16) + 1])
        assert SC.n == sc0 + len(jobs)

        reps = []
        for j in range(2):
            reps.append((bvrep_a[:, 512 * j:512 * (j + 1)],
                         brow["aw_inb"][0:1, 2048 + 512 * j:2048 + 512 * (j + 1)], 512, 1.0))
        for j in range(2):
            reps.append((borep_a2[:, 512 * j:512 * (j + 1)],
                         brow["aw_outb"][0:1, 512 * j:512 * (j + 1)], 512, 2.0))

        for j in range(2):
            reps.append((borep_b[:, 512 * j:512 * (j + 1)],
                         brow["bb_o"][0:1, 512 * j:512 * (j + 1)], 512, 1.0))
        reps.append((rbrep[:, :], brow["fb_r"][0:1, :], 8, 1.0))

        tr0 = TM.n
        sr0 = SC.n
        if _DBG_MODE in ("nobias", "norep"):
            reps = []

        @blk.tensor
        def _(t_):
            for i, (dst, src, w, sc) in enumerate(reps):
                if i >= 2:
                    t_.wait_ge(SC.h, sr0 + (i - 1))
                TM.inc(t_.matmul(prep[:, i % 2, 0:w], onesf[0:1, :], src,
                                 start=True, stop=True))

        @blk.scalar
        def _(s_):
            s = CE(s_, SC)
            for i, (dst, src, w, sc) in enumerate(reps):
                s.w(TM, tr0 + i + 1)
                if sc == 1.0:
                    s.copy(dst, prep[:, i % 2, 0:w])
                else:
                    s.activation(dst, prep[:, i % 2, 0:w], AF.Copy, scale=sc)

        fence(blk, [(SC, SC.n), (TM, TM.n), (VC, ve_pro), (WID, 16)])

    cn = {"xsl": xsl, "wstage": None, "identb": identb, "identf": identf,
          "onesf": onesf, "onesf2": onesf2, "onesb": onesb, "sel4f": sel4f}
    state = {"wfree": None, "XI": XI, "XO": XO,
             "slot_free": [[] for _ in range(NS)]}
    state["phase_done"] = [(SC, SC.n), (TM, TM.n), (VC, VC.n)]

    seq = []
    if upto >= 1:
        seq.append(("a", xsA, xsB if upto > 1 else out_hbm))
    if upto >= 2:
        seq.append(("f1", xsA, xsB if upto > 2 else out_hbm))
    if upto >= 3:
        seq.append(("b", xsA, xsB if upto > 3 else out_hbm))
    if upto >= 4:
        seq.append(("f2", xsA, out_hbm))

    xin = dr["x"]
    for tag, xmid, xout in seq:
        if tag == "a":
            emit_attn_a(p, blk, dr, cn, bcol_a, bvrep_a, borep_a2,
                        state, xin, xmid, xout)
            arep.close()
        elif tag in ("f1", "f2"):
            emit_ffn(p, blk, dr, cn, bcol_fq, bcol_fk, bcol_fo, bcol_f1,
                     bcol_fv2, rbrep, fb2n, state, tag, xin, xmid, xout)
        else:
            emit_attn_b(p, blk, dr, cn, bcol_bq, bcol_bk, (bcol_bve, bcol_bvo), borep_b,
                        state, xin, xmid, xout)
        xin = xout

    if upto == 0:
        xi_val, xo_val, gate = plan_stream(state)

        @blk.sync
        def _(y):
            for t in range(NT):
                s = t % NS
                for sem, val in gate[t]:
                    y.wait_ge(sem.h, val)
                v = XI[s].inc(y.dma_start(out=xsl[:, s, :],
                                          in_=dr["x"][128 * t:128 * (t + 1), :]), 16)
                assert v == xi_val[t]
                y.wait_ge(XI[s].h, xi_val[t])
                v = XO[s].inc(y.dma_start(out=out_hbm[128 * t:128 * (t + 1), :],
                                          in_=xsl[:, s, :]), 16)
                assert v == xo_val[t]

    @blk.sync
    def _(y):
        for fr in state["slot_free"]:
            for sem, val in fr:
                y.wait_ge(sem.h, val)


def emit_ln(p, blk, cn, state, lnT, tag, xin, xmid, dup=0, add_hs=False):
    nc = p.nc
    SC, VC, TM = p.sem("SC"), p.sem("VC"), p.sem("TM")
    XI, XO = state["XI"], state["XO"]
    xsl, identb = cn["xsl"], cn["identb"]

    with ExitStack() as c:
        sums = c.enter_context(nc.sbuf_tensor(f"ln_sums_{tag}", [128, NT], F32))
        sumsq = c.enter_context(nc.sbuf_tensor(f"ln_sumsq_{tag}", [128, NT], F32))
        means = c.enter_context(nc.sbuf_tensor(f"ln_means_{tag}", [128, NT], F32))
        rstds = c.enter_context(nc.sbuf_tensor(f"ln_rstds_{tag}", [128, NT], F32))
        s1p = c.enter_context(nc.sbuf_tensor(f"ln_s1p_{tag}", [128, NT], F32))
        s2p = c.enter_context(nc.sbuf_tensor(f"ln_s2p_{tag}", [128, NT], F32))
        vscr = c.enter_context(nc.sbuf_tensor(f"ln_vscr_{tag}", [128, 4, NT], F32))
        sqscr = c.enter_context(nc.sbuf_tensor(f"ln_sqscr_{tag}", [128, DM], F32))
        lnscr = c.enter_context(nc.sbuf_tensor(f"ln_lnscr_{tag}", [128, 1], F32))
        lnbf = c.enter_context(nc.sbuf_tensor(f"ln_lnbf_{tag}", [128, 2, DM], BF16))
        psT = c.enter_context(nc.psum_tensor(f"ln_psT_{tag}", [128, 2, DM], BF16))

        sc0, vc0, tm0 = SC.n, VC.n, TM.n
        xi_val, xo_val, gate = plan_stream(state, store=add_hs)

        nv3 = 4 if add_hs else 1
        sc_sq, sc_rstd, sc_evac = {}, {}, {}
        vc_sums, vc_veps, vc_xup = {}, {}, {}
        n = sc0
        for u in range(NT + 2):
            if u < NT:
                n += 1; sc_sq[u] = n
            if 1 <= u <= NT:
                n += 2; sc_rstd[u - 1] = n
            if 2 <= u:
                n += 1; sc_evac[u - 2] = n
        m = vc0
        for u in range(NT + 2):
            if u < NT:
                m += 1; vc_sums[u] = m
            if 1 <= u <= NT:
                m += 4; vc_veps[u - 1] = m
            if 2 <= u:
                m += nv3; vc_xup[u - 2] = m
        tm_tr = lambda t: tm0 + t + 1
        if not add_hs:
            fr = [[] for _ in range(NS)]
            for t in range(NT):
                fr[t % NS] = [(VC, vc_xup[t]), (SC, sc_sq[t])]
            state["slot_free"] = fr

        @blk.sync
        def _(y):
            for t in range(NT):
                s = t % NS
                if t >= NS and not add_hs:
                    y.wait_ge(VC.h, vc_xup[t - NS])
                    y.wait_ge(SC.h, sc_sq[t - NS])
                for sem, val in gate[t]:
                    y.wait_ge(sem.h, val)
                v = XI[s].inc(y.dma_start(out=xsl[:, s, :],
                                          in_=xin[128 * t:128 * (t + 1), :]), 16)
                assert v == xi_val[t]

        @blk.scalar
        def _(s_):
            s = CE(s_, SC)
            for u in range(NT + 2):
                if u < NT:
                    s.w(XI[u % NS], xi_val[u])
                    r = s.activation(sqscr[:, :], xsl[:, u % NS, :], AF.Square,
                                     accum_out=sumsq[:, u:u + 1])
                    assert r == sc_sq[u]
                if 1 <= u <= NT:
                    t = u - 1
                    s.w(VC, vc_veps[t])
                    s.activation(lnscr[:, 0:1], vscr[:, 2, t:t + 1], AF.Ln)
                    r = s.activation(rstds[:, t:t + 1], lnscr[:, 0:1], AF.Exp, scale=-0.5)
                    assert r == sc_rstd[t]
                if 2 <= u:
                    t = u - 2
                    s.w(TM, tm_tr(t))
                    r = s.copy(lnT[:, :, 128 * t:128 * (t + 1)],
                               psT[:, t % 2, :].rearrange("p (c t) -> p c t", t=128))
                    assert r == sc_evac[t]
                    if add_hs:
                        s_.wait_ge(VC.h, vc_xup[t])
                        v = XO[t % NS].inc(s_.dma_start(out=xmid[128 * t:128 * (t + 1), :],
                                                        in_=xsl[:, t % NS, :]), 16)
                        assert v == xo_val[t]

        @blk.vector
        def _(v_):
            v = CE(v_, VC)
            for u in range(NT + 2):
                if u < NT:
                    v.w(XI[u % NS], xi_val[u])
                    r = v.reduce_sum(sums[:, u:u + 1], xsl[:, u % NS, :], axis=AX.X)
                    assert r == vc_sums[u]
                if 1 <= u <= NT:
                    t = u - 1
                    v.w(SC, sc_sq[t])
                    v.tensor_scalar(means[:, t:t + 1], sums[:, t:t + 1], 1.0 / DM, None, ALU.mult)
                    v.tensor_scalar(vscr[:, 0, t:t + 1], means[:, t:t + 1], means[:, t:t + 1],
                                    None, ALU.mult)
                    v.tensor_scalar(vscr[:, 1, t:t + 1], sumsq[:, t:t + 1], 1.0 / DM, EPS_LN,
                                    ALU.mult, ALU.add)
                    r = v.tensor_sub(vscr[:, 2, t:t + 1], vscr[:, 1, t:t + 1], vscr[:, 0, t:t + 1])
                    assert r == vc_veps[t]
                if 2 <= u:
                    t = u - 2
                    v.w(SC, sc_rstd[t])
                    if add_hs:
                        v.tensor_scalar(s1p[:, t:t + 1], rstds[:, t:t + 1], 1.0, None, ALU.add)
                        v.tensor_scalar(s2p[:, t:t + 1], means[:, t:t + 1], rstds[:, t:t + 1],
                                        -1.0, ALU.mult, ALU.mult)
                    if t >= 2:
                        v.w(TM, tm_tr(t - 2))
                    r = v.tensor_scalar(lnbf[:, t % 2, :], xsl[:, t % NS, :],
                                        means[:, t:t + 1], rstds[:, t:t + 1],
                                        ALU.subtract, ALU.mult)
                    if add_hs:
                        r = v.tensor_scalar(xsl[:, t % NS, :], xsl[:, t % NS, :],
                                            s1p[:, t:t + 1], s2p[:, t:t + 1],
                                            ALU.mult, ALU.add)
                    assert r == vc_xup[t]

        @blk.tensor
        def _(t_):
            for t in range(NT):
                t_.wait_ge(VC.h, vc_xup[t])
                if t >= 2:
                    t_.wait_ge(SC.h, sc_evac[t - 2])
                for cc in range(8):
                    ins = t_.transpose(psT[:, t % 2, 128 * cc:128 * (cc + 1)],
                                       lnbf[:, t % 2, 128 * cc:128 * (cc + 1)],
                                       identb[:, :])
                TM.inc(ins)
        assert TM.n == tm_tr(NT - 1)

        dup_dep = None
        if dup:
            vdup = [None]

            @blk.vector
            def _(v_):
                v = CE(v_, VC)
                v.w(SC, sc_evac[NT - 1])
                vdup[0] = v.tensor_copy(lnT[:, :, TOK:TOK + dup], lnT[:, :, 0:dup])
            dup_dep = (VC, vdup[0])

        ln_done = [(VC, VC.n), (SC, SC.n), (TM, TM.n)]
        return (SC, sc_evac[NT - 1]), dup_dep, ln_done


def jit_w(p, blk, dr_w, row0, nchunk, cn, state, psW, dst):
    TA, SC = p.sem("TA"), p.sem("SC")
    WJ = [p.sem("WJA"), p.sem("WJB")]
    wstage, identb = cn["wstage"], cn["identb"]
    ta0 = TA.n
    sc0 = SC.n
    sc_evac = lambda k: sc0 + k + 1
    wfree = state["wfree"]
    pdone = state.get("phase_done") or []
    wjv = [None] * nchunk

    @blk.gpsimd
    def _(g):
        for k in range(nchunk):
            if k < 2:
                if wfree is not None:
                    g.wait_ge(wfree[0].h, wfree[1])
                else:
                    for sem, val in pdone:
                        g.wait_ge(sem.h, val)
            else:
                g.wait_ge(TA.h, ta0 + (k - 1))
            wjv[k] = WJ[k % 2].inc(
                g.dma_start(out=wstage[:, k % 2, 0:DM],
                            in_=dr_w[row0 + 128 * k:row0 + 128 * (k + 1), :]), 16)

    @blk.tensor
    def _(t_):
        for k in range(nchunk):
            t_.wait_ge(WJ[k % 2].h, wjv[k])
            if k >= 2:
                t_.wait_ge(SC.h, sc_evac(k - 2))
            for cc in range(8):
                ins = t_.transpose(psW[:, k % 2, 128 * cc:128 * (cc + 1)],
                                   wstage[:, k % 2, 128 * cc:128 * (cc + 1)],
                                   identb[:, :])
            TA.inc(ins)

    @blk.scalar
    def _(s_):
        s = CE(s_, SC)
        for k in range(nchunk):
            s.w(TA, ta0 + k + 1)
            r = s.copy(dst[:, :, 128 * k:128 * (k + 1)],
                       psW[:, k % 2, :].rearrange("p (c i) -> p c i", i=128))
            assert r == sc_evac(k)
    state["wfree"] = (TA, ta0 + nchunk)
    return ta0, sc_evac


def emit_attn_a(p, blk, dr, cn, bcol_a, bvrep_a, borep_a2, state, xin, xmid, xout):
    nc = p.nc
    SC, VC, GC, TM, TA = p.sem("SC"), p.sem("VC"), p.sem("GC"), p.sem("TM"), p.sem("TA")
    XI, XO = state["XI"], state["XO"]
    xsl = cn["xsl"]
    onesf, onesb, identb = cn["onesf"], cn["onesb"], cn["identb"]
    TW = TOK + 64

    with ExitStack() as c:
        sb = lambda n, sh, dt=F32: c.enter_context(nc.sbuf_tensor(n, sh, dt))
        cn["wstage"] = sb("a_wstage", [128, 2, 2048], BF16)
        Vwa = sb("a_Vwa", [128, NT, DM], BF16)
        QKT = sb("a_QKT", [128, 16, TW], BF16)
        cl = c.enter_context(ExitStack())
        lnT = cl.enter_context(nc.sbuf_tensor("a_lnT", [128, 8, TW], BF16))

        ln_all, dup_dep, ln_done = emit_ln(p, blk, cn, state, lnT, "a", xin, None,
                                           dup=64, add_hs=False)
        fence(blk, ln_done)
        if _DBG_MODE == "a:ln":
            _dbg_copy_out(p, blk, cn, state, xin, xout)
            return

        with ExitStack() as cv:
            WvT = cv.enter_context(nc.sbuf_tensor("a_WvT", [128, 8, DM], BF16))
            psW = cv.enter_context(nc.psum_tensor("a_psWv", [128, 2, DM], BF16))
            psV = cv.enter_context(nc.psum_tensor("a_psV", [128, 2, 512], F32))

            ta_w0, scw = jit_w(p, blk, dr["aw_inproj"], 2048, 8, cn, state, psW, WvT)

            ta_v0 = TA.n
            vc0 = VC.n
            vc_vadd = lambda i: vc0 + i + 1

            @blk.tensor
            def _(t_):
                t_.wait_ge(SC.h, scw(7))
                t_.wait_ge(SC.h, ln_all[1])
                t_.wait_ge(dup_dep[0].h, dup_dep[1])
                i = 0
                for wp in range(NT):
                    t0 = 32 + 128 * wp
                    for vch in range(2):
                        if i >= 2:
                            t_.wait_ge(VC.h, vc_vadd(i - 2))
                        for cc in range(8):
                            ins = t_.matmul(psV[:, i % 2, :],
                                            lnT[:, cc, t0:t0 + 128],
                                            WvT[:, cc, 512 * vch:512 * (vch + 1)],
                                            start=(cc == 0), stop=(cc == 7))
                        TA.inc(ins)
                        i += 1

            @blk.vector
            def _(v_):
                v = CE(v_, VC)
                i = 0
                for wp in range(NT):
                    for vch in range(2):
                        v.w(TA, ta_v0 + i + 1)
                        r = v.tensor_add(Vwa[:, wp, 512 * vch:512 * (vch + 1)],
                                         psV[:, i % 2, :],
                                         bvrep_a[:, 512 * vch:512 * (vch + 1)])
                        assert r == vc_vadd(i)
                        i += 1
            vc_v = vc_vadd
            fence(blk, [(VC, VC.n), (TA, TA.n), (SC, SC.n)])
        if _DBG_MODE == "a:v":
            _dbg_copy_out(p, blk, cn, state, xin, xout)
            return

        with ExitStack() as cq:
            WqkTr = cq.enter_context(nc.sbuf_tensor("a_WqkTr", [128, 2, 8, 128], BF16))
            psW = cq.enter_context(nc.psum_tensor("a_psWq", [128, 2, DM], BF16))
            psQ = cq.enter_context(nc.psum_tensor("a_psQ", [128, 2, 512], F32))

            WJ = [p.sem("WJA"), p.sem("WJB")]
            wstage = cn["wstage"]
            ta0, sc0 = TA.n, SC.n
            NJ = TOK // 512
            wfree = state["wfree"]
            wjv = [None] * 16
            sc_w = lambda k: sc0 + k * (1 + NJ) + 1
            sc_q = lambda k, j: sc0 + k * (1 + NJ) + 1 + (j + 1)

            @blk.gpsimd
            def _(g):
                for k in range(16):
                    if k < 2:
                        g.wait_ge(wfree[0].h, wfree[1])
                    else:
                        g.wait_ge(TA.h, ta0 + (k - 1) * (1 + NJ) + 1)
                    wjv[k] = WJ[k % 2].inc(
                        g.dma_start(out=wstage[:, k % 2, 0:DM],
                                    in_=dr["aw_inproj"][128 * k:128 * (k + 1), :]), 16)

            @blk.tensor
            def _(t_):
                for k in range(16):
                    t_.wait_ge(WJ[k % 2].h, wjv[k])
                    if k >= 2:
                        t_.wait_ge(SC.h, sc_w(k - 2))
                    for cc in range(8):
                        ins = t_.transpose(psW[:, k % 2, 128 * cc:128 * (cc + 1)],
                                           wstage[:, k % 2, 128 * cc:128 * (cc + 1)],
                                           identb[:, :])
                    TA.inc(ins)
                    t_.wait_ge(SC.h, sc_w(k))
                    for j in range(NJ):
                        gi = k * NJ + j
                        if gi >= 2:
                            pk, pj = (gi - 2) // NJ, (gi - 2) % NJ
                            t_.wait_ge(SC.h, sc_q(pk, pj))
                        for cc in range(8):
                            ins = t_.matmul(psQ[:, gi % 2, :],
                                            WqkTr[:, k % 2, cc, :],
                                            lnT[:, cc, 512 * j:512 * (j + 1)],
                                            start=(cc == 0), stop=(cc == 7))
                        TA.inc(ins)

            @blk.scalar
            def _(s_):
                s = CE(s_, SC)
                for k in range(16):
                    s.w(TA, ta0 + k * (1 + NJ) + 1)
                    r = s.copy(WqkTr[:, k % 2, :, :],
                               psW[:, k % 2, :].rearrange("p (c i) -> p c i", i=128))
                    assert r == sc_w(k)
                    for j in range(NJ):
                        s.w(TA, ta0 + k * (1 + NJ) + 1 + (j + 1))
                        r = s.activation(QKT[:, k, 512 * j:512 * (j + 1)],
                                         psQ[:, (k * NJ + j) % 2, :], AF.Identity,
                                         bias=bcol_a[:, k:k + 1])
                        assert r == sc_q(k, j)

            vdup = [None]

            @blk.vector
            def _(v_):
                v = CE(v_, VC)
                v.w(SC, SC.n)
                vdup[0] = v.tensor_copy(QKT[:, :, TOK:TW], QKT[:, :, 0:64])
            vc_qkdup = vdup[0]

            state["wfree"] = (TA, ta0 + 15 * (1 + NJ) + 1)
            fence(blk, [(SC, SC.n), (TA, TA.n), (VC, VC.n)])

        cl.close()
        if _DBG_MODE == "a:qk":
            _dbg_copy_out(p, blk, cn, state, xin, xout)
            return

        OT = sb("a_OT", [128, 8, TOK], BF16)
        with ExitStack() as ca:
            EA = ca.enter_context(nc.sbuf_tensor("a_EA", [128, 512], BF16))
            Asb = ca.enter_context(nc.sbuf_tensor("a_Asb", [128, 512], BF16))
            recD = ca.enter_context(nc.sbuf_tensor("a_recD", [1, 1024], F32))
            psS = ca.enter_context(nc.psum_tensor("a_psS", [128, 512], F32))
            psDa = ca.enter_context(nc.psum_tensor("a_psDa", [1, 512], F32))
            psDb = ca.enter_context(nc.psum_tensor("a_psDb", [1, 512], F32))
            psB = ca.enter_context(nc.psum_tensor("a_psB", [128, 512], F32))
            psO = ca.enter_context(nc.psum_tensor("a_psO", [128, 2, 1024], F32))

            tb0, sc0, vc0 = TA.n, SC.n, VC.n
            sc_exp, sc_ot = [], []
            n = sc0
            for wp in range(NT):
                n += 1
                sc_exp.append(n)
                n += 3 if wp == NT - 1 else 1
                sc_ot.append(n)
            vc_rec = lambda wp: vc0 + 3 * wp + 2
            vc_mul = lambda wp: vc0 + 3 * wp + 3

            if _DBG_MODE.startswith("a:att:"):
                _stage = _DBG_MODE.split(":")[2]
                _nwp = int(_DBG_MODE.split(":")[3])
            else:
                _stage, _nwp = "av", NT
            _so = {"s": 0, "d": 1, "b": 2, "m": 3, "av": 4}[_stage]

            @blk.tensor
            def _(t_):
                t_.wait_ge(VC.h, vc_qkdup)
                for wp in range(_nwp):
                    t0 = 32 + 128 * wp
                    if wp >= 1:
                        t_.wait_ge(SC.h, sc_exp[wp - 1])
                    for h in range(8):
                        for w in range(2):
                            b = 64 * w
                            ins = t_.matmul(psS[b:b + 64, 64 * h:64 * h + 64],
                                            QKT[:, 8 + h, t0 + b:t0 + b + 64],
                                            QKT[:, h, t0 + b:t0 + b + 64],
                                            start=True, stop=True)
                    TA.inc(ins)
                    if _so < 1:
                        [TA.inc(t_.nop()) for _ in range(3)]
                        continue
                    t_.wait_ge(SC.h, sc_exp[wp])
                    if wp >= 1:
                        t_.wait_ge(VC.h, vc_rec(wp - 1))
                    t_.matmul(psDa[0:1, :], onesb[0:64, 0:1], EA[0:64, :],
                              start=True, stop=True)
                    ins = t_.matmul(psDb[0:1, :], onesb[64:128, 0:1], EA[64:128, :],
                                    start=True, stop=True)
                    TA.inc(ins)
                    if _so < 2:
                        [TA.inc(t_.nop()) for _ in range(2)]
                        continue
                    t_.wait_ge(VC.h, vc_rec(wp))
                    t_.matmul(psB[0:64, :], onesf[0:1, 0:64], recD[0:1, 0:512],
                              start=True, stop=True)
                    ins = t_.matmul(psB[64:128, :], onesf[0:1, 0:64], recD[0:1, 512:1024],
                                    start=True, stop=True)
                    TA.inc(ins)
                    if _so < 4:
                        TA.inc(t_.nop())
                        continue
                    t_.wait_ge(VC.h, vc_mul(wp))
                    t_.wait_ge(VC.h, vc_v(2 * wp + 1))
                    if wp >= 2:
                        t_.wait_ge(SC.h, sc_ot[wp - 2])
                    for h in range(8):
                        for w in range(2):
                            ins = t_.matmul(
                                psO[:, wp % 2, 512 * w + 64 * h:512 * w + 64 * h + 64],
                                Vwa[64 * w:64 * w + 64, wp, 128 * h:128 * (h + 1)],
                                Asb[64 * w:64 * w + 64, 64 * h:64 * h + 64],
                                start=True, stop=True)
                    TA.inc(ins)

            @blk.scalar
            def _(s_):
                s = CE(s_, SC)
                for wp in range(_nwp):
                    t0 = 32 + 128 * wp
                    s.w(TA, tb0 + 4 * wp + 1)
                    if wp >= 1 and _so >= 3:
                        s.w(VC, vc_mul(wp - 1))
                    r = s.activation(EA[:, :], psS[:, :], AF.Exp, scale=ISR128)
                    assert r == sc_exp[wp]
                    if _so < 4:
                        [SC.inc(s.nop()) for _ in range(sc_ot[wp] - r)]
                        continue
                    s.w(TA, tb0 + 4 * wp + 4)
                    src = psO[:, wp % 2, :].rearrange("p (w h t) -> p w h t", w=2, t=64)
                    if t0 + 128 <= TOK:
                        dst = OT[:, :, t0:t0 + 128].rearrange("p h (w t) -> p w h t", w=2)
                        r = s.copy(dst, src)
                    else:
                        s.copy(OT[:, :, t0:t0 + 64], src[:, 0, :, :])
                        s.copy(OT[:, :, t0 + 64:TOK], src[:, 1, :, 0:32])
                        r = s.copy(OT[:, :, 0:32], src[:, 1, :, 32:64])
                    assert r == sc_ot[wp]

            @blk.vector
            def _(v_):
                v = CE(v_, VC)
                for wp in range(_nwp):
                    if _so < 1:
                        continue
                    v.w(TA, tb0 + 4 * wp + 2)
                    v.reciprocal(recD[0:1, 0:512], psDa[0:1, :])
                    r = v.reciprocal(recD[0:1, 512:1024], psDb[0:1, :])
                    assert r == vc_rec(wp)
                    if _so < 3:
                        [VC.inc(v.nop()) for _ in range(vc_mul(wp) - r)]
                        continue
                    v.w(TA, tb0 + 4 * wp + 3)
                    r = v.tensor_mul(Asb[:, :], EA[:, :], psB[:, :])
                    assert r == vc_mul(wp)

            fence(blk, [(SC, SC.n), (TA, TA.n), (VC, VC.n)])
        if _DBG_MODE.startswith("a:att"):
            _dbg_copy_out(p, blk, cn, state, xin, xout)
            return

        with ExitStack() as co:
            WoT = co.enter_context(nc.sbuf_tensor("a_WoT", [128, 8, DM], BF16))
            psW = co.enter_context(nc.psum_tensor("a_psWo", [128, 2, DM], BF16))
            psY = co.enter_context(nc.psum_tensor("a_psY", [128, 2, 512], F32))

            ta_w0, scw = jit_w(p, blk, dr["aw_outw"], 0, 8, cn, state, psW, WoT)

            ta0 = TA.n
            vc0 = VC.n
            gc0 = GC.n
            vc_stt = lambda t: vc0 + 2 * (t + 1)
            gc_add = lambda t: gc0 + t + 1
            xi_val, xo_val, gate = plan_stream(state)

            @blk.sync
            def _(y):
                for t in range(NT):
                    s = t % NS
                    for sem, val in gate[t]:
                        y.wait_ge(sem.h, val)
                    v = XI[s].inc(y.dma_start(out=xsl[:, s, :],
                                              in_=xin[128 * t:128 * (t + 1), :]), 16)
                    assert v == xi_val[t]

            @blk.tensor
            def _(t_):
                t_.wait_ge(SC.h, scw(7))
                i = 0
                for t in range(NT):
                    for vch in range(2):
                        if i >= 2:
                            t_.wait_ge(VC.h, vc0 + (i - 1))
                        for h in range(8):
                            ins = t_.matmul(psY[:, i % 2, :],
                                            OT[:, h, 128 * t:128 * (t + 1)],
                                            WoT[:, h, 512 * vch:512 * (vch + 1)],
                                            start=(h == 0), stop=(h == 7))
                        TA.inc(ins)
                        i += 1

            @blk.vector
            def _(v_):
                v = CE(v_, VC)
                i = 0
                for t in range(NT):
                    v.w(XI[t % NS], xi_val[t])
                    for vch in range(2):
                        v.w(TA, ta0 + i + 1)
                        r = v.scalar_tensor_tensor(
                            xsl[:, t % NS, 512 * vch:512 * (vch + 1)],
                            psY[:, i % 2, :], 2.0,
                            xsl[:, t % NS, 512 * vch:512 * (vch + 1)],
                            ALU.mult, ALU.add)
                        i += 1
                    assert r == vc_stt(t)

            @blk.gpsimd
            def _(g_):
                g = CE(g_, GC)
                for t in range(NT):
                    g.w(VC, vc_stt(t))
                    r = g.tensor_add(xsl[:, t % NS, :], xsl[:, t % NS, :],
                                     borep_a2[:, :])
                    assert r == gc_add(t)

            @blk.scalar
            def _(s_):
                for t in range(NT):
                    s_.wait_ge(GC.h, gc_add(t))
                    v = XO[t % NS].inc(s_.dma_start(out=xout[128 * t:128 * (t + 1), :],
                                                    in_=xsl[:, t % NS, :]), 16)
                    assert v == xo_val[t]

            xow = [(XO[t % NS], xo_val[t]) for t in range(NT - NS, NT)]
            state["phase_done"] = [(TA, TA.n), (VC, VC.n), (GC, GC.n), (SC, SC.n)] + xow
            fence(blk, [(TA, TA.n), (VC, VC.n), (GC, GC.n), (SC, SC.n)] + xow,
                  engines=("tensor", "scalar", "vector", "gpsimd"))


def _dbg_copy_out(p, blk, cn, state, xin, xout):
    XI, XO = state["XI"], state["XO"]
    xsl = cn["xsl"]
    xi_val, xo_val, gate = plan_stream(state)

    @blk.sync
    def _(y):
        for t in range(NT):
            s = t % NS
            for sem, val in gate[t]:
                y.wait_ge(sem.h, val)
            v = XI[s].inc(y.dma_start(out=xsl[:, s, :],
                                      in_=xin[128 * t:128 * (t + 1), :]), 16)
            assert v == xi_val[t]
            y.wait_ge(XI[s].h, xi_val[t])
            v = XO[s].inc(y.dma_start(out=xout[128 * t:128 * (t + 1), :],
                                      in_=xsl[:, s, :]), 16)
            assert v == xo_val[t]


def emit_ffn(p, blk, dr, cn, bcol_fq, bcol_fk, bcol_fo, bcol_f1,
             bcol_fv2, rbrep, fb2n, state, tag, xin, xmid, xout):
    nc = p.nc
    SC, VC, GC, TM, TA = p.sem("SC"), p.sem("VC"), p.sem("GC"), p.sem("TM"), p.sem("TA")
    XI, XO = state["XI"], state["XO"]
    xsl = cn["xsl"]
    onesf2, identf, sel4f = cn["onesf2"], cn["identf"], cn["sel4f"]
    WJ = [p.sem("WJA"), p.sem("WJB")]
    W1 = [p.sem("W1A"), p.sem("W1B")]
    W2 = [p.sem("W2A"), p.sem("W2B")]
    RD = p.sem("RDF")
    FQ = [p.sem(f"FQ{q}") for q in range(4)]
    pdone = state["phase_done"]

    with ExitStack() as c:
        sb = lambda n, sh, dt=F32: c.enter_context(nc.sbuf_tensor(n + "_" + tag, sh, dt))
        mlaT = sb("f_mlaT", [128, 8, 2048], BF16)
        comb = sb("f_comb", [128, NT, 8], F32)
        combT = sb("f_combT", [8, 2048], BF16)
        cA = c.enter_context(ExitStack())
        QuT = cA.enter_context(nc.sbuf_tensor("f_QuT_" + tag, [128, 4, 2048], BF16, side="right"))
        KuT = cA.enter_context(nc.sbuf_tensor("f_KuT_" + tag, [128, 4, 2048], BF16, side="right"))
        Vn = cA.enter_context(nc.sbuf_tensor("f_Vn_" + tag, [128, 16, 528], BF16, side="right"))
        cB = c.enter_context(ExitStack())
        lnT = cB.enter_context(nc.sbuf_tensor("f_lnT_" + tag, [128, 8, TOK], BF16))
        qwu = cB.enter_context(nc.sbuf_tensor("f_qwu_" + tag, [128, 8, 512], BF16))
        cqw = ExitStack()
        qw = cqw.enter_context(nc.sbuf_tensor("f_qw_" + tag, [128, 8, 512], BF16))

        wja0, wjb0 = WJ[0].n, WJ[1].n

        cqw.close()
        kw = cB.enter_context(nc.sbuf_tensor("f_kw_" + tag, [128, 8, 512], BF16))
        vw = cB.enter_context(nc.sbuf_tensor("f_vw_" + tag, [128, 8, 512], BF16))
        psP = cB.enter_context(nc.psum_tensor("f_psP_" + tag, [128, 2, 512], F32))

        ln_all, _, ln_done = emit_ln(p, blk, cn, state, lnT, "ln" + tag, xin, xmid,
                                     dup=0, add_hs=True)
        ln_slotfree = [list(fr) for fr in state["slot_free"]]
        fence(blk, ln_done)

        vc0 = VC.n
        vc_ones = vc0 + 1
        vc_rep = lambda hf, dc: vc0 + 1 + 8 * hf + dc + 1
        vc_vev = lambda kt: vc0 + 17 + kt + 1
        wja_h = [None, None]

        @blk.gpsimd
        def _(g):
            for sem, val in pdone:
                g.wait_ge(sem.h, val)
            for hf in range(2):
                if hf == 1:
                    g.wait_ge(VC.h, vc_rep(0, 7))
                for dc in range(8):
                    WJ[0].inc(g.dma_start(
                        out=qw[:, dc, :],
                        in_=dr["fw_q"][128 * dc:128 * (dc + 1), 512 * hf:512 * (hf + 1)]), 16)
                wja_h[hf] = WJ[0].n

        @blk.gpsimd
        def _(g):
            g.wait_ge(VC.h, vc_rep(1, 7))
            for dc in range(8):
                WJ[1].inc(g.dma_start(out=kw[:, dc, :],
                                      in_=dr["fw_k"][128 * dc:128 * (dc + 1), :]), 16)
                WJ[1].inc(g.dma_start(out=vw[:, dc, :],
                                      in_=dr["fw_v"][128 * dc:128 * (dc + 1), :]), 16)
        wjb_kv = WJ[1].n

        tb0 = TA.n
        ta_qu = lambda j: tb0 + j + 1
        ta_ku = lambda j: tb0 + 16 + j + 1
        ta_v = lambda kt: tb0 + 32 + kt + 1
        sc0 = SC.n
        sc_qu = lambda j: sc0 + j + 1
        sc_ku = lambda j: sc0 + 16 + j + 1

        @blk.vector
        def _(v_):
            v = CE(v_, VC)
            r = v.memset(Vn[:, :, :].rearrange("p k (h u) -> p k h u", u=33)[:, :, :, 32:33], 1.0)
            assert r == vc_ones
            for hf in range(2):
                v.w(WJ[0], wja_h[hf])
                for dc in range(8):
                    r = v.tensor_copy(
                        qwu[:, dc, 256 * hf:256 * (hf + 1)].rearrange("p (h s) -> p h s", s=32),
                        qw[:, dc, :].rearrange("p (h s) -> p h s", s=64)[:, :, 0:32])
                    assert r == vc_rep(hf, dc)
            for kt in range(16):
                v.w(TA, ta_v(kt))
                r = v.tensor_copy(Vn[:, kt, :].rearrange("p (h u) -> p h u", u=33)[:, :, 0:32],
                                  psP[:, kt % 2, :].rearrange("p (h u) -> p h u", u=32))
                assert r == vc_vev(kt)

        @blk.tensor
        def _(t_):
            t_.wait_ge(VC.h, vc_rep(1, 7))
            t_.wait_ge(SC.h, ln_all[1])
            for cc in range(4):
                for tc in range(4):
                    j = cc * 4 + tc
                    if j >= 2:
                        t_.wait_ge(SC.h, sc_qu(j - 2))
                    for dc in range(8):
                        ins = t_.matmul(psP[:, j % 2, :],
                                        qwu[:, dc, 128 * cc:128 * (cc + 1)],
                                        lnT[:, dc, 512 * tc:512 * (tc + 1)],
                                        start=(dc == 0), stop=(dc == 7))
                    TA.inc(ins)
            t_.wait_ge(WJ[1].h, wjb_kv)
            for cc in range(4):
                for tc in range(4):
                    j = cc * 4 + tc
                    t_.wait_ge(SC.h, sc_ku(j - 2) if j >= 2 else sc_qu(15))
                    for dc in range(8):
                        ins = t_.matmul(psP[:, j % 2, :],
                                        kw[:, dc, 128 * cc:128 * (cc + 1)],
                                        lnT[:, dc, 512 * tc:512 * (tc + 1)],
                                        start=(dc == 0), stop=(dc == 7))
                    TA.inc(ins)
            t_.wait_ge(VC.h, vc_ones)
            for kt in range(16):
                if kt >= 2:
                    t_.wait_ge(VC.h, vc_vev(kt - 2))
                else:
                    t_.wait_ge(SC.h, sc_ku(14 + kt))
                for dc in range(8):
                    ins = t_.matmul(psP[:, kt % 2, 0:512],
                                    lnT[:, dc, 128 * kt:128 * (kt + 1)],
                                    vw[:, dc, :],
                                    start=(dc == 0), stop=(dc == 7))
                TA.inc(ins)

        @blk.scalar
        def _(s_):
            s = CE(s_, SC)
            for cc in range(4):
                for tc in range(4):
                    j = cc * 4 + tc
                    s.w(TA, ta_qu(j))
                    r = s.activation(QuT[:, cc, 512 * tc:512 * (tc + 1)],
                                     psP[:, j % 2, :], AF.Identity,
                                     bias=bcol_fq[:, cc:cc + 1])
                    assert r == sc_qu(j)
            for cc in range(4):
                for tc in range(4):
                    j = cc * 4 + tc
                    s.w(TA, ta_ku(j))
                    r = s.activation(KuT[:, cc, 512 * tc:512 * (tc + 1)],
                                     psP[:, j % 2, :], AF.Identity,
                                     bias=bcol_fk[:, cc:cc + 1])
                    assert r == sc_ku(j)

        fence(blk, [(TA, TA.n), (SC, SC.n), (VC, VC.n)])
        cB.close()

        cAO = c.enter_context(ExitStack())
        aoT = cAO.enter_context(nc.sbuf_tensor("f_aoT_" + tag, [128, 4, 2048], BF16))
        cE = cAO.enter_context(ExitStack())
        E2 = cE.enter_context(nc.sbuf_tensor("f_E2_" + tag, [128, 2, 4, 2, 1024], BF16))
        rtmp = cE.enter_context(nc.sbuf_tensor("f_rtmp_" + tag, [128, 2, 512], F32))
        recD = cE.enter_context(nc.sbuf_tensor("f_recD_" + tag, [128, 512], F32))
        o32d = cE.enter_context(nc.sbuf_tensor("f_o32_" + tag, [128, 512], BF16))
        psS = cE.enter_context(nc.psum_tensor("f_psS_" + tag, [128, 6, 512], F32))
        psA = cE.enter_context(nc.psum_tensor("f_psA_" + tag, [128, 2, 512], F32))
        psN = psS[:, 4, :]

        ta0 = TA.n
        sc0 = SC.n
        vc0 = VC.n
        rd0 = RD.n
        ta_sc = {}
        ta_av = {}
        ta_bc = {}
        tn = ta0
        for j in range(16):
            if j > 0:
                tn += 1
                ta_bc[j - 1] = tn
            for s4 in range(4):
                for ktl in range(4):
                    for r in range(4):
                        tn += 1
                        ta_sc[(j, s4, ktl, r)] = tn
                for ktl in range(4):
                    for pp in range(2):
                        for q2 in range(2):
                            tn += 1
                            ta_av[(j, s4, ktl, pp, q2)] = tn
        tn += 1
        ta_bc[15] = tn
        ta_av_last = lambda gs: ta_av[(gs // 4, gs % 4, 3, 1, 1)]
        sc_exp = lambda gk, pp: sc0 + 2 * gk + pp + 1
        vc_it = lambda j, k: vc0 + 10 * j + k
        rd_tot = lambda j: rd0 + 64 * (j + 1)
        slot = lambda gk, pp: (2 * gk + pp) % 3
        s2_first = {}
        s2_last = {}
        for j in range(16):
            for kt in range(16):
                for pp2 in range(2):
                    if slot(16 * j + kt, pp2) == 2:
                        if j not in s2_first:
                            s2_first[j] = 16 * j + kt
                        s2_last[j] = (16 * j + kt, pp2)

        @blk.tensor
        def _(t_):
            for j in range(17):
                if j > 0:
                    t_.wait_ge(RD.h, rd_tot(j - 1))
                    t_.wait_ge(SC.h, sc_exp(*s2_last[j - 1]))
                    if j >= 2:
                        t_.wait_ge(VC.h, vc_it(j - 2, 9))
                    ins = t_.matmul(psN, sel4f[0:4, :], recD[0:4, :],
                                    start=True, stop=True)
                    assert TA.inc(ins) == ta_bc[j - 1]
                if j == 16:
                    break
                cch, qc = j // 4, j % 4
                for s4 in range(4):
                    gs = 4 * j + s4
                    for ktl in range(4):
                        kt = 4 * s4 + ktl
                        gk = 16 * j + kt
                        for pp in range(2):
                            prev = 2 * gk + pp - 3
                            if prev >= 0:
                                t_.wait_ge(SC.h, sc_exp(prev // 2, prev % 2))
                        if j > 0 and s2_first[j] == gk:
                            t_.wait_ge(VC.h, vc_it(j - 1, 9))
                        for r in range(4):
                            sl = slot(gk, r // 2)
                            ins = t_.matmul(
                                psS[:, 2 * sl + (r % 2), :],
                                KuT[32 * r:32 * r + 32, cch, 128 * kt:128 * (kt + 1)],
                                QuT[32 * r:32 * r + 32, cch, 512 * qc:512 * (qc + 1)],
                                start=True, stop=True, tile_position=(32 * r, 0))
                            assert TA.inc(ins) == ta_sc[(j, s4, ktl, r)]
                    for ktl in range(4):
                        kt = 4 * s4 + ktl
                        t_.wait_ge(SC.h, sc_exp(16 * j + kt, 1))
                        if s4 == 0 and ktl == 0 and j > 0:
                            t_.wait_ge(VC.h, vc_it(j - 1, 8))
                        for pp in range(2):
                            for q2 in range(2):
                                h = 4 * cch + 2 * pp + q2
                                ins = t_.matmul(
                                    psA[64 * q2:64 * q2 + 33, pp, :],
                                    Vn[:, kt, 33 * h:33 * h + 33],
                                    E2[:, gs % 2, ktl, pp, 512 * q2:512 * (q2 + 1)],
                                    start=(s4 == 0 and ktl == 0),
                                    stop=(s4 == 3 and ktl == 3),
                                    tile_position=(0, 64 * q2),
                                    skip_group_check=True)
                                assert TA.inc(ins) == ta_av[(j, s4, ktl, pp, q2)]

        @blk.scalar
        def _(s_):
            s = CE(s_, SC)
            for gs in range(64):
                j, s4 = gs // 4, gs % 4
                for ktl in range(4):
                    gk = 16 * j + 4 * s4 + ktl
                    for pp in range(2):
                        s.w(TA, ta_sc[(j, s4, ktl, 2 * pp + 1)])
                        if gs >= 2 and ktl == 0 and pp == 0:
                            s.w(TA, ta_av_last(gs - 2))
                        sl = slot(gk, pp)
                        r = s.activation(E2[:, gs % 2, ktl, pp, :],
                                         psS[:, 2 * sl:2 * sl + 2, :],
                                         AF.Exp, scale=ISR32)
                        assert r == sc_exp(gk, pp)

        @blk.vector
        def _(v_):
            v = CE(v_, VC)
            for j in range(16):
                cch, qc = j // 4, j % 4
                v.w(TA, ta_av_last(4 * j + 3))
                if j > 0:
                    v.w(RD, rd_tot(j - 1))
                for k in range(4):
                    pp, rr = k // 2, k % 2
                    r = v.reciprocal(rtmp[32 + 64 * rr:33 + 64 * rr, pp, :],
                                     psA[32 + 64 * rr:33 + 64 * rr, pp, :])
                    assert r == vc_it(j, k + 1)
                for k in range(4):
                    r = v.tensor_copy(o32d[32 * k:32 * (k + 1), :],
                                      psA[64 * (k % 2):64 * (k % 2) + 32, k // 2, :])
                    assert r == vc_it(j, k + 5)
                v.w(TA, ta_bc[j])
                r = v.tensor_mul(aoT[:, cch, 512 * qc:512 * (qc + 1)],
                                 o32d[:, :], psN)
                assert r == vc_it(j, 9)
                r = v.tensor_scalar_add(aoT[:, cch, 512 * qc:512 * (qc + 1)],
                                        aoT[:, cch, 512 * qc:512 * (qc + 1)],
                                        bcol_fv2[:, cch:cch + 1])
                assert r == vc_it(j, 10)

        @blk.gpsimd
        def _(g_):
            for j in range(16):
                if j > 0:
                    g_.wait_ge(TA.h, ta_bc[j - 1])
                for k in range(4):
                    pp, rr = k // 2, k % 2
                    g_.wait_ge(VC.h, vc_it(j, k + 1))
                    RD.inc(g_.dma_start(
                        out=recD[k:k + 1, :],
                        in_=rtmp[32 + 64 * rr:33 + 64 * rr, pp, :]), 16)
                assert RD.n == rd_tot(j)

        fence(blk, [(TA, TA.n), (SC, SC.n), (VC, VC.n)])
        cE.close()
        cA.close()

        c4 = c.enter_context(ExitStack())
        ow = c4.enter_context(nc.sbuf_tensor("f_ow_" + tag, [128, 4, 1024], BF16))
        rw = c4.enter_context(nc.sbuf_tensor("f_rw_" + tag, [128, 8, 8], BF16))
        lgT = c4.enter_context(nc.sbuf_tensor("f_lgT_" + tag, [8, 2048], F32))
        lg2 = c4.enter_context(nc.sbuf_tensor("f_lg2_" + tag, [128, 2, 8], F32))
        E8 = c4.enter_context(nc.sbuf_tensor("f_E8_" + tag, [128, 2, 8], F32))
        rsc = c4.enter_context(nc.sbuf_tensor("f_rsc_" + tag, [128, 8], F32))
        psM = c4.enter_context(nc.psum_tensor("f_psM_" + tag, [128, 2, 512], F32))
        psR = c4.enter_context(nc.psum_tensor("f_psR_" + tag, [8, 2, 512], F32))

        wja0 = WJ[0].n

        @blk.gpsimd
        def _(g):
            g.wait_ge(TA.h, TA.n)
            g.wait_ge(VC.h, VC.n)
            for cc in range(4):
                WJ[0].inc(g.dma_start(out=ow[:, cc, :],
                                      in_=dr["fw_o"][128 * cc:128 * (cc + 1), :]), 16)
            WJ[0].inc(g.dma_start(
                out=rw[:, :, :],
                in_=dr["fw_r"].rearrange("(c p) e -> p c e", p=128)), 16)
        wja_o = WJ[0].n

        ta0 = TA.n
        sc0 = SC.n
        vc0 = VC.n
        ta_op = lambda j: ta0 + j + 1
        sc_op = lambda j: sc0 + j + 1
        ta_rt = lambda tc: ta0 + 32 + tc + 1
        sc_rt = lambda tc: sc0 + 32 + tc + 1
        ta_tr = lambda tt: ta0 + 36 + tt + 1
        ta_ct = lambda tt: ta0 + 52 + tt + 1
        sc_e8 = lambda tt: sc0 + 36 + tt + 1
        sc_ct = lambda tt: sc0 + 52 + tt + 1

        @blk.tensor
        def _(t_):
            t_.wait_ge(WJ[0].h, wja_o)
            for dc in range(8):
                for tc in range(4):
                    j = dc * 4 + tc
                    if j >= 2:
                        t_.wait_ge(SC.h, sc_op(j - 2))
                    for cc in range(4):
                        ins = t_.matmul(psM[:, j % 2, :],
                                        ow[:, cc, 128 * dc:128 * (dc + 1)],
                                        aoT[:, cc, 512 * tc:512 * (tc + 1)],
                                        start=(cc == 0), stop=(cc == 3))
                    TA.inc(ins)
            for tc in range(4):
                t_.wait_ge(SC.h, sc_op(31))
                if tc >= 2:
                    t_.wait_ge(SC.h, sc_rt(tc - 2))
                for dc in range(8):
                    ins = t_.matmul(psR[0:8, tc % 2, :], rw[:, dc, :],
                                    mlaT[:, dc, 512 * tc:512 * (tc + 1)],
                                    start=(dc == 0), stop=(dc == 7))
                TA.inc(ins)
            for tt in range(NT):
                t_.wait_ge(SC.h, sc_rt(3))
                if tt >= 2:
                    t_.wait_ge(VC.h, vc0 + 10 * (tt - 2) + 1)
                TA.inc(t_.transpose(psM[:, tt % 2, 0:8], lgT[0:8, 128 * tt:128 * (tt + 1)],
                                    identf[0:8, 0:8]))
            for tt in range(NT):
                t_.wait_ge(VC.h, vc0 + 10 * tt + 10)
                if tt >= 2:
                    t_.wait_ge(SC.h, sc_ct(tt - 2))
                TA.inc(t_.transpose(psR[0:8, tt % 2, 0:128], comb[:, tt, :],
                                    identf[:, :]))

        @blk.scalar
        def _(s_):
            s = CE(s_, SC)
            for dc in range(8):
                for tc in range(4):
                    j = dc * 4 + tc
                    s.w(TA, ta_op(j))
                    r = s.activation(mlaT[:, dc, 512 * tc:512 * (tc + 1)],
                                     psM[:, j % 2, :], AF.Identity,
                                     bias=bcol_fo[:, dc:dc + 1])
                    assert r == sc_op(j)
            for tc in range(4):
                s.w(TA, ta_rt(tc))
                r = s.copy(lgT[0:8, 512 * tc:512 * (tc + 1)], psR[0:8, tc % 2, :])
                assert r == sc_rt(tc)
            for tt in range(NT):
                s.w(VC, vc0 + 10 * tt + 1)
                r = s.activation(E8[:, tt % 2, :], lg2[:, tt % 2, :], AF.Exp,
                                 accum_out=rsc[:, (tt % 2):(tt % 2) + 1])
                assert r == sc_e8(tt)
            for tt in range(NT):
                s.w(TA, ta_ct(tt))
                r = s.copy(combT[0:8, 128 * tt:128 * (tt + 1)], psR[0:8, tt % 2, 0:128])
                assert r == sc_ct(tt)

        @blk.vector
        def _(v_):
            v = CE(v_, VC)
            for tt in range(NT):
                v.w(TA, ta_tr(tt))
                if tt >= 2:
                    v.w(SC, sc_e8(tt - 2))
                r = v.tensor_add(lg2[:, tt % 2, :], psM[:, tt % 2, 0:8], rbrep[:, :])
                assert r == vc0 + 10 * tt + 1
                v.w(SC, sc_e8(tt))
                v.reduce_max(rsc[:, 2:3], E8[:, tt % 2, :], axis=AX.X)
                v.scalar_tensor_tensor(lg2[:, tt % 2, :], E8[:, tt % 2, :], rsc[:, 2:3],
                                       E8[:, tt % 2, :], ALU.is_lt, ALU.mult)
                v.reduce_max(rsc[:, 3:4], lg2[:, tt % 2, :], axis=AX.X)
                v.tensor_scalar(rsc[:, 4:5], rsc[:, (tt % 2):(tt % 2) + 1], 1e-8,
                                rsc[:, 2:3], ALU.mult, ALU.add)
                v.tensor_add(rsc[:, 4:5], rsc[:, 4:5], rsc[:, 3:4])
                v.reciprocal(rsc[:, 5:6], rsc[:, 4:5])
                v.tensor_scalar(rsc[:, 6:7], rsc[:, 5:6], 0.3, None, ALU.mult)
                v.scalar_tensor_tensor(lg2[:, tt % 2, :], E8[:, tt % 2, :], rsc[:, 3:4],
                                       E8[:, tt % 2, :], ALU.is_ge, ALU.mult)
                r = v.tensor_scalar(comb[:, tt, :], lg2[:, tt % 2, :], rsc[:, 6:7],
                                    None, ALU.mult)
                assert r == vc0 + 10 * tt + 10

        fence(blk, [(TA, TA.n), (SC, SC.n), (VC, VC.n)])
        c4.close()
        cAO.close()

        c5 = c.enter_context(ExitStack())
        xful = c5.enter_context(nc.sbuf_tensor("f_xful_" + tag, [128, NT, DM], F32))
        w1sb = c5.enter_context(nc.sbuf_tensor("f_w1_" + tag, [128, 8, 2048], BF16))
        w2sb = c5.enter_context(nc.sbuf_tensor("f_w2_" + tag, [128, 16, 1024], BF16))
        g1T = c5.enter_context(nc.sbuf_tensor("f_g1_" + tag, [128, 16, 512], BF16))
        psH = c5.enter_context(nc.psum_tensor("f_psH_" + tag, [128, 2, 512], F32))
        psG = c5.enter_context(nc.psum_tensor("f_psG_" + tag, [128, 2, 512], F32))

        ta0 = TA.n
        sc0 = SC.n
        vc0 = VC.n
        fq0 = [q.n for q in FQ]
        f3_done = [(TA, TA.n), (SC, SC.n), (VC, VC.n)]

        ta_b2 = lambda j: ta0 + j + 1
        vc_b2 = lambda j: vc0 + j + 1
        def ta_h1(e, tc, fb):
            return ta0 + 32 + e * (4 * (16 + 8)) + tc * (16 + 8) + fb + 1
        def ta_h2(e, tc, j):
            return ta0 + 32 + e * 96 + tc * 24 + 16 + j + 1
        def sc_g1(e, tc, fb):
            return sc0 + e * 64 + tc * 16 + fb + 1
        def vc_h2(e, tc, j):
            return vc0 + 32 + e * 32 + tc * 8 + j + 1
        w1v = [0, 0]
        w2v = [0, 0]
        w1tot = {}
        w2tot = {}

        @blk.sync
        def _(y):
            for sem, val in f3_done:
                y.wait_ge(sem.h, val)
            for fr in ln_slotfree:
                for sem, val in fr:
                    y.wait_ge(sem.h, val)
            for q in range(4):
                src = xmid[512 * q:512 * (q + 1), :].rearrange("(t p) d -> p t d", p=128)
                FQ[q].inc(y.dma_start(out=xful[:, 4 * q:4 * (q + 1), :], in_=src), 16)

        @blk.gpsimd
        def _(g):
            for e in range(8):
                if e == 0:
                    for sem, val in f3_done:
                        g.wait_ge(sem.h, val)
                else:
                    g.wait_ge(TA.h, ta_h1(e - 1, 3, 15))
                for dc in range(8):
                    W1[e % 2].inc(g.dma_start(
                        out=w1sb[:, dc, :],
                        in_=dr["fw_1"][e, 128 * dc:128 * (dc + 1), :]), 16)
                w1tot[e] = W1[e % 2].n
                if e > 0:
                    g.wait_ge(TA.h, ta_h2(e - 1, 3, 7))
                for fb in range(16):
                    W2[e % 2].inc(g.dma_start(
                        out=w2sb[:, fb, :],
                        in_=dr["fw_2"][e, 128 * fb:128 * (fb + 1), :]), 16)
                w2tot[e] = W2[e % 2].n

        @blk.tensor
        def _(t_):
            t_.wait_ge(SC.h, sc0)
            for tt in range(NT):
                for dh in range(2):
                    j = tt * 2 + dh
                    if j >= 2:
                        t_.wait_ge(VC.h, vc_b2(j - 2))
                    TA.inc(t_.matmul(psG[:, j % 2, :], combT[0:8, 128 * tt:128 * (tt + 1)],
                                     fb2n[0:8, 512 * dh:512 * (dh + 1)],
                                     start=True, stop=True))
            for e in range(8):
                t_.wait_ge(W1[e % 2].h, w1tot[e])
                for tc in range(4):
                    for fb in range(16):
                        prev = (e, tc, fb - 2)
                        if fb < 2:
                            prev = (e, tc - 1, 14 + fb) if tc > 0 else \
                                   ((e - 1, 3, 14 + fb) if e > 0 else None)
                        if prev is not None:
                            t_.wait_ge(SC.h, sc_g1(*prev))
                        for dc in range(8):
                            ins = t_.matmul(psH[:, fb % 2, :],
                                            w1sb[:, dc, 128 * fb:128 * (fb + 1)],
                                            mlaT[:, dc, 512 * tc:512 * (tc + 1)],
                                            start=(dc == 0), stop=(dc == 7))
                        TA.inc(ins)
                    if e == 0 and tc == 0:
                        t_.wait_ge(W2[0].h, w2tot[0])
                    elif tc == 0:
                        t_.wait_ge(W2[e % 2].h, w2tot[e])
                    t_.wait_ge(SC.h, sc_g1(e, tc, 15))
                    for tt in range(4):
                        for dh in range(2):
                            j = tt * 2 + dh
                            prev = (e, tc, j - 2)
                            if j < 2:
                                prev = (e, tc - 1, 6 + j) if tc > 0 else \
                                       ((e - 1, 3, 6 + j) if e > 0 else None)
                            if prev is not None:
                                t_.wait_ge(VC.h, vc_h2(*prev))
                            elif j < 2:
                                t_.wait_ge(VC.h, vc_b2(30 + j))
                            for fb in range(16):
                                ins = t_.matmul(psG[:, j % 2, :],
                                                g1T[:, fb, 128 * tt:128 * (tt + 1)],
                                                w2sb[:, fb, 512 * dh:512 * (dh + 1)],
                                                start=(fb == 0), stop=(fb == 15))
                            TA.inc(ins)

        @blk.scalar
        def _(s_):
            s = CE(s_, SC)
            for e in range(8):
                for tc in range(4):
                    for fb in range(16):
                        s.w(TA, ta_h1(e, tc, fb))
                        r = s.activation(g1T[:, fb, :], psH[:, fb % 2, :], AF.Gelu,
                                         bias=bcol_f1[:, e, fb:fb + 1])
                        assert r == sc_g1(e, tc, fb)

        @blk.vector
        def _(v_):
            v = CE(v_, VC)
            for tt in range(NT):
                for dh in range(2):
                    j = tt * 2 + dh
                    v.w(TA, ta_b2(j))
                    v.w(FQ[tt // 4], fq0[tt // 4] + 16)
                    r = v.scalar_tensor_tensor(
                        xful[:, tt, 512 * dh:512 * (dh + 1)],
                        psG[:, j % 2, :], 1.0,
                        xful[:, tt, 512 * dh:512 * (dh + 1)],
                        ALU.mult, ALU.add)
                    assert r == vc_b2(j)
            for e in range(8):
                for tc in range(4):
                    for tt in range(4):
                        for dh in range(2):
                            j = tt * 2 + dh
                            v.w(TA, ta_h2(e, tc, j))
                            t_g = 4 * tc + tt
                            r = v.scalar_tensor_tensor(
                                xful[:, t_g, 512 * dh:512 * (dh + 1)],
                                psG[:, j % 2, :], comb[:, t_g, e:e + 1],
                                xful[:, t_g, 512 * dh:512 * (dh + 1)],
                                ALU.mult, ALU.add)
                            assert r == vc_h2(e, tc, j)

        xo_val = [0] * NT
        sfree = [None] * NS

        @blk.sync
        def _(y):
            for t in range(NT):
                y.wait_ge(VC.h, vc_h2(7, t // 4, (t % 4) * 2 + 1))
                xo_val[t] = XO[t % NS].inc(
                    y.dma_start(out=xout[128 * t:128 * (t + 1), :], in_=xful[:, t, :]), 16)
        for t in range(NT):
            sfree[t % NS] = [(XO[t % NS], xo_val[t])]
        state["slot_free"] = sfree
        xow = [(XO[t % NS], xo_val[t]) for t in range(NT - NS, NT)]
        state["phase_done"] = [(TA, TA.n), (SC, SC.n), (VC, VC.n)] + xow
        fence(blk, [(TA, TA.n), (SC, SC.n), (VC, VC.n)] + xow,
              engines=("tensor", "scalar", "vector", "gpsimd"))


def emit_attn_b(p, blk, dr, cn, bcol_bq, bcol_bk, bcol_bv2, borep_b,
                state, xin, xmid, xout):
    nc = p.nc
    SC, VC, GC, TM, TA = p.sem("SC"), p.sem("VC"), p.sem("GC"), p.sem("TM"), p.sem("TA")
    XI, XO = state["XI"], state["XO"]
    xsl = cn["xsl"]
    onesf2, identb = cn["onesf2"], cn["identb"]
    CB = p.sem("CB")
    WK = p.sem("WK")
    WJ = [p.sem("WJA"), p.sem("WJB")]
    WR = [p.sem("WRA"), p.sem("WRB")]
    WM = [p.sem("WMA"), p.sem("WMB"), p.sem("WMC"), p.sem("WMD")]
    pdone = state["phase_done"]

    with ExitStack() as c:
        sb = lambda n, sh, dt=F32: c.enter_context(nc.sbuf_tensor(n, sh, dt))
        cn["wstage"] = sb("b_wstage", [128, 2, 2048], BF16)
        wstage = cn["wstage"]
        cosb = sb("b_cos", [128, TOK], F32)
        sinb = sb("b_sinm", [128, TOK], F32)
        maskb = sb("b_mask", [128, 4, 512], BF16)
        aoTb = sb("b_aoT", [128, 8, TOK], BF16)
        lnT = sb("b_lnT", [128, 8, TOK], BF16)

        @blk.sync
        def _(y):
            CB.inc(y.dma_start(out=cosb[:, :], in_=dr["c_cos"]), 16)
            CB.inc(y.dma_start(out=sinb[:, :], in_=dr["c_sinm"]), 16)
        cb_tab = CB.n

        @blk.gpsimd
        def _(g):
            for sem, val in pdone:
                g.wait_ge(sem.h, val)
            for o in range(4):
                WK.inc(g.dma_start(out=maskb[:, o, :], in_=dr["c_mask"][o, :, :]), 16)
        wk_msk = WK.n

        ln_all, _, ln_done = emit_ln(p, blk, cn, state, lnT, "b", xin, None,
                                     dup=0, add_hs=False)
        fence(blk, ln_done)

        cw = c.enter_context(ExitStack())
        wpT = cw.enter_context(nc.sbuf_tensor("b_wpT", [128, 3, 8, 128], BF16))
        Qf = cw.enter_context(nc.sbuf_tensor("b_Qf", [128, TOK], F32))
        Kf = cw.enter_context(nc.sbuf_tensor("b_Kf", [128, TOK], F32))
        rot = cw.enter_context(nc.sbuf_tensor("b_rot", [128, TOK], F32))
        rtm = cw.enter_context(nc.sbuf_tensor("b_rtm", [128, TOK], F32))
        Qr = cw.enter_context(nc.sbuf_tensor("b_Qr", [128, TOK], BF16))
        Kr = cw.enter_context(nc.sbuf_tensor("b_Kr", [128, TOK], BF16))
        Vb = cw.enter_context(nc.sbuf_tensor("b_Vb", [128, 16, 130], BF16))
        Eb = cw.enter_context(nc.sbuf_tensor("b_Eb", [128, 16, 512], BF16))
        recRb = cw.enter_context(nc.sbuf_tensor("b_recR", [128, 512], F32))
        o64 = cw.enter_context(nc.sbuf_tensor("b_o64", [128, 2, 512], BF16))
        aostg = cw.enter_context(nc.sbuf_tensor("b_aostg", [128, 4, 512], BF16))
        psW = cw.enter_context(nc.psum_tensor("b_psW", [128, 2, DM], BF16))
        psP = cw.enter_context(nc.psum_tensor("b_psP", [128, 2, 512], F32))
        psS = cw.enter_context(nc.psum_tensor("b_psS", [128, 2, 512], F32))
        psA = cw.enter_context(nc.psum_tensor("b_psA", [128, 512], F32))
        psN = cw.enter_context(nc.psum_tensor("b_psN", [128, 512], F32))

        NKT = lambda qc: 4 * qc + 4
        ta0, sc0, vc0, gc0 = TA.n, SC.n, VC.n, GC.n
        wj0 = [WJ[0].n, WJ[1].n]
        wr0 = [WR[0].n, WR[1].n]
        wm0 = [WM[0].n, WM[1].n]

        TPP = 3 + 12 + 2 * sum(NKT(qc) + 2 for qc in range(4))
        def ta_tr(p_, w3):
            return ta0 + TPP * p_ + w3 + 1
        def ta_qk(p_, qk, tc):
            return ta0 + TPP * p_ + 3 + 4 * qk + tc + 1
        def ta_v(p_, sl):
            return ta0 + TPP * p_ + 11 + sl + 1
        def ta_att0(p_, w, qc):
            base = ta0 + TPP * p_ + 15
            for ww in range(2):
                for q2 in range(4):
                    if (ww, q2) == (w, qc):
                        return base
                    base += NKT(q2) + 2
            return base
        def ta_sc(p_, w, qc, kt):
            return ta_att0(p_, w, qc) + kt + 1
        def ta_av(p_, w, qc):
            return ta_att0(p_, w, qc) + NKT(qc) + 1
        def ta_bc(p_, w, qc):
            return ta_att0(p_, w, qc) + NKT(qc) + 2
        SPP = 3 + 8 + 2 * sum(NKT(qc) + 1 for qc in range(4))
        def sc_w(p_, w3):
            return sc0 + SPP * p_ + w3 + 1
        def sc_qk(p_, qk, tc):
            return sc0 + SPP * p_ + 3 + 4 * qk + tc + 1
        def sc_att0(p_, w, qc):
            base = sc0 + SPP * p_ + 11
            for ww in range(2):
                for q2 in range(4):
                    if (ww, q2) == (w, qc):
                        return base
                    base += NKT(q2) + 1
            return base
        def sc_exp(p_, w, qc, kt):
            return sc_att0(p_, w, qc) + kt + 1
        def sc_o64(p_, w, qc):
            return sc_att0(p_, w, qc) + NKT(qc) + 1
        VOFF = 2
        VPP = 16 + 4 + 2 * sum(4 + 3 for qc in range(4))
        def vc_vev(p_, sl, tt):
            return vc0 + VOFF + VPP * p_ + 4 * sl + tt + 1
        def vc_rope(p_, qk, ph):
            return vc0 + VOFF + VPP * p_ + 16 + 2 * qk + ph + 1
        def vc_att0(p_, w, qc):
            return vc0 + VOFF + VPP * p_ + 20 + 7 * (4 * w + qc)
        def vc_msk(p_, w, qc, j):
            return vc_att0(p_, w, qc) + j + 1
        def vc_rec(p_, w, qc):
            return vc_att0(p_, w, qc) + 5
        def vc_mul(p_, w, qc):
            return vc_att0(p_, w, qc) + 7
        GPP = 2
        def gc_rp(p_, qk):
            return gc0 + GPP * p_ + qk + 1
        wr_val = {}
        wm_val = {}
        wrn = [WR[0].n, WR[1].n]
        wmn = [sem.n for sem in WM]
        wm_prev = {}
        for p_ in range(8):
            for qk in range(2):
                wrn[qk] += 64
                wr_val[(p_, qk)] = wrn[qk]
            for qc in range(4):
                i = p_ * 4 + qc
                wm_prev[i] = wm_val.get((p_ - 1) * 4 + qc) if p_ > 0 else None
                wmn[qc] += 16
                wm_val[i] = wmn[qc]

        @blk.gpsimd
        def _(g_):
            g = CE(g_, GC)
            for p_ in range(8):
                for mp in ([p_ - 1] if p_ > 0 else []):
                    for qc in range(4):
                        i = mp * 4 + qc
                        g_.wait_ge(VC.h, vc_mul(mp, 1, qc))
                        vv = WM[qc].inc(
                            g_.dma_start(out=aoTb[64:128, mp, 512 * qc:512 * (qc + 1)],
                                         in_=aostg[0:64, qc, :]), 16)
                        assert vv == wm_val[i]
                if p_ == 0:
                    g_.wait_ge(VC.h, ln_done[0][1])
                else:
                    g_.wait_ge(TA.h, ta_tr(p_ - 1, 2))
                WJ[0].inc(g_.dma_start(out=wstage[:, 0, 0:1024],
                                       in_=dr["bw_q"][128 * p_:128 * (p_ + 1), :]), 16)
                WJ[0].inc(g_.dma_start(out=wstage[:, 0, 1024:2048],
                                       in_=dr["bw_k"][128 * p_:128 * (p_ + 1), :]), 16)
                WJ[1].inc(g_.dma_start(out=wstage[:, 1, 0:1024],
                                       in_=dr["bw_v"][128 * p_:128 * (p_ + 1), :]), 16)
                for qk in range(2):
                    src = Qf if qk == 0 else Kf
                    rr = rot if qk == 0 else rtm
                    g_.wait_ge(SC.h, sc_qk(p_, qk, 3))
                    if p_ > 0:
                        g_.wait_ge(VC.h, vc_rope(p_ - 1, qk, 1))
                    for piece in range(4):
                        sb_, db_ = [32, 0, 96, 64][piece], [0, 32, 64, 96][piece]
                        WR[qk % 2].inc(g_.dma_start(out=rr[db_:db_ + 32, :],
                                                    in_=src[sb_:sb_ + 32, :]), 16)
                    g.w(WR[qk % 2], wr_val[(p_, qk)])
                    g.w(CB, cb_tab)
                    r = g.tensor_mul(src[:, :], src[:, :], cosb[:, :])
                    assert r == gc_rp(p_, qk)
            for qc in range(4):
                i = 7 * 4 + qc
                g_.wait_ge(VC.h, vc_mul(7, 1, qc))
                vv = WM[qc].inc(
                    g_.dma_start(out=aoTb[64:128, 7, 512 * qc:512 * (qc + 1)],
                                 in_=aostg[0:64, qc, :]), 16)
                assert vv == wm_val[i]

        @blk.tensor
        def _(t_):
            for p_ in range(8):
                for w3 in range(3):
                    sl = 0 if w3 < 2 else 1
                    off = 1024 * (w3 % 2) if w3 < 2 else 0
                    t_.wait_ge(WJ[sl].h, wj0[sl] + (2 if sl == 0 else 1) * 16 * (p_ + 1))
                    if p_ >= 1 or w3 >= 2:
                        pk, pw = (p_, w3 - 2) if w3 >= 2 else (p_ - 1, w3 + 1)
                        t_.wait_ge(SC.h, sc_w(pk, pw))
                    for cc in range(8):
                        ins = t_.transpose(psW[:, w3 % 2, 128 * cc:128 * (cc + 1)],
                                           wstage[:, sl, off + 128 * cc:off + 128 * (cc + 1)],
                                           identb[:, :])
                    TA.inc(ins)
                for qk in range(2):
                    t_.wait_ge(SC.h, sc_w(p_, qk))
                    for tc in range(4):
                        j = 4 * qk + tc
                        if j >= 2:
                            t_.wait_ge(SC.h, sc_qk(p_, (j - 2) // 4, (j - 2) % 4))
                        for dc in range(8):
                            ins = t_.matmul(psP[:, j % 2, :],
                                            wpT[:, qk, dc, :],
                                            lnT[:, dc, 512 * tc:512 * (tc + 1)],
                                            start=(dc == 0), stop=(dc == 7))
                        TA.inc(ins)
                t_.wait_ge(SC.h, sc_w(p_, 2))
                for sl in range(4):
                    if sl >= 2:
                        t_.wait_ge(VC.h, vc_vev(p_, sl - 2, 3))
                    else:
                        if p_ > 0:
                            t_.wait_ge(VC.h, vc_vev(p_ - 1, 2 + sl, 3))
                        t_.wait_ge(SC.h, sc_qk(p_, 1, 2 + sl))
                    for tt in range(4):
                        kt = 4 * sl + tt
                        for dc in range(8):
                            ins = t_.matmul(psP[:, (8 + sl) % 2, 128 * tt:128 * (tt + 1)],
                                            lnT[:, dc, 128 * kt:128 * (kt + 1)],
                                            wpT[:, 2, dc, :],
                                            start=(dc == 0), stop=(dc == 7))
                    TA.inc(ins)
                for w in range(2):
                    b = 64 * w
                    for qc in range(4):
                        nkt = NKT(qc)
                        for kt in range(nkt):
                            pr = None
                            if kt >= 2:
                                pr = sc_exp(p_, w, qc, kt - 2)
                            elif (w, qc) != (0, 0):
                                pw, pq = (w, qc - 1) if qc > 0 else (0, 3)
                                pkt = NKT(pq) - 2 + kt
                                pr = sc_exp(p_, pw, pq, pkt)
                            elif p_ > 0:
                                pr = sc_exp(p_ - 1, 1, 3, 14 + kt)
                            if pr is not None:
                                t_.wait_ge(SC.h, pr)
                            if kt == 0:
                                t_.wait_ge(VC.h, vc_rope(p_, 1, 1))
                            ins = t_.matmul(psS[:, kt % 2, :],
                                            Kr[b:b + 64, 128 * kt:128 * (kt + 1)],
                                            Qr[b:b + 64, 512 * qc:512 * (qc + 1)],
                                            start=True, stop=True)
                            TA.inc(ins)
                        t_.wait_ge(SC.h, sc_exp(p_, w, qc, nkt - 1))
                        t_.wait_ge(VC.h, vc_msk(p_, w, qc, 3))
                        i = p_ * 4 + qc
                        if (w, qc) != (0, 0) or p_ > 0:
                            if qc > 0:
                                pp, pw, pq = p_, w, qc - 1
                            elif w == 1:
                                pp, pw, pq = p_, 0, 3
                            else:
                                pp, pw, pq = p_ - 1, 1, 3
                            t_.wait_ge(SC.h, sc_o64(pp, pw, pq))
                            t_.wait_ge(VC.h, vc_rec(pp, pw, pq))
                        for kt in range(nkt):
                            ins = t_.matmul(psA[0:65, :],
                                            Vb[:, kt, 65 * w:65 * w + 65],
                                            Eb[:, kt, 512 * 0:512],
                                            start=(kt == 0), stop=(kt == nkt - 1))
                        TA.inc(ins)
                        t_.wait_ge(VC.h, vc_rec(p_, w, qc))
                        TA.inc(t_.matmul(psN[0:64, :], onesf2[64:65, 0:64],
                                         recRb[64:65, :], start=True, stop=True))

        @blk.scalar
        def _(s_):
            s = CE(s_, SC)
            for p_ in range(8):
                for w3 in range(3):
                    s.w(TA, ta_tr(p_, w3))
                    r = s.copy(wpT[:, w3, :, :],
                               psW[:, w3 % 2, :].rearrange("p (c i) -> p c i", i=128))
                    assert r == sc_w(p_, w3)
                for qk in range(2):
                    dst = Qf if qk == 0 else Kf
                    bcol = bcol_bq if qk == 0 else bcol_bk
                    for tc in range(4):
                        s.w(TA, ta_qk(p_, qk, tc))
                        if tc == 0 and p_ > 0:
                            s.w(WR[qk % 2], wr_val[(p_ - 1, qk)])
                            s.w(GC, gc_rp(p_ - 1, qk))
                            s.w(VC, vc_rope(p_ - 1, qk, 1))
                        r = s.activation(dst[:, 512 * tc:512 * (tc + 1)],
                                         psP[:, (4 * qk + tc) % 2, :], AF.Identity,
                                         bias=bcol[:, p_:p_ + 1])
                        assert r == sc_qk(p_, qk, tc)
                for w in range(2):
                    for qc in range(4):
                        nkt = NKT(qc)
                        for kt in range(nkt):
                            s.w(TA, ta_sc(p_, w, qc, kt))
                            if kt == 0:
                                if qc > 0:
                                    s.w(TA, ta_av(p_, w, qc - 1))
                                elif w == 1:
                                    s.w(TA, ta_av(p_, 0, 3))
                                elif p_ > 0:
                                    s.w(TA, ta_av(p_ - 1, 1, 3))
                            r = s.activation(Eb[:, kt, :], psS[:, kt % 2, :],
                                             AF.Exp, scale=ISR64)
                            assert r == sc_exp(p_, w, qc, kt)
                        s.w(TA, ta_av(p_, w, qc))
                        r = s.copy(o64[0:64, (4 * w + qc) % 2, :], psA[0:64, :])
                        assert r == sc_o64(p_, w, qc)

        @blk.vector
        def _(v_):
            v = CE(v_, VC)
            v.memset(Vb[:, :, 64:65], 1.0)
            v.memset(Vb[:, :, 129:130], 1.0)
            for p_ in range(8):
                for sl in range(4):
                    for tt in range(4):
                        kt = 4 * sl + tt
                        v.w(TA, ta_v(p_, sl))
                        r = v.tensor_copy(
                            Vb[:, kt, :].rearrange("p (w u) -> p w u", u=65)[:, :, 0:64],
                            psP[:, (8 + sl) % 2, 128 * tt:128 * (tt + 1)]
                            .rearrange("p (w u) -> p w u", u=64))
                        assert r == vc_vev(p_, sl, tt)
                for qk in range(2):
                    src = Qf if qk == 0 else Kf
                    rr = rot if qk == 0 else rtm
                    dst = Qr if qk == 0 else Kr
                    v.w(WR[qk % 2], wr_val[(p_, qk)])
                    v.w(GC, gc_rp(p_, qk))
                    r = v.tensor_mul(rr[:, :], rr[:, :], sinb[:, :])
                    assert r == vc_rope(p_, qk, 0)
                    r = v.tensor_add(dst[:, :], src[:, :], rr[:, :])
                    assert r == vc_rope(p_, qk, 1)
                for w in range(2):
                    for qc in range(4):
                        i = p_ * 4 + qc
                        for j in range(4):
                            kt = 4 * qc + j
                            v.w(SC, sc_exp(p_, w, qc, kt))
                            v.w(WK, wk_msk)
                            r = v.tensor_mul(Eb[:, kt, :], Eb[:, kt, :],
                                             maskb[:, j, :])
                            assert r == vc_msk(p_, w, qc, j)
                        v.w(TA, ta_av(p_, w, qc))
                        r = v.reciprocal(recRb[64:65, :], psA[64:65, :])
                        assert r == vc_rec(p_, w, qc)
                        v.w(TA, ta_bc(p_, w, qc))
                        v.w(SC, sc_o64(p_, w, qc))
                        if w == 0:
                            v.tensor_mul(aoTb[0:64, p_, 512 * qc:512 * (qc + 1)],
                                         o64[0:64, (4 * w + qc) % 2, :],
                                         psN[0:64, :])
                            r = v.tensor_scalar_add(
                                aoTb[0:64, p_, 512 * qc:512 * (qc + 1)],
                                aoTb[0:64, p_, 512 * qc:512 * (qc + 1)],
                                bcol_bv2[0][0:64, p_:p_ + 1])
                        else:
                            if wm_prev.get(i) is not None:
                                v.w(WM[qc], wm_prev[i])
                            v.tensor_mul(aostg[0:64, qc, :],
                                         o64[0:64, (4 * w + qc) % 2, :],
                                         psN[0:64, :])
                            r = v.tensor_scalar_add(aostg[0:64, qc, :],
                                                    aostg[0:64, qc, :],
                                                    bcol_bv2[1][0:64, p_:p_ + 1])
                        assert r == vc_mul(p_, w, qc)

        fence(blk, [(TA, TA.n), (SC, SC.n), (VC, VC.n), (GC, GC.n)] +
              [(sem, sem.n) for sem in WM])
        cw.close()

        with ExitStack() as co:
            WoT = co.enter_context(nc.sbuf_tensor("b_WoT", [128, 8, DM], BF16))
            psW2 = co.enter_context(nc.psum_tensor("b_psWo", [128, 2, DM], BF16))
            psY = co.enter_context(nc.psum_tensor("b_psY", [128, 2, 512], F32))

            state["wfree"] = None
            state["phase_done"] = [(TA, TA.n), (SC, SC.n), (VC, VC.n)]
            ta_w0, scw = jit_w(p, blk, dr["bw_o"], 0, 8, cn, state, psW2, WoT)

            ta0b = TA.n
            vc0b = VC.n
            gc0b = GC.n
            vc_stt = lambda t: vc0b + 2 * (t + 1)
            gc_add = lambda t: gc0b + t + 1
            xi_val, xo_val, gate = plan_stream(state)

            @blk.sync
            def _(y):
                for t in range(NT):
                    sl = t % NS
                    for sem, val in gate[t]:
                        y.wait_ge(sem.h, val)
                    vv = XI[sl].inc(y.dma_start(out=xsl[:, sl, :],
                                                in_=xin[128 * t:128 * (t + 1), :]), 16)
                    assert vv == xi_val[t]

            @blk.tensor
            def _(t_):
                t_.wait_ge(SC.h, scw(7))
                i = 0
                for t in range(NT):
                    for vch in range(2):
                        if i >= 2:
                            t_.wait_ge(VC.h, vc0b + (i - 1))
                        for hp in range(8):
                            ins = t_.matmul(psY[:, i % 2, :],
                                            aoTb[:, hp, 128 * t:128 * (t + 1)],
                                            WoT[:, hp, 512 * vch:512 * (vch + 1)],
                                            start=(hp == 0), stop=(hp == 7))
                        TA.inc(ins)
                        i += 1

            @blk.vector
            def _(v_):
                v = CE(v_, VC)
                i = 0
                for t in range(NT):
                    v.w(XI[t % NS], xi_val[t])
                    for vch in range(2):
                        v.w(TA, ta0b + i + 1)
                        r = v.scalar_tensor_tensor(
                            xsl[:, t % NS, 512 * vch:512 * (vch + 1)],
                            psY[:, i % 2, :], 1.0,
                            xsl[:, t % NS, 512 * vch:512 * (vch + 1)],
                            ALU.mult, ALU.add)
                        i += 1
                    assert r == vc_stt(t)

            @blk.gpsimd
            def _(g_):
                g = CE(g_, GC)
                for t in range(NT):
                    g.w(VC, vc_stt(t))
                    r = g.tensor_add(xsl[:, t % NS, :], xsl[:, t % NS, :],
                                     borep_b[:, :])
                    assert r == gc_add(t)

            @blk.scalar
            def _(s_):
                for t in range(NT):
                    s_.wait_ge(GC.h, gc_add(t))
                    vv = XO[t % NS].inc(s_.dma_start(out=xout[128 * t:128 * (t + 1), :],
                                                     in_=xsl[:, t % NS, :]), 16)
                    assert vv == xo_val[t]

            xow = [(XO[t % NS], xo_val[t]) for t in range(NT - NS, NT)]
            state["phase_done"] = [(TA, TA.n), (VC, VC.n), (GC, GC.n), (SC, SC.n)] + xow
            fence(blk, [(TA, TA.n), (VC, VC.n), (GC, GC.n), (SC, SC.n)] + xow,
                  engines=("tensor", "scalar", "vector", "gpsimd"))


def make_consts():
    ident = np.eye(128, dtype=np.float32)
    hd = 64
    theta = 1.0 / (10000.0 ** (np.arange(0, hd, 2, dtype=np.float64) / hd))
    pos = np.arange(1, TOK + 1, dtype=np.float64)
    ang = pos[None, :] * theta[:, None]
    cos = np.cos(ang).astype(np.float32)
    sin = np.sin(ang).astype(np.float32)
    cos64 = np.concatenate([cos, cos], axis=0)
    sinm64 = np.concatenate([-sin, sin], axis=0)
    c_cos = np.concatenate([cos64, cos64], axis=0)
    c_sinm = np.concatenate([sinm64, sinm64], axis=0)
    masks = np.zeros((4, 128, 512), dtype=np.float32)
    for o in range(4):
        kk = np.arange(128)[:, None] + 128 * o
        qq = np.arange(512)[None, :]
        masks[o] = (kk <= qq).astype(np.float32)
    sel4 = np.zeros((4, 128), dtype=np.float32)
    for k in range(4):
        sel4[k, 32 * k:32 * k + 32] = 1.0
    return ident, c_cos, c_sinm, masks, sel4


def kernel(L0_attn_a_x, L0_attn_a_in_proj_w, L0_attn_a_in_proj_b, L0_attn_a_out_w,
           L0_attn_a_out_b, L0_attn_b_Wq, L0_attn_b_bq, L0_attn_b_Wk, L0_attn_b_bk,
           L0_attn_b_Wv, L0_attn_b_bv, L0_attn_b_Wo, L0_attn_b_bo, L0_ffn_q_w,
           L0_ffn_q_b, L0_ffn_k_w, L0_ffn_k_b, L0_ffn_v_w, L0_ffn_v_b, L0_ffn_o_w,
           L0_ffn_o_b, L0_ffn_router_w, L0_ffn_router_b, L0_ffn_w1, L0_ffn_b1,
           L0_ffn_w2, L0_ffn_b2, _upto=4):
    x = np.ascontiguousarray(np.asarray(L0_attn_a_x, dtype=np.float32))
    ident, c_cos, c_sinm, masks, sel4 = make_consts()
    f32c = lambda a: np.ascontiguousarray(np.asarray(a, dtype=np.float32))
    row = lambda a: np.ascontiguousarray(np.asarray(a, dtype=np.float32).reshape(1, -1))
    shared = {
        "aw_inproj": f32c(L0_attn_a_in_proj_w), "aw_inb": row(L0_attn_a_in_proj_b),
        "aw_outw": f32c(L0_attn_a_out_w), "aw_outb": row(L0_attn_a_out_b),
        "bw_q": f32c(L0_attn_b_Wq), "bw_k": f32c(L0_attn_b_Wk),
        "bw_v": f32c(L0_attn_b_Wv), "bw_o": f32c(L0_attn_b_Wo),
        "bb_q": row(L0_attn_b_bq), "bb_k": row(L0_attn_b_bk),
        "bb_v": row(L0_attn_b_bv), "bb_o": row(L0_attn_b_bo),
        "fw_q": f32c(L0_ffn_q_w), "fw_k": f32c(L0_ffn_k_w),
        "fw_v": f32c(L0_ffn_v_w), "fw_o": f32c(L0_ffn_o_w),
        "fb_q": row(L0_ffn_q_b), "fb_k": row(L0_ffn_k_b),
        "fb_v": row(L0_ffn_v_b), "fb_o": row(L0_ffn_o_b),
        "fw_r": f32c(L0_ffn_router_w), "fb_r": row(L0_ffn_router_b),
        "fw_1": f32c(L0_ffn_w1), "fb_1": f32c(L0_ffn_b1),
        "fw_2": f32c(L0_ffn_w2), "fb_2": f32c(L0_ffn_b2),
        "c_ident": ident, "c_cos": c_cos, "c_sinm": c_sinm, "c_mask": masks,
        "c_sel4": sel4,
    }
    nc = build_program(upto=_upto)
    in_maps = []
    for i in range(NC8):
        m = dict(shared)
        m["x"] = np.ascontiguousarray(x[i])
        in_maps.append(m)
    res = run_bass_kernel_spmd(nc, in_maps, core_ids=list(range(NC8)))
    return np.stack([np.asarray(r["out"], dtype=np.float32) for r in res.results], axis=0)


# version 04
# speedup vs baseline: 1.1840x; 1.1840x over previous
import numpy as np
import ml_dtypes
from contextlib import ExitStack
import concourse.bass as bass
import concourse.mybir as mybir
from concourse.bass_utils import run_bass_kernel_spmd

F32 = mybir.dt.float32
BF16 = mybir.dt.bfloat16
AF = mybir.ActivationFunctionType
ALU = mybir.AluOpType
AX = mybir.AxisListType

TOK = 2048
NT = TOK // 128
DM = 1024
NC8 = 8
NS = 4

ISR128 = float(1.0 / np.sqrt(128.0))
ISR32 = float(1.0 / np.sqrt(32.0))
ISR64 = float(1.0 / np.sqrt(64.0))
EPS_LN = 1e-5
_DBG_MODE = "full"


class Sem:
    __slots__ = ("h", "n")

    def __init__(self, handle):
        self.h = handle
        self.n = 0

    def inc(self, instr, k=1):
        instr.then_inc(self.h, k)
        self.n += k
        return self.n


class CE:

    _RAW = {"wait_ge", "dma_start", "nop"}

    def __init__(self, eng, chain):
        self._e = eng
        self._c = chain

    @property
    def n(self):
        return self._c.n

    def w(self, sem, val):
        self._e.wait_ge(sem.h, val)

    def raw(self):
        return self._e

    def __getattr__(self, name):
        f = getattr(self._e, name)
        if name in CE._RAW:
            return f

        def g(*a, **k):
            self._e.wait_ge(self._c.h, self._c.n)
            ins = f(*a, **k)
            return self._c.inc(ins)
        return g


class P:
    def __init__(self, nc, ctx):
        self.nc = nc
        self.ctx = ctx
        self._sems = {}

    def sem(self, name):
        if name not in self._sems:
            self._sems[name] = Sem(self.ctx.enter_context(self.nc.semaphore(name)))
        return self._sems[name]


def fence(blk, waits, engines=("tensor", "scalar", "vector")):
    def body(e):
        for sem, val in waits:
            e.wait_ge(sem.h, val)
    for name in engines:
        getattr(blk, name)(body)


def plan_stream(state, nt=NT, store=True):
    XI, XO = state["XI"], state["XO"]
    xi_n = [s.n for s in XI]
    xi_val, gate = [], []
    prev = [list(fr) for fr in state["slot_free"]]
    xo_val = [None] * nt
    if store:
        xo_n = [s.n for s in XO]
    for t in range(nt):
        s = t % NS
        gate.append(list(prev[s]))
        xi_n[s] += 16
        xi_val.append(xi_n[s])
        if store:
            xo_n[s] += 16
            xo_val[t] = xo_n[s]
            prev[s] = [(XO[s], xo_n[s])]
        else:
            prev[s] = []
    if store:
        state["slot_free"] = prev
    return xi_val, xo_val, gate


def build_program(upto=4):
    nc = bass.Bass("TRN2", target_bir_lowering=False)
    ctx = ExitStack()
    p = P(nc, ctx)

    dr = {}
    def din(name, shape):
        dr[name] = nc.dram_tensor(name, list(shape), F32, kind="ExternalInput").ap()

    din("x", (TOK, DM))
    din("aw_inproj", (3072, DM)); din("aw_inb", (1, 3072))
    din("aw_outw", (DM, DM)); din("aw_outb", (1, DM))
    din("bw_q", (DM, DM)); din("bw_k", (DM, DM)); din("bw_v", (DM, DM)); din("bw_o", (DM, DM))
    din("bb_q", (1, DM)); din("bb_k", (1, DM)); din("bb_v", (1, DM)); din("bb_o", (1, DM))
    din("fw_q", (DM, DM)); din("fw_k", (DM, 512)); din("fw_v", (DM, 512)); din("fw_o", (DM, DM))
    din("fb_q", (1, DM)); din("fb_k", (1, 512)); din("fb_v", (1, 512)); din("fb_o", (1, DM))
    din("fw_r", (DM, 8)); din("fb_r", (1, 8))
    din("fw_1", (8, DM, 2048)); din("fb_1", (8, 2048))
    din("fw_2", (8, 2048, DM)); din("fb_2", (8, DM))
    din("c_ident", (128, 128))
    din("c_cos", (128, TOK)); din("c_sinm", (128, TOK))
    din("c_mask", (4, 128, 512)); din("c_sel4", (4, 128))
    out_hbm = nc.dram_tensor("out", [TOK, DM], F32, kind="ExternalOutput").ap()
    xsA = nc.dram_tensor("xsA", [TOK, DM], F32, kind="Internal").ap()
    xsB = nc.dram_tensor("xsB", [TOK, DM], F32, kind="Internal").ap()

    with ctx:
        emit(p, dr, out_hbm, xsA, xsB, upto)
    return nc


def emit(p, dr, out_hbm, xsA, xsB, upto):
    nc = p.nc
    blk = p.ctx.enter_context(nc.Block())
    alloc = lambda n, sh, dt=F32: p.ctx.enter_context(nc.sbuf_tensor(n, sh, dt))

    xsl = alloc("xsl", [128, NS, DM], F32)
    identb = alloc("identb", [128, 128], BF16)
    identf = alloc("identf", [128, 128], F32)
    onesf = alloc("onesf", [1, 128], F32)
    onesf2 = alloc("onesf2", [128, 64], F32)
    onesb = alloc("onesb", [128, 1], BF16)

    CD = p.sem("CD")
    SC = p.sem("SC"); VC = p.sem("VC"); GC = p.sem("GC")
    TM = p.sem("TM")
    XI = [p.sem(f"XI{s}") for s in range(NS)]
    XO = [p.sem(f"XO{s}") for s in range(NS)]
    WID = p.sem("WID")

    @blk.sync
    def _(y):
        CD.inc(y.dma_start(out=identf[:, :], in_=dr["c_ident"]), 16)

    ge_ones = [None]

    @blk.gpsimd
    def _(g_):
        g = CE(g_, GC)
        WID.inc(g_.dma_start(out=identb[:, :], in_=dr["c_ident"]), 16)
        g.memset(onesf[:, :], 1.0)
        g.memset(onesf2[:, :], 1.0)
        ge_ones[0] = g.memset(onesb[:, :], 1.0)
    ge_ones = ge_ones[0]

    bcol_a = alloc("bcol_a", [128, 16], F32)
    bcol_fo = alloc("bcol_fo", [128, 8], F32)
    bcol_fq = alloc("bcol_fq", [128, 4], F32)
    bcol_fk = alloc("bcol_fk", [128, 4], F32)
    bcol_fv2 = alloc("bcol_fv2", [128, 4], F32)
    bcol_bq = alloc("bcol_bq", [128, 8], F32)
    bcol_bk = alloc("bcol_bk", [128, 8], F32)
    bcol_f1 = alloc("bcol_f1", [128, 8, 16], F32)
    bcol_bve = alloc("bcol_bve", [64, 8], F32)
    bcol_bvo = alloc("bcol_bvo", [64, 8], F32)
    sel4f = alloc("sel4f", [128, 128], F32)
    borep_b = alloc("borep_b", [128, DM], F32)
    rbrep = alloc("rbrep", [128, 8], F32)
    fb2n = alloc("fb2n", [8, DM], BF16)
    arep = ExitStack()
    bvrep_a = arep.enter_context(nc.sbuf_tensor("bvrep_a", [128, DM], F32))
    borep_a2 = arep.enter_context(nc.sbuf_tensor("borep_a2", [128, DM], F32))

    with ExitStack() as pc:
        brows = [("aw_inb", 3072), ("aw_outb", DM), ("bb_q", DM), ("bb_k", DM),
                 ("bb_v", DM), ("bb_o", DM), ("fb_q", DM), ("fb_k", 512),
                 ("fb_v", 512), ("fb_o", DM), ("fb_r", 8)]
        brow = {}
        for name, n in brows:
            brow[name] = pc.enter_context(nc.sbuf_tensor("brow_" + name, [1, n], F32))
        fb1row = pc.enter_context(nc.sbuf_tensor("fb1row", [1, 16384], F32))
        fb2n32 = pc.enter_context(nc.sbuf_tensor("fb2n32", [8, DM], F32))
        qbrow = pc.enter_context(nc.sbuf_tensor("qbrow", [1, 512], F32))
        pbias = pc.enter_context(nc.psum_tensor("pbias", [128, 2, 512], F32))
        prep = pc.enter_context(nc.psum_tensor("prep", [128, 2, 512], F32))

        @blk.sync
        def _(y):
            for name, n in brows:
                CD.inc(y.dma_start(out=brow[name][:, :], in_=dr[name]), 16)
            CD.inc(y.dma_start(out=fb1row[:, :],
                               in_=dr["fb_1"].rearrange("e f -> (e f)")
                               .rearrange("(o f) -> o f", o=1)), 16)
            CD.inc(y.dma_start(out=fb2n32[:, :], in_=dr["fb_2"]), 16)
            CD.inc(y.dma_start(out=sel4f[0:4, :], in_=dr["c_sel4"]), 16)

        ve_pro = [None]

        @blk.vector
        def _(v_):
            v = CE(v_, VC)
            v.w(CD, CD.n)
            v.tensor_copy(fb2n[:, :], fb2n32[:, :])
            ve_pro[0] = v.tensor_copy(
                qbrow[:, :].rearrange("o (h s) -> o h s", s=32),
                brow["fb_q"][:, :].rearrange("o (h s) -> o h s", s=64)[:, :, 0:32])
        ve_pro = ve_pro[0]

        jobs = []
        for c in range(16):
            jobs.append((bcol_a[:, c:c + 1], brow["aw_inb"][0:1, 128 * c:128 * (c + 1)], 128))
        for c in range(8):
            jobs.append((bcol_fo[:, c:c + 1], brow["fb_o"][0:1, 128 * c:128 * (c + 1)], 128))
        for cc in range(4):
            jobs.append((bcol_fq[0:64, cc:cc + 1], qbrow[0:1, 128 * cc:128 * cc + 64], 64))
            jobs.append((bcol_fq[64:128, cc:cc + 1], qbrow[0:1, 128 * cc + 64:128 * (cc + 1)], 64))
            jobs.append((bcol_fk[0:64, cc:cc + 1], brow["fb_k"][0:1, 128 * cc:128 * cc + 64], 64))
            jobs.append((bcol_fk[64:128, cc:cc + 1], brow["fb_k"][0:1, 128 * cc + 64:128 * (cc + 1)], 64))
            jobs.append((bcol_fv2[0:64, cc:cc + 1], brow["fb_v"][0:1, 128 * cc:128 * cc + 64], 64))
            jobs.append((bcol_fv2[64:128, cc:cc + 1], brow["fb_v"][0:1, 128 * cc + 64:128 * (cc + 1)], 64))
        for c in range(8):
            jobs.append((bcol_bq[:, c:c + 1], brow["bb_q"][0:1, 128 * c:128 * (c + 1)], 128))
            jobs.append((bcol_bk[:, c:c + 1], brow["bb_k"][0:1, 128 * c:128 * (c + 1)], 128))
        for g in range(8):
            jobs.append((bcol_bve[0:64, g:g + 1], brow["bb_v"][0:1, 128 * g:128 * g + 64], 64))
            jobs.append((bcol_bvo[0:64, g:g + 1], brow["bb_v"][0:1, 128 * g + 64:128 * (g + 1)], 64))
        for e in range(8):
            for c in range(16):
                jobs.append((bcol_f1[:, e, c:c + 1],
                             fb1row[0:1, 2048 * e + 128 * c:2048 * e + 128 * (c + 1)], 128))

        tm0 = TM.n
        sc0 = SC.n
        if _DBG_MODE == "nobias":
            jobs = []
        elif _DBG_MODE.startswith("jobs:"):
            a, b = _DBG_MODE[5:].split(":")
            jobs = jobs[int(a):int(b)]

        @blk.tensor
        def _(t_):
            t_.wait_ge(CD.h, CD.n)
            t_.wait_ge(GC.h, ge_ones)
            t_.wait_ge(VC.h, ve_pro)
            for i, (dst, src, m) in enumerate(jobs):
                g = i // 16
                if g >= 2 and i % 16 == 0:
                    t_.wait_ge(SC.h, sc0 + 16 * (g - 1))
                TM.inc(t_.matmul(pbias[0:m, g % 2, (i % 16):(i % 16) + 1],
                                 src, onesf[0:1, 0:1], start=True, stop=True))

        @blk.scalar
        def _(s_):
            s = CE(s_, SC)
            for i, (dst, src, m) in enumerate(jobs):
                g = i // 16
                s.w(TM, tm0 + min(len(jobs), (g + 1) * 16))
                s.copy(dst, pbias[0:m, g % 2, (i % 16):(i % 16) + 1])
        assert SC.n == sc0 + len(jobs)

        reps = []
        for j in range(2):
            reps.append((bvrep_a[:, 512 * j:512 * (j + 1)],
                         brow["aw_inb"][0:1, 2048 + 512 * j:2048 + 512 * (j + 1)], 512, 1.0))
        for j in range(2):
            reps.append((borep_a2[:, 512 * j:512 * (j + 1)],
                         brow["aw_outb"][0:1, 512 * j:512 * (j + 1)], 512, 2.0))

        for j in range(2):
            reps.append((borep_b[:, 512 * j:512 * (j + 1)],
                         brow["bb_o"][0:1, 512 * j:512 * (j + 1)], 512, 1.0))
        reps.append((rbrep[:, :], brow["fb_r"][0:1, :], 8, 1.0))

        tr0 = TM.n
        sr0 = SC.n
        if _DBG_MODE in ("nobias", "norep"):
            reps = []

        @blk.tensor
        def _(t_):
            for i, (dst, src, w, sc) in enumerate(reps):
                if i >= 2:
                    t_.wait_ge(SC.h, sr0 + (i - 1))
                TM.inc(t_.matmul(prep[:, i % 2, 0:w], onesf[0:1, :], src,
                                 start=True, stop=True))

        @blk.scalar
        def _(s_):
            s = CE(s_, SC)
            for i, (dst, src, w, sc) in enumerate(reps):
                s.w(TM, tr0 + i + 1)
                if sc == 1.0:
                    s.copy(dst, prep[:, i % 2, 0:w])
                else:
                    s.activation(dst, prep[:, i % 2, 0:w], AF.Copy, scale=sc)

        fence(blk, [(SC, SC.n), (TM, TM.n), (VC, ve_pro), (WID, 16)])

    cn = {"xsl": xsl, "wstage": None, "identb": identb, "identf": identf,
          "onesf": onesf, "onesf2": onesf2, "onesb": onesb, "sel4f": sel4f}
    state = {"wfree": None, "XI": XI, "XO": XO,
             "slot_free": [[] for _ in range(NS)]}
    state["phase_done"] = [(SC, SC.n), (TM, TM.n), (VC, VC.n)]

    seq = []
    if upto >= 1:
        seq.append(("a", xsA, xsB if upto > 1 else out_hbm))
    if upto >= 2:
        seq.append(("f1", xsA, xsB if upto > 2 else out_hbm))
    if upto >= 3:
        seq.append(("b", xsA, xsB if upto > 3 else out_hbm))
    if upto >= 4:
        seq.append(("f2", xsA, out_hbm))

    xin = dr["x"]
    for tag, xmid, xout in seq:
        if tag == "a":
            emit_attn_a(p, blk, dr, cn, bcol_a, bvrep_a, borep_a2,
                        state, xin, xmid, xout)
            arep.close()
        elif tag in ("f1", "f2"):
            emit_ffn(p, blk, dr, cn, bcol_fq, bcol_fk, bcol_fo, bcol_f1,
                     bcol_fv2, rbrep, fb2n, state, tag, xin, xmid, xout)
        else:
            emit_attn_b(p, blk, dr, cn, bcol_bq, bcol_bk, (bcol_bve, bcol_bvo), borep_b,
                        state, xin, xmid, xout)
        xin = xout

    if upto == 0:
        xi_val, xo_val, gate = plan_stream(state)

        @blk.sync
        def _(y):
            for t in range(NT):
                s = t % NS
                for sem, val in gate[t]:
                    y.wait_ge(sem.h, val)
                v = XI[s].inc(y.dma_start(out=xsl[:, s, :],
                                          in_=dr["x"][128 * t:128 * (t + 1), :]), 16)
                assert v == xi_val[t]
                y.wait_ge(XI[s].h, xi_val[t])
                v = XO[s].inc(y.dma_start(out=out_hbm[128 * t:128 * (t + 1), :],
                                          in_=xsl[:, s, :]), 16)
                assert v == xo_val[t]

    @blk.sync
    def _(y):
        for fr in state["slot_free"]:
            for sem, val in fr:
                y.wait_ge(sem.h, val)


def emit_ln(p, blk, cn, state, lnT, tag, xin, xmid, dup=0, add_hs=False):
    nc = p.nc
    SC, VC, TM = p.sem("SC"), p.sem("VC"), p.sem("TM")
    XI, XO = state["XI"], state["XO"]
    xsl, identb = cn["xsl"], cn["identb"]

    with ExitStack() as c:
        sums = c.enter_context(nc.sbuf_tensor(f"ln_sums_{tag}", [128, NT], F32))
        sumsq = c.enter_context(nc.sbuf_tensor(f"ln_sumsq_{tag}", [128, NT], F32))
        means = c.enter_context(nc.sbuf_tensor(f"ln_means_{tag}", [128, NT], F32))
        rstds = c.enter_context(nc.sbuf_tensor(f"ln_rstds_{tag}", [128, NT], F32))
        s1p = c.enter_context(nc.sbuf_tensor(f"ln_s1p_{tag}", [128, NT], F32))
        s2p = c.enter_context(nc.sbuf_tensor(f"ln_s2p_{tag}", [128, NT], F32))
        vscr = c.enter_context(nc.sbuf_tensor(f"ln_vscr_{tag}", [128, 4, NT], F32))
        sqscr = c.enter_context(nc.sbuf_tensor(f"ln_sqscr_{tag}", [128, DM], F32))
        lnscr = c.enter_context(nc.sbuf_tensor(f"ln_lnscr_{tag}", [128, 1], F32))
        lnbf = c.enter_context(nc.sbuf_tensor(f"ln_lnbf_{tag}", [128, 2, DM], BF16))
        psT = c.enter_context(nc.psum_tensor(f"ln_psT_{tag}", [128, 2, DM], BF16))

        sc0, vc0, tm0 = SC.n, VC.n, TM.n
        xi_val, xo_val, gate = plan_stream(state, store=add_hs)

        nv3 = 4 if add_hs else 1
        sc_sq, sc_rstd, sc_evac = {}, {}, {}
        vc_sums, vc_veps, vc_xup = {}, {}, {}
        n = sc0
        for u in range(NT + 2):
            if u < NT:
                n += 1; sc_sq[u] = n
            if 1 <= u <= NT:
                n += 2; sc_rstd[u - 1] = n
            if 2 <= u:
                n += 1; sc_evac[u - 2] = n
        m = vc0
        for u in range(NT + 2):
            if u < NT:
                m += 1; vc_sums[u] = m
            if 1 <= u <= NT:
                m += 4; vc_veps[u - 1] = m
            if 2 <= u:
                m += nv3; vc_xup[u - 2] = m
        tm_tr = lambda t: tm0 + t + 1
        if not add_hs:
            fr = [[] for _ in range(NS)]
            for t in range(NT):
                fr[t % NS] = [(VC, vc_xup[t]), (SC, sc_sq[t])]
            state["slot_free"] = fr

        @blk.sync
        def _(y):
            for t in range(NT):
                s = t % NS
                if t >= NS and not add_hs:
                    y.wait_ge(VC.h, vc_xup[t - NS])
                    y.wait_ge(SC.h, sc_sq[t - NS])
                for sem, val in gate[t]:
                    y.wait_ge(sem.h, val)
                v = XI[s].inc(y.dma_start(out=xsl[:, s, :],
                                          in_=xin[128 * t:128 * (t + 1), :]), 16)
                assert v == xi_val[t]

        @blk.scalar
        def _(s_):
            s = CE(s_, SC)
            for u in range(NT + 2):
                if u < NT:
                    s.w(XI[u % NS], xi_val[u])
                    r = s.activation(sqscr[:, :], xsl[:, u % NS, :], AF.Square,
                                     accum_out=sumsq[:, u:u + 1])
                    assert r == sc_sq[u]
                if 1 <= u <= NT:
                    t = u - 1
                    s.w(VC, vc_veps[t])
                    s.activation(lnscr[:, 0:1], vscr[:, 2, t:t + 1], AF.Ln)
                    r = s.activation(rstds[:, t:t + 1], lnscr[:, 0:1], AF.Exp, scale=-0.5)
                    assert r == sc_rstd[t]
                if 2 <= u:
                    t = u - 2
                    s.w(TM, tm_tr(t))
                    r = s.copy(lnT[:, :, 128 * t:128 * (t + 1)],
                               psT[:, t % 2, :].rearrange("p (c t) -> p c t", t=128))
                    assert r == sc_evac[t]
                    if add_hs:
                        s_.wait_ge(VC.h, vc_xup[t])
                        v = XO[t % NS].inc(s_.dma_start(out=xmid[128 * t:128 * (t + 1), :],
                                                        in_=xsl[:, t % NS, :]), 16)
                        assert v == xo_val[t]

        @blk.vector
        def _(v_):
            v = CE(v_, VC)
            for u in range(NT + 2):
                if u < NT:
                    v.w(XI[u % NS], xi_val[u])
                    r = v.reduce_sum(sums[:, u:u + 1], xsl[:, u % NS, :], axis=AX.X)
                    assert r == vc_sums[u]
                if 1 <= u <= NT:
                    t = u - 1
                    v.w(SC, sc_sq[t])
                    v.tensor_scalar(means[:, t:t + 1], sums[:, t:t + 1], 1.0 / DM, None, ALU.mult)
                    v.tensor_scalar(vscr[:, 0, t:t + 1], means[:, t:t + 1], means[:, t:t + 1],
                                    None, ALU.mult)
                    v.tensor_scalar(vscr[:, 1, t:t + 1], sumsq[:, t:t + 1], 1.0 / DM, EPS_LN,
                                    ALU.mult, ALU.add)
                    r = v.tensor_sub(vscr[:, 2, t:t + 1], vscr[:, 1, t:t + 1], vscr[:, 0, t:t + 1])
                    assert r == vc_veps[t]
                if 2 <= u:
                    t = u - 2
                    v.w(SC, sc_rstd[t])
                    if add_hs:
                        v.tensor_scalar(s1p[:, t:t + 1], rstds[:, t:t + 1], 1.0, None, ALU.add)
                        v.tensor_scalar(s2p[:, t:t + 1], means[:, t:t + 1], rstds[:, t:t + 1],
                                        -1.0, ALU.mult, ALU.mult)
                    if t >= 2:
                        v.w(TM, tm_tr(t - 2))
                    r = v.tensor_scalar(lnbf[:, t % 2, :], xsl[:, t % NS, :],
                                        means[:, t:t + 1], rstds[:, t:t + 1],
                                        ALU.subtract, ALU.mult)
                    if add_hs:
                        r = v.tensor_scalar(xsl[:, t % NS, :], xsl[:, t % NS, :],
                                            s1p[:, t:t + 1], s2p[:, t:t + 1],
                                            ALU.mult, ALU.add)
                    assert r == vc_xup[t]

        @blk.tensor
        def _(t_):
            for t in range(NT):
                t_.wait_ge(VC.h, vc_xup[t])
                if t >= 2:
                    t_.wait_ge(SC.h, sc_evac[t - 2])
                for cc in range(8):
                    ins = t_.transpose(psT[:, t % 2, 128 * cc:128 * (cc + 1)],
                                       lnbf[:, t % 2, 128 * cc:128 * (cc + 1)],
                                       identb[:, :])
                TM.inc(ins)
        assert TM.n == tm_tr(NT - 1)

        dup_dep = None
        if dup:
            vdup = [None]

            @blk.vector
            def _(v_):
                v = CE(v_, VC)
                v.w(SC, sc_evac[NT - 1])
                vdup[0] = v.tensor_copy(lnT[:, :, TOK:TOK + dup], lnT[:, :, 0:dup])
            dup_dep = (VC, vdup[0])

        ln_done = [(VC, VC.n), (SC, SC.n), (TM, TM.n)]
        return (SC, sc_evac[NT - 1]), dup_dep, ln_done


def jit_w(p, blk, dr_w, row0, nchunk, cn, state, psW, dst):
    TA, SC = p.sem("TA"), p.sem("SC")
    WJ = [p.sem("WJA"), p.sem("WJB")]
    wstage, identb = cn["wstage"], cn["identb"]
    ta0 = TA.n
    sc0 = SC.n
    sc_evac = lambda k: sc0 + k + 1
    wfree = state["wfree"]
    pdone = state.get("phase_done") or []
    wjv = [None] * nchunk

    @blk.gpsimd
    def _(g):
        for k in range(nchunk):
            if k < 2:
                if wfree is not None:
                    g.wait_ge(wfree[0].h, wfree[1])
                else:
                    for sem, val in pdone:
                        g.wait_ge(sem.h, val)
            else:
                g.wait_ge(TA.h, ta0 + (k - 1))
            wjv[k] = WJ[k % 2].inc(
                g.dma_start(out=wstage[:, k % 2, 0:DM],
                            in_=dr_w[row0 + 128 * k:row0 + 128 * (k + 1), :]), 16)

    @blk.tensor
    def _(t_):
        for k in range(nchunk):
            t_.wait_ge(WJ[k % 2].h, wjv[k])
            if k >= 2:
                t_.wait_ge(SC.h, sc_evac(k - 2))
            for cc in range(8):
                ins = t_.transpose(psW[:, k % 2, 128 * cc:128 * (cc + 1)],
                                   wstage[:, k % 2, 128 * cc:128 * (cc + 1)],
                                   identb[:, :])
            TA.inc(ins)

    @blk.scalar
    def _(s_):
        s = CE(s_, SC)
        for k in range(nchunk):
            s.w(TA, ta0 + k + 1)
            r = s.copy(dst[:, :, 128 * k:128 * (k + 1)],
                       psW[:, k % 2, :].rearrange("p (c i) -> p c i", i=128))
            assert r == sc_evac(k)
    state["wfree"] = (TA, ta0 + nchunk)
    return ta0, sc_evac


def emit_attn_a(p, blk, dr, cn, bcol_a, bvrep_a, borep_a2, state, xin, xmid, xout):
    nc = p.nc
    SC, VC, GC, TM, TA = p.sem("SC"), p.sem("VC"), p.sem("GC"), p.sem("TM"), p.sem("TA")
    XI, XO = state["XI"], state["XO"]
    xsl = cn["xsl"]
    onesf, onesb, identb = cn["onesf"], cn["onesb"], cn["identb"]
    TW = TOK + 64

    with ExitStack() as c:
        sb = lambda n, sh, dt=F32: c.enter_context(nc.sbuf_tensor(n, sh, dt))
        cn["wstage"] = sb("a_wstage", [128, 2, 2048], BF16)
        Vwa = sb("a_Vwa", [128, NT, DM], BF16)
        QKT = sb("a_QKT", [128, 16, TW], BF16)
        cl = c.enter_context(ExitStack())
        lnT = cl.enter_context(nc.sbuf_tensor("a_lnT", [128, 8, TW], BF16))

        ln_all, dup_dep, ln_done = emit_ln(p, blk, cn, state, lnT, "a", xin, None,
                                           dup=64, add_hs=False)
        fence(blk, ln_done)
        if _DBG_MODE == "a:ln":
            _dbg_copy_out(p, blk, cn, state, xin, xout)
            return

        with ExitStack() as cv:
            WvT = cv.enter_context(nc.sbuf_tensor("a_WvT", [128, 8, DM], BF16))
            psW = cv.enter_context(nc.psum_tensor("a_psWv", [128, 2, DM], BF16))
            psV = cv.enter_context(nc.psum_tensor("a_psV", [128, 2, 512], F32))

            ta_w0, scw = jit_w(p, blk, dr["aw_inproj"], 2048, 8, cn, state, psW, WvT)

            ta_v0 = TA.n
            vc0 = VC.n
            vc_vadd = lambda i: vc0 + i + 1

            @blk.tensor
            def _(t_):
                t_.wait_ge(SC.h, scw(7))
                t_.wait_ge(SC.h, ln_all[1])
                t_.wait_ge(dup_dep[0].h, dup_dep[1])
                i = 0
                for wp in range(NT):
                    t0 = 32 + 128 * wp
                    for vch in range(2):
                        if i >= 2:
                            t_.wait_ge(VC.h, vc_vadd(i - 2))
                        for cc in range(8):
                            ins = t_.matmul(psV[:, i % 2, :],
                                            lnT[:, cc, t0:t0 + 128],
                                            WvT[:, cc, 512 * vch:512 * (vch + 1)],
                                            start=(cc == 0), stop=(cc == 7))
                        TA.inc(ins)
                        i += 1

            @blk.vector
            def _(v_):
                v = CE(v_, VC)
                i = 0
                for wp in range(NT):
                    for vch in range(2):
                        v.w(TA, ta_v0 + i + 1)
                        r = v.tensor_add(Vwa[:, wp, 512 * vch:512 * (vch + 1)],
                                         psV[:, i % 2, :],
                                         bvrep_a[:, 512 * vch:512 * (vch + 1)])
                        assert r == vc_vadd(i)
                        i += 1
            vc_v = vc_vadd
            fence(blk, [(VC, VC.n), (TA, TA.n), (SC, SC.n)])
        if _DBG_MODE == "a:v":
            _dbg_copy_out(p, blk, cn, state, xin, xout)
            return

        with ExitStack() as cq:
            WqkTr = cq.enter_context(nc.sbuf_tensor("a_WqkTr", [128, 2, 8, 128], BF16))
            psW = cq.enter_context(nc.psum_tensor("a_psWq", [128, 2, DM], BF16))
            psQ = cq.enter_context(nc.psum_tensor("a_psQ", [128, 2, 512], F32))

            WJ = [p.sem("WJA"), p.sem("WJB")]
            wstage = cn["wstage"]
            ta0, sc0 = TA.n, SC.n
            NJ = TOK // 512
            wfree = state["wfree"]
            wjv = [None] * 16
            sc_w = lambda k: sc0 + k * (1 + NJ) + 1
            sc_q = lambda k, j: sc0 + k * (1 + NJ) + 1 + (j + 1)

            @blk.gpsimd
            def _(g):
                for k in range(16):
                    if k < 2:
                        g.wait_ge(wfree[0].h, wfree[1])
                    else:
                        g.wait_ge(TA.h, ta0 + (k - 1) * (1 + NJ) + 1)
                    wjv[k] = WJ[k % 2].inc(
                        g.dma_start(out=wstage[:, k % 2, 0:DM],
                                    in_=dr["aw_inproj"][128 * k:128 * (k + 1), :]), 16)

            @blk.tensor
            def _(t_):
                for k in range(16):
                    t_.wait_ge(WJ[k % 2].h, wjv[k])
                    if k >= 2:
                        t_.wait_ge(SC.h, sc_w(k - 2))
                    for cc in range(8):
                        ins = t_.transpose(psW[:, k % 2, 128 * cc:128 * (cc + 1)],
                                           wstage[:, k % 2, 128 * cc:128 * (cc + 1)],
                                           identb[:, :])
                    TA.inc(ins)
                    t_.wait_ge(SC.h, sc_w(k))
                    for j in range(NJ):
                        gi = k * NJ + j
                        if gi >= 2:
                            pk, pj = (gi - 2) // NJ, (gi - 2) % NJ
                            t_.wait_ge(SC.h, sc_q(pk, pj))
                        for cc in range(8):
                            ins = t_.matmul(psQ[:, gi % 2, :],
                                            WqkTr[:, k % 2, cc, :],
                                            lnT[:, cc, 512 * j:512 * (j + 1)],
                                            start=(cc == 0), stop=(cc == 7))
                        TA.inc(ins)

            @blk.scalar
            def _(s_):
                s = CE(s_, SC)
                for k in range(16):
                    s.w(TA, ta0 + k * (1 + NJ) + 1)
                    r = s.copy(WqkTr[:, k % 2, :, :],
                               psW[:, k % 2, :].rearrange("p (c i) -> p c i", i=128))
                    assert r == sc_w(k)
                    for j in range(NJ):
                        s.w(TA, ta0 + k * (1 + NJ) + 1 + (j + 1))
                        r = s.activation(QKT[:, k, 512 * j:512 * (j + 1)],
                                         psQ[:, (k * NJ + j) % 2, :], AF.Identity,
                                         bias=bcol_a[:, k:k + 1])
                        assert r == sc_q(k, j)

            vdup = [None]

            @blk.vector
            def _(v_):
                v = CE(v_, VC)
                v.w(SC, SC.n)
                vdup[0] = v.tensor_copy(QKT[:, :, TOK:TW], QKT[:, :, 0:64])
            vc_qkdup = vdup[0]

            state["wfree"] = (TA, ta0 + 15 * (1 + NJ) + 1)
            fence(blk, [(SC, SC.n), (TA, TA.n), (VC, VC.n)])

        cl.close()
        if _DBG_MODE == "a:qk":
            _dbg_copy_out(p, blk, cn, state, xin, xout)
            return

        OT = sb("a_OT", [128, 8, TOK], BF16)
        with ExitStack() as ca:
            EA = ca.enter_context(nc.sbuf_tensor("a_EA", [128, 512], BF16))
            Asb = ca.enter_context(nc.sbuf_tensor("a_Asb", [128, 512], BF16))
            recD = ca.enter_context(nc.sbuf_tensor("a_recD", [1, 1024], F32))
            psS = ca.enter_context(nc.psum_tensor("a_psS", [128, 512], F32))
            psDa = ca.enter_context(nc.psum_tensor("a_psDa", [1, 512], F32))
            psDb = ca.enter_context(nc.psum_tensor("a_psDb", [1, 512], F32))
            psB = ca.enter_context(nc.psum_tensor("a_psB", [128, 512], F32))
            psO = ca.enter_context(nc.psum_tensor("a_psO", [128, 2, 1024], F32))

            tb0, sc0, vc0 = TA.n, SC.n, VC.n
            sc_exp, sc_ot = [], []
            n = sc0
            for wp in range(NT):
                n += 1
                sc_exp.append(n)
                n += 3 if wp == NT - 1 else 1
                sc_ot.append(n)
            vc_rec = lambda wp: vc0 + 3 * wp + 2
            vc_mul = lambda wp: vc0 + 3 * wp + 3

            if _DBG_MODE.startswith("a:att:"):
                _stage = _DBG_MODE.split(":")[2]
                _nwp = int(_DBG_MODE.split(":")[3])
            else:
                _stage, _nwp = "av", NT
            _so = {"s": 0, "d": 1, "b": 2, "m": 3, "av": 4}[_stage]

            @blk.tensor
            def _(t_):
                t_.wait_ge(VC.h, vc_qkdup)
                for wp in range(_nwp):
                    t0 = 32 + 128 * wp
                    if wp >= 1:
                        t_.wait_ge(SC.h, sc_exp[wp - 1])
                    for h in range(8):
                        for w in range(2):
                            b = 64 * w
                            ins = t_.matmul(psS[b:b + 64, 64 * h:64 * h + 64],
                                            QKT[:, 8 + h, t0 + b:t0 + b + 64],
                                            QKT[:, h, t0 + b:t0 + b + 64],
                                            start=True, stop=True)
                    TA.inc(ins)
                    if _so < 1:
                        [TA.inc(t_.nop()) for _ in range(3)]
                        continue
                    t_.wait_ge(SC.h, sc_exp[wp])
                    if wp >= 1:
                        t_.wait_ge(VC.h, vc_rec(wp - 1))
                    t_.matmul(psDa[0:1, :], onesb[0:64, 0:1], EA[0:64, :],
                              start=True, stop=True)
                    ins = t_.matmul(psDb[0:1, :], onesb[64:128, 0:1], EA[64:128, :],
                                    start=True, stop=True)
                    TA.inc(ins)
                    if _so < 2:
                        [TA.inc(t_.nop()) for _ in range(2)]
                        continue
                    t_.wait_ge(VC.h, vc_rec(wp))
                    t_.matmul(psB[0:64, :], onesf[0:1, 0:64], recD[0:1, 0:512],
                              start=True, stop=True)
                    ins = t_.matmul(psB[64:128, :], onesf[0:1, 0:64], recD[0:1, 512:1024],
                                    start=True, stop=True)
                    TA.inc(ins)
                    if _so < 4:
                        TA.inc(t_.nop())
                        continue
                    t_.wait_ge(VC.h, vc_mul(wp))
                    t_.wait_ge(VC.h, vc_v(2 * wp + 1))
                    if wp >= 2:
                        t_.wait_ge(SC.h, sc_ot[wp - 2])
                    for h in range(8):
                        for w in range(2):
                            ins = t_.matmul(
                                psO[:, wp % 2, 512 * w + 64 * h:512 * w + 64 * h + 64],
                                Vwa[64 * w:64 * w + 64, wp, 128 * h:128 * (h + 1)],
                                Asb[64 * w:64 * w + 64, 64 * h:64 * h + 64],
                                start=True, stop=True)
                    TA.inc(ins)

            @blk.scalar
            def _(s_):
                s = CE(s_, SC)
                for wp in range(_nwp):
                    t0 = 32 + 128 * wp
                    s.w(TA, tb0 + 4 * wp + 1)
                    if wp >= 1 and _so >= 3:
                        s.w(VC, vc_mul(wp - 1))
                    r = s.activation(EA[:, :], psS[:, :], AF.Exp, scale=ISR128)
                    assert r == sc_exp[wp]
                    if _so < 4:
                        [SC.inc(s.nop()) for _ in range(sc_ot[wp] - r)]
                        continue
                    s.w(TA, tb0 + 4 * wp + 4)
                    src = psO[:, wp % 2, :].rearrange("p (w h t) -> p w h t", w=2, t=64)
                    if t0 + 128 <= TOK:
                        dst = OT[:, :, t0:t0 + 128].rearrange("p h (w t) -> p w h t", w=2)
                        r = s.copy(dst, src)
                    else:
                        s.copy(OT[:, :, t0:t0 + 64], src[:, 0, :, :])
                        s.copy(OT[:, :, t0 + 64:TOK], src[:, 1, :, 0:32])
                        r = s.copy(OT[:, :, 0:32], src[:, 1, :, 32:64])
                    assert r == sc_ot[wp]

            @blk.vector
            def _(v_):
                v = CE(v_, VC)
                for wp in range(_nwp):
                    if _so < 1:
                        continue
                    v.w(TA, tb0 + 4 * wp + 2)
                    v.reciprocal(recD[0:1, 0:512], psDa[0:1, :])
                    r = v.reciprocal(recD[0:1, 512:1024], psDb[0:1, :])
                    assert r == vc_rec(wp)
                    if _so < 3:
                        [VC.inc(v.nop()) for _ in range(vc_mul(wp) - r)]
                        continue
                    v.w(TA, tb0 + 4 * wp + 3)
                    r = v.tensor_mul(Asb[:, :], EA[:, :], psB[:, :])
                    assert r == vc_mul(wp)

            fence(blk, [(SC, SC.n), (TA, TA.n), (VC, VC.n)])
        if _DBG_MODE.startswith("a:att"):
            _dbg_copy_out(p, blk, cn, state, xin, xout)
            return

        with ExitStack() as co:
            WoT = co.enter_context(nc.sbuf_tensor("a_WoT", [128, 8, DM], BF16))
            psW = co.enter_context(nc.psum_tensor("a_psWo", [128, 2, DM], BF16))
            psY = co.enter_context(nc.psum_tensor("a_psY", [128, 2, 512], F32))

            ta_w0, scw = jit_w(p, blk, dr["aw_outw"], 0, 8, cn, state, psW, WoT)

            ta0 = TA.n
            vc0 = VC.n
            gc0 = GC.n
            vc_stt = lambda t: vc0 + 2 * (t + 1)
            gc_add = lambda t: gc0 + t + 1
            xi_val, xo_val, gate = plan_stream(state)

            @blk.sync
            def _(y):
                for t in range(NT):
                    s = t % NS
                    for sem, val in gate[t]:
                        y.wait_ge(sem.h, val)
                    v = XI[s].inc(y.dma_start(out=xsl[:, s, :],
                                              in_=xin[128 * t:128 * (t + 1), :]), 16)
                    assert v == xi_val[t]

            @blk.tensor
            def _(t_):
                t_.wait_ge(SC.h, scw(7))
                i = 0
                for t in range(NT):
                    for vch in range(2):
                        if i >= 2:
                            t_.wait_ge(VC.h, vc0 + (i - 1))
                        for h in range(8):
                            ins = t_.matmul(psY[:, i % 2, :],
                                            OT[:, h, 128 * t:128 * (t + 1)],
                                            WoT[:, h, 512 * vch:512 * (vch + 1)],
                                            start=(h == 0), stop=(h == 7))
                        TA.inc(ins)
                        i += 1

            @blk.vector
            def _(v_):
                v = CE(v_, VC)
                i = 0
                for t in range(NT):
                    v.w(XI[t % NS], xi_val[t])
                    for vch in range(2):
                        v.w(TA, ta0 + i + 1)
                        r = v.scalar_tensor_tensor(
                            xsl[:, t % NS, 512 * vch:512 * (vch + 1)],
                            psY[:, i % 2, :], 2.0,
                            xsl[:, t % NS, 512 * vch:512 * (vch + 1)],
                            ALU.mult, ALU.add)
                        i += 1
                    assert r == vc_stt(t)

            @blk.gpsimd
            def _(g_):
                g = CE(g_, GC)
                for t in range(NT):
                    g.w(VC, vc_stt(t))
                    r = g.tensor_add(xsl[:, t % NS, :], xsl[:, t % NS, :],
                                     borep_a2[:, :])
                    assert r == gc_add(t)

            @blk.scalar
            def _(s_):
                for t in range(NT):
                    s_.wait_ge(GC.h, gc_add(t))
                    v = XO[t % NS].inc(s_.dma_start(out=xout[128 * t:128 * (t + 1), :],
                                                    in_=xsl[:, t % NS, :]), 16)
                    assert v == xo_val[t]

            xow = [(XO[t % NS], xo_val[t]) for t in range(NT - NS, NT)]
            state["phase_done"] = [(TA, TA.n), (VC, VC.n), (GC, GC.n), (SC, SC.n)] + xow
            fence(blk, [(TA, TA.n), (VC, VC.n), (GC, GC.n), (SC, SC.n)] + xow,
                  engines=("tensor", "scalar", "vector", "gpsimd"))


def _dbg_copy_out(p, blk, cn, state, xin, xout):
    XI, XO = state["XI"], state["XO"]
    xsl = cn["xsl"]
    xi_val, xo_val, gate = plan_stream(state)

    @blk.sync
    def _(y):
        for t in range(NT):
            s = t % NS
            for sem, val in gate[t]:
                y.wait_ge(sem.h, val)
            v = XI[s].inc(y.dma_start(out=xsl[:, s, :],
                                      in_=xin[128 * t:128 * (t + 1), :]), 16)
            assert v == xi_val[t]
            y.wait_ge(XI[s].h, xi_val[t])
            v = XO[s].inc(y.dma_start(out=xout[128 * t:128 * (t + 1), :],
                                      in_=xsl[:, s, :]), 16)
            assert v == xo_val[t]


def emit_ffn(p, blk, dr, cn, bcol_fq, bcol_fk, bcol_fo, bcol_f1,
             bcol_fv2, rbrep, fb2n, state, tag, xin, xmid, xout):
    nc = p.nc
    SC, VC, GC, TM, TA = p.sem("SC"), p.sem("VC"), p.sem("GC"), p.sem("TM"), p.sem("TA")
    XI, XO = state["XI"], state["XO"]
    xsl = cn["xsl"]
    onesf2, identf, sel4f = cn["onesf2"], cn["identf"], cn["sel4f"]
    WJ = [p.sem("WJA"), p.sem("WJB")]
    W1 = [p.sem("W1A"), p.sem("W1B")]
    W2 = [p.sem("W2A"), p.sem("W2B")]
    RD = p.sem("RDF")
    FQ = [p.sem(f"FQ{q}") for q in range(4)]
    pdone = state["phase_done"]

    with ExitStack() as c:
        sb = lambda n, sh, dt=F32: c.enter_context(nc.sbuf_tensor(n + "_" + tag, sh, dt))
        mlaT = sb("f_mlaT", [128, 8, 2048], BF16)
        comb = sb("f_comb", [128, NT, 8], F32)
        combT = sb("f_combT", [8, 2048], BF16)
        cA = c.enter_context(ExitStack())
        QuT = cA.enter_context(nc.sbuf_tensor("f_QuT_" + tag, [128, 4, 2048], BF16, side="right"))
        KuT = cA.enter_context(nc.sbuf_tensor("f_KuT_" + tag, [128, 4, 2048], BF16, side="right"))
        Vn = cA.enter_context(nc.sbuf_tensor("f_Vn_" + tag, [128, 16, 528], BF16, side="right"))
        cB = c.enter_context(ExitStack())
        lnT = cB.enter_context(nc.sbuf_tensor("f_lnT_" + tag, [128, 8, TOK], BF16))
        qwu = cB.enter_context(nc.sbuf_tensor("f_qwu_" + tag, [128, 8, 512], BF16))
        cqw = ExitStack()
        qw = cqw.enter_context(nc.sbuf_tensor("f_qw_" + tag, [128, 8, 512], BF16))

        wja0, wjb0 = WJ[0].n, WJ[1].n

        cqw.close()
        kw = cB.enter_context(nc.sbuf_tensor("f_kw_" + tag, [128, 8, 512], BF16))
        vw = cB.enter_context(nc.sbuf_tensor("f_vw_" + tag, [128, 8, 512], BF16))
        psP = cB.enter_context(nc.psum_tensor("f_psP_" + tag, [128, 2, 512], F32))

        ln_all, _, ln_done = emit_ln(p, blk, cn, state, lnT, "ln" + tag, xin, xmid,
                                     dup=0, add_hs=True)
        ln_slotfree = [list(fr) for fr in state["slot_free"]]
        fence(blk, ln_done)

        vc0 = VC.n
        vc_ones = vc0 + 1
        vc_rep = lambda hf, dc: vc0 + 1 + 8 * hf + dc + 1
        vc_vev = lambda kt: vc0 + 17 + kt + 1
        wja_h = [None, None]

        @blk.gpsimd
        def _(g):
            for sem, val in pdone:
                g.wait_ge(sem.h, val)
            for hf in range(2):
                if hf == 1:
                    g.wait_ge(VC.h, vc_rep(0, 7))
                for dc in range(8):
                    WJ[0].inc(g.dma_start(
                        out=qw[:, dc, :],
                        in_=dr["fw_q"][128 * dc:128 * (dc + 1), 512 * hf:512 * (hf + 1)]), 16)
                wja_h[hf] = WJ[0].n

        @blk.gpsimd
        def _(g):
            g.wait_ge(VC.h, vc_rep(1, 7))
            for dc in range(8):
                WJ[1].inc(g.dma_start(out=kw[:, dc, :],
                                      in_=dr["fw_k"][128 * dc:128 * (dc + 1), :]), 16)
                WJ[1].inc(g.dma_start(out=vw[:, dc, :],
                                      in_=dr["fw_v"][128 * dc:128 * (dc + 1), :]), 16)
        wjb_kv = WJ[1].n

        tb0 = TA.n
        ta_qu = lambda j: tb0 + j + 1
        ta_ku = lambda j: tb0 + 16 + j + 1
        ta_v = lambda kt: tb0 + 32 + kt + 1
        sc0 = SC.n
        sc_qu = lambda j: sc0 + j + 1
        sc_ku = lambda j: sc0 + 16 + j + 1

        @blk.vector
        def _(v_):
            v = CE(v_, VC)
            r = v.memset(Vn[:, :, :].rearrange("p k (h u) -> p k h u", u=33)[:, :, :, 32:33], 1.0)
            assert r == vc_ones
            for hf in range(2):
                v.w(WJ[0], wja_h[hf])
                for dc in range(8):
                    r = v.tensor_copy(
                        qwu[:, dc, 256 * hf:256 * (hf + 1)].rearrange("p (h s) -> p h s", s=32),
                        qw[:, dc, :].rearrange("p (h s) -> p h s", s=64)[:, :, 0:32])
                    assert r == vc_rep(hf, dc)
            for kt in range(16):
                v.w(TA, ta_v(kt))
                r = v.tensor_copy(Vn[:, kt, :].rearrange("p (h u) -> p h u", u=33)[:, :, 0:32],
                                  psP[:, kt % 2, :].rearrange("p (h u) -> p h u", u=32))
                assert r == vc_vev(kt)

        @blk.tensor
        def _(t_):
            t_.wait_ge(VC.h, vc_rep(1, 7))
            t_.wait_ge(SC.h, ln_all[1])
            for cc in range(4):
                for tc in range(4):
                    j = cc * 4 + tc
                    if j >= 2:
                        t_.wait_ge(SC.h, sc_qu(j - 2))
                    for dc in range(8):
                        ins = t_.matmul(psP[:, j % 2, :],
                                        qwu[:, dc, 128 * cc:128 * (cc + 1)],
                                        lnT[:, dc, 512 * tc:512 * (tc + 1)],
                                        start=(dc == 0), stop=(dc == 7))
                    TA.inc(ins)
            t_.wait_ge(WJ[1].h, wjb_kv)
            for cc in range(4):
                for tc in range(4):
                    j = cc * 4 + tc
                    t_.wait_ge(SC.h, sc_ku(j - 2) if j >= 2 else sc_qu(15))
                    for dc in range(8):
                        ins = t_.matmul(psP[:, j % 2, :],
                                        kw[:, dc, 128 * cc:128 * (cc + 1)],
                                        lnT[:, dc, 512 * tc:512 * (tc + 1)],
                                        start=(dc == 0), stop=(dc == 7))
                    TA.inc(ins)
            t_.wait_ge(VC.h, vc_ones)
            for kt in range(16):
                if kt >= 2:
                    t_.wait_ge(VC.h, vc_vev(kt - 2))
                else:
                    t_.wait_ge(SC.h, sc_ku(14 + kt))
                for dc in range(8):
                    ins = t_.matmul(psP[:, kt % 2, 0:512],
                                    lnT[:, dc, 128 * kt:128 * (kt + 1)],
                                    vw[:, dc, :],
                                    start=(dc == 0), stop=(dc == 7))
                TA.inc(ins)

        @blk.scalar
        def _(s_):
            s = CE(s_, SC)
            for cc in range(4):
                for tc in range(4):
                    j = cc * 4 + tc
                    s.w(TA, ta_qu(j))
                    r = s.activation(QuT[:, cc, 512 * tc:512 * (tc + 1)],
                                     psP[:, j % 2, :], AF.Identity,
                                     bias=bcol_fq[:, cc:cc + 1])
                    assert r == sc_qu(j)
            for cc in range(4):
                for tc in range(4):
                    j = cc * 4 + tc
                    s.w(TA, ta_ku(j))
                    r = s.activation(KuT[:, cc, 512 * tc:512 * (tc + 1)],
                                     psP[:, j % 2, :], AF.Identity,
                                     bias=bcol_fk[:, cc:cc + 1])
                    assert r == sc_ku(j)

        fence(blk, [(TA, TA.n), (SC, SC.n), (VC, VC.n)])
        cB.close()

        cAO = c.enter_context(ExitStack())
        aoT = cAO.enter_context(nc.sbuf_tensor("f_aoT_" + tag, [128, 4, 2048], BF16))
        cE = cAO.enter_context(ExitStack())
        E2 = cE.enter_context(nc.sbuf_tensor("f_E2_" + tag, [128, 2, 4, 2, 1024], BF16))
        rtmp = cE.enter_context(nc.sbuf_tensor("f_rtmp_" + tag, [128, 2, 512], F32))
        recD = cE.enter_context(nc.sbuf_tensor("f_recD_" + tag, [128, 512], F32))
        o32d = cE.enter_context(nc.sbuf_tensor("f_o32_" + tag, [128, 512], BF16))
        psS = cE.enter_context(nc.psum_tensor("f_psS_" + tag, [128, 6, 512], F32))
        psA = cE.enter_context(nc.psum_tensor("f_psA_" + tag, [128, 2, 512], F32))
        psN = psS[:, 4, :]

        ta0 = TA.n
        sc0 = SC.n
        vc0 = VC.n
        rd0 = RD.n
        ta_sc = {}
        ta_av = {}
        ta_bc = {}
        tn = ta0
        for j in range(16):
            if j > 0:
                tn += 1
                ta_bc[j - 1] = tn
            for s4 in range(4):
                for ktl in range(4):
                    for r in range(4):
                        tn += 1
                        ta_sc[(j, s4, ktl, r)] = tn
                for ktl in range(4):
                    for pp in range(2):
                        for q2 in range(2):
                            tn += 1
                            ta_av[(j, s4, ktl, pp, q2)] = tn
        tn += 1
        ta_bc[15] = tn
        ta_av_last = lambda gs: ta_av[(gs // 4, gs % 4, 3, 1, 1)]
        sc_exp = lambda gk, pp: sc0 + 2 * gk + pp + 1
        vc_it = lambda j, k: vc0 + 10 * j + k
        rd_tot = lambda j: rd0 + 64 * (j + 1)
        slot = lambda gk, pp: (2 * gk + pp) % 3
        s2_first = {}
        s2_last = {}
        for j in range(16):
            for kt in range(16):
                for pp2 in range(2):
                    if slot(16 * j + kt, pp2) == 2:
                        if j not in s2_first:
                            s2_first[j] = 16 * j + kt
                        s2_last[j] = (16 * j + kt, pp2)

        @blk.tensor
        def _(t_):
            for j in range(17):
                if j > 0:
                    t_.wait_ge(RD.h, rd_tot(j - 1))
                    t_.wait_ge(SC.h, sc_exp(*s2_last[j - 1]))
                    if j >= 2:
                        t_.wait_ge(VC.h, vc_it(j - 2, 9))
                    ins = t_.matmul(psN, sel4f[0:4, :], recD[0:4, :],
                                    start=True, stop=True)
                    assert TA.inc(ins) == ta_bc[j - 1]
                if j == 16:
                    break
                cch, qc = j // 4, j % 4
                for s4 in range(4):
                    gs = 4 * j + s4
                    for ktl in range(4):
                        kt = 4 * s4 + ktl
                        gk = 16 * j + kt
                        for pp in range(2):
                            prev = 2 * gk + pp - 3
                            if prev >= 0:
                                t_.wait_ge(SC.h, sc_exp(prev // 2, prev % 2))
                        if j > 0 and s2_first[j] == gk:
                            t_.wait_ge(VC.h, vc_it(j - 1, 9))
                        for r in range(4):
                            sl = slot(gk, r // 2)
                            ins = t_.matmul(
                                psS[:, 2 * sl + (r % 2), :],
                                KuT[32 * r:32 * r + 32, cch, 128 * kt:128 * (kt + 1)],
                                QuT[32 * r:32 * r + 32, cch, 512 * qc:512 * (qc + 1)],
                                start=True, stop=True, tile_position=(32 * r, 0))
                            assert TA.inc(ins) == ta_sc[(j, s4, ktl, r)]
                    for ktl in range(4):
                        kt = 4 * s4 + ktl
                        t_.wait_ge(SC.h, sc_exp(16 * j + kt, 1))
                        if s4 == 0 and ktl == 0 and j > 0:
                            t_.wait_ge(VC.h, vc_it(j - 1, 8))
                        for pp in range(2):
                            for q2 in range(2):
                                h = 4 * cch + 2 * pp + q2
                                ins = t_.matmul(
                                    psA[64 * q2:64 * q2 + 33, pp, :],
                                    Vn[:, kt, 33 * h:33 * h + 33],
                                    E2[:, gs % 2, ktl, pp, 512 * q2:512 * (q2 + 1)],
                                    start=(s4 == 0 and ktl == 0),
                                    stop=(s4 == 3 and ktl == 3),
                                    tile_position=(0, 64 * q2),
                                    skip_group_check=True)
                                assert TA.inc(ins) == ta_av[(j, s4, ktl, pp, q2)]

        @blk.scalar
        def _(s_):
            s = CE(s_, SC)
            for gs in range(64):
                j, s4 = gs // 4, gs % 4
                for ktl in range(4):
                    gk = 16 * j + 4 * s4 + ktl
                    for pp in range(2):
                        s.w(TA, ta_sc[(j, s4, ktl, 2 * pp + 1)])
                        if gs >= 2 and ktl == 0 and pp == 0:
                            s.w(TA, ta_av_last(gs - 2))
                        sl = slot(gk, pp)
                        r = s.activation(E2[:, gs % 2, ktl, pp, :],
                                         psS[:, 2 * sl:2 * sl + 2, :],
                                         AF.Exp, scale=ISR32)
                        assert r == sc_exp(gk, pp)

        @blk.vector
        def _(v_):
            v = CE(v_, VC)
            for j in range(16):
                cch, qc = j // 4, j % 4
                v.w(TA, ta_av_last(4 * j + 3))
                if j > 0:
                    v.w(RD, rd_tot(j - 1))
                for k in range(4):
                    pp, rr = k // 2, k % 2
                    r = v.reciprocal(rtmp[32 + 64 * rr:33 + 64 * rr, pp, :],
                                     psA[32 + 64 * rr:33 + 64 * rr, pp, :])
                    assert r == vc_it(j, k + 1)
                for k in range(4):
                    r = v.tensor_copy(o32d[32 * k:32 * (k + 1), :],
                                      psA[64 * (k % 2):64 * (k % 2) + 32, k // 2, :])
                    assert r == vc_it(j, k + 5)
                v.w(TA, ta_bc[j])
                r = v.tensor_mul(aoT[:, cch, 512 * qc:512 * (qc + 1)],
                                 o32d[:, :], psN)
                assert r == vc_it(j, 9)
                r = v.tensor_scalar_add(aoT[:, cch, 512 * qc:512 * (qc + 1)],
                                        aoT[:, cch, 512 * qc:512 * (qc + 1)],
                                        bcol_fv2[:, cch:cch + 1])
                assert r == vc_it(j, 10)

        @blk.gpsimd
        def _(g_):
            for j in range(16):
                if j > 0:
                    g_.wait_ge(TA.h, ta_bc[j - 1])
                for k in range(4):
                    pp, rr = k // 2, k % 2
                    g_.wait_ge(VC.h, vc_it(j, k + 1))
                    RD.inc(g_.dma_start(
                        out=recD[k:k + 1, :],
                        in_=rtmp[32 + 64 * rr:33 + 64 * rr, pp, :]), 16)
                assert RD.n == rd_tot(j)

        fence(blk, [(TA, TA.n), (SC, SC.n), (VC, VC.n)])
        cE.close()
        cA.close()

        c4 = c.enter_context(ExitStack())
        ow = c4.enter_context(nc.sbuf_tensor("f_ow_" + tag, [128, 4, 1024], BF16))
        rw = c4.enter_context(nc.sbuf_tensor("f_rw_" + tag, [128, 8, 8], BF16))
        lgT = c4.enter_context(nc.sbuf_tensor("f_lgT_" + tag, [8, 2048], F32))
        lg2 = c4.enter_context(nc.sbuf_tensor("f_lg2_" + tag, [128, 2, 8], F32))
        E8 = c4.enter_context(nc.sbuf_tensor("f_E8_" + tag, [128, 2, 8], F32))
        rsc = c4.enter_context(nc.sbuf_tensor("f_rsc_" + tag, [128, 8], F32))
        psM = c4.enter_context(nc.psum_tensor("f_psM_" + tag, [128, 2, 512], F32))
        psR = c4.enter_context(nc.psum_tensor("f_psR_" + tag, [8, 2, 512], F32))

        wja0 = WJ[0].n

        @blk.gpsimd
        def _(g):
            g.wait_ge(TA.h, TA.n)
            g.wait_ge(VC.h, VC.n)
            for cc in range(4):
                WJ[0].inc(g.dma_start(out=ow[:, cc, :],
                                      in_=dr["fw_o"][128 * cc:128 * (cc + 1), :]), 16)
            WJ[0].inc(g.dma_start(
                out=rw[:, :, :],
                in_=dr["fw_r"].rearrange("(c p) e -> p c e", p=128)), 16)
        wja_o = WJ[0].n

        ta0 = TA.n
        sc0 = SC.n
        vc0 = VC.n
        ta_op = lambda j: ta0 + j + 1
        sc_op = lambda j: sc0 + j + 1
        ta_rt = lambda tc: ta0 + 32 + tc + 1
        sc_rt = lambda tc: sc0 + 32 + tc + 1
        ta_tr = lambda tt: ta0 + 36 + tt + 1
        ta_ct = lambda tt: ta0 + 52 + tt + 1
        sc_e8 = lambda tt: sc0 + 36 + tt + 1
        sc_ct = lambda tt: sc0 + 52 + tt + 1

        @blk.tensor
        def _(t_):
            t_.wait_ge(WJ[0].h, wja_o)
            for dc in range(8):
                for tc in range(4):
                    j = dc * 4 + tc
                    if j >= 2:
                        t_.wait_ge(SC.h, sc_op(j - 2))
                    for cc in range(4):
                        ins = t_.matmul(psM[:, j % 2, :],
                                        ow[:, cc, 128 * dc:128 * (dc + 1)],
                                        aoT[:, cc, 512 * tc:512 * (tc + 1)],
                                        start=(cc == 0), stop=(cc == 3))
                    TA.inc(ins)
            for tc in range(4):
                t_.wait_ge(SC.h, sc_op(31))
                if tc >= 2:
                    t_.wait_ge(SC.h, sc_rt(tc - 2))
                for dc in range(8):
                    ins = t_.matmul(psR[0:8, tc % 2, :], rw[:, dc, :],
                                    mlaT[:, dc, 512 * tc:512 * (tc + 1)],
                                    start=(dc == 0), stop=(dc == 7))
                TA.inc(ins)
            for tt in range(NT):
                t_.wait_ge(SC.h, sc_rt(3))
                if tt >= 2:
                    t_.wait_ge(VC.h, vc0 + 10 * (tt - 2) + 1)
                TA.inc(t_.transpose(psM[:, tt % 2, 0:8], lgT[0:8, 128 * tt:128 * (tt + 1)],
                                    identf[0:8, 0:8]))
            for tt in range(NT):
                t_.wait_ge(VC.h, vc0 + 10 * tt + 10)
                if tt >= 2:
                    t_.wait_ge(SC.h, sc_ct(tt - 2))
                TA.inc(t_.transpose(psR[0:8, tt % 2, 0:128], comb[:, tt, :],
                                    identf[:, :]))

        @blk.scalar
        def _(s_):
            s = CE(s_, SC)
            for dc in range(8):
                for tc in range(4):
                    j = dc * 4 + tc
                    s.w(TA, ta_op(j))
                    r = s.activation(mlaT[:, dc, 512 * tc:512 * (tc + 1)],
                                     psM[:, j % 2, :], AF.Identity,
                                     bias=bcol_fo[:, dc:dc + 1])
                    assert r == sc_op(j)
            for tc in range(4):
                s.w(TA, ta_rt(tc))
                r = s.copy(lgT[0:8, 512 * tc:512 * (tc + 1)], psR[0:8, tc % 2, :])
                assert r == sc_rt(tc)
            for tt in range(NT):
                s.w(VC, vc0 + 10 * tt + 1)
                r = s.activation(E8[:, tt % 2, :], lg2[:, tt % 2, :], AF.Exp,
                                 accum_out=rsc[:, (tt % 2):(tt % 2) + 1])
                assert r == sc_e8(tt)
            for tt in range(NT):
                s.w(TA, ta_ct(tt))
                r = s.copy(combT[0:8, 128 * tt:128 * (tt + 1)], psR[0:8, tt % 2, 0:128])
                assert r == sc_ct(tt)

        @blk.vector
        def _(v_):
            v = CE(v_, VC)
            for tt in range(NT):
                v.w(TA, ta_tr(tt))
                if tt >= 2:
                    v.w(SC, sc_e8(tt - 2))
                r = v.tensor_add(lg2[:, tt % 2, :], psM[:, tt % 2, 0:8], rbrep[:, :])
                assert r == vc0 + 10 * tt + 1
                v.w(SC, sc_e8(tt))
                v.reduce_max(rsc[:, 2:3], E8[:, tt % 2, :], axis=AX.X)
                v.scalar_tensor_tensor(lg2[:, tt % 2, :], E8[:, tt % 2, :], rsc[:, 2:3],
                                       E8[:, tt % 2, :], ALU.is_lt, ALU.mult)
                v.reduce_max(rsc[:, 3:4], lg2[:, tt % 2, :], axis=AX.X)
                v.tensor_scalar(rsc[:, 4:5], rsc[:, (tt % 2):(tt % 2) + 1], 1e-8,
                                rsc[:, 2:3], ALU.mult, ALU.add)
                v.tensor_add(rsc[:, 4:5], rsc[:, 4:5], rsc[:, 3:4])
                v.reciprocal(rsc[:, 5:6], rsc[:, 4:5])
                v.tensor_scalar(rsc[:, 6:7], rsc[:, 5:6], 0.3, None, ALU.mult)
                v.scalar_tensor_tensor(lg2[:, tt % 2, :], E8[:, tt % 2, :], rsc[:, 3:4],
                                       E8[:, tt % 2, :], ALU.is_ge, ALU.mult)
                r = v.tensor_scalar(comb[:, tt, :], lg2[:, tt % 2, :], rsc[:, 6:7],
                                    None, ALU.mult)
                assert r == vc0 + 10 * tt + 10

        fence(blk, [(TA, TA.n), (SC, SC.n), (VC, VC.n)])
        c4.close()
        cAO.close()

        c5 = c.enter_context(ExitStack())
        xful = c5.enter_context(nc.sbuf_tensor("f_xful_" + tag, [128, NT, DM], F32))
        w1sb = c5.enter_context(nc.sbuf_tensor("f_w1_" + tag, [128, 8, 2048], BF16))
        w2sb = c5.enter_context(nc.sbuf_tensor("f_w2_" + tag, [128, 16, 1024], BF16))
        g1T = c5.enter_context(nc.sbuf_tensor("f_g1_" + tag, [128, 16, 512], BF16))
        psH = c5.enter_context(nc.psum_tensor("f_psH_" + tag, [128, 2, 512], F32))
        psG = c5.enter_context(nc.psum_tensor("f_psG_" + tag, [128, 2, 512], F32))

        ta0 = TA.n
        sc0 = SC.n
        vc0 = VC.n
        fq0 = [q.n for q in FQ]
        f3_done = [(TA, TA.n), (SC, SC.n), (VC, VC.n)]

        ta_b2 = lambda j: ta0 + j + 1
        vc_b2 = lambda j: vc0 + j + 1
        def ta_h1(e, tc, fb):
            return ta0 + 32 + e * (4 * (16 + 8)) + tc * (16 + 8) + fb + 1
        def ta_h2(e, tc, j):
            return ta0 + 32 + e * 96 + tc * 24 + 16 + j + 1
        def sc_g1(e, tc, fb):
            return sc0 + e * 64 + tc * 16 + fb + 1
        def vc_h2(e, tc, j):
            return vc0 + 32 + e * 32 + tc * 8 + j + 1
        w1v = [0, 0]
        w2v = [0, 0]
        w1tot = {}
        w2tot = {}

        @blk.sync
        def _(y):
            for sem, val in f3_done:
                y.wait_ge(sem.h, val)
            for fr in ln_slotfree:
                for sem, val in fr:
                    y.wait_ge(sem.h, val)
            for q in range(4):
                src = xmid[512 * q:512 * (q + 1), :].rearrange("(t p) d -> p t d", p=128)
                FQ[q].inc(y.dma_start(out=xful[:, 4 * q:4 * (q + 1), :], in_=src), 16)

        @blk.gpsimd
        def _(g):
            for e in range(8):
                if e == 0:
                    for sem, val in f3_done:
                        g.wait_ge(sem.h, val)
                else:
                    g.wait_ge(TA.h, ta_h1(e - 1, 3, 15))
                for dc in range(8):
                    W1[e % 2].inc(g.dma_start(
                        out=w1sb[:, dc, :],
                        in_=dr["fw_1"][e, 128 * dc:128 * (dc + 1), :]), 16)
                w1tot[e] = W1[e % 2].n
                if e > 0:
                    g.wait_ge(TA.h, ta_h2(e - 1, 3, 7))
                for fb in range(16):
                    W2[e % 2].inc(g.dma_start(
                        out=w2sb[:, fb, :],
                        in_=dr["fw_2"][e, 128 * fb:128 * (fb + 1), :]), 16)
                w2tot[e] = W2[e % 2].n

        @blk.tensor
        def _(t_):
            t_.wait_ge(SC.h, sc0)
            for tt in range(NT):
                for dh in range(2):
                    j = tt * 2 + dh
                    if j >= 2:
                        t_.wait_ge(VC.h, vc_b2(j - 2))
                    TA.inc(t_.matmul(psG[:, j % 2, :], combT[0:8, 128 * tt:128 * (tt + 1)],
                                     fb2n[0:8, 512 * dh:512 * (dh + 1)],
                                     start=True, stop=True))
            for e in range(8):
                t_.wait_ge(W1[e % 2].h, w1tot[e])
                for tc in range(4):
                    for fb in range(16):
                        prev = (e, tc, fb - 2)
                        if fb < 2:
                            prev = (e, tc - 1, 14 + fb) if tc > 0 else \
                                   ((e - 1, 3, 14 + fb) if e > 0 else None)
                        if prev is not None:
                            t_.wait_ge(SC.h, sc_g1(*prev))
                        for dc in range(8):
                            ins = t_.matmul(psH[:, fb % 2, :],
                                            w1sb[:, dc, 128 * fb:128 * (fb + 1)],
                                            mlaT[:, dc, 512 * tc:512 * (tc + 1)],
                                            start=(dc == 0), stop=(dc == 7))
                        TA.inc(ins)
                    if e == 0 and tc == 0:
                        t_.wait_ge(W2[0].h, w2tot[0])
                    elif tc == 0:
                        t_.wait_ge(W2[e % 2].h, w2tot[e])
                    t_.wait_ge(SC.h, sc_g1(e, tc, 15))
                    for tt in range(4):
                        for dh in range(2):
                            j = tt * 2 + dh
                            prev = (e, tc, j - 2)
                            if j < 2:
                                prev = (e, tc - 1, 6 + j) if tc > 0 else \
                                       ((e - 1, 3, 6 + j) if e > 0 else None)
                            if prev is not None:
                                t_.wait_ge(VC.h, vc_h2(*prev))
                            elif j < 2:
                                t_.wait_ge(VC.h, vc_b2(30 + j))
                            for fb in range(16):
                                ins = t_.matmul(psG[:, j % 2, :],
                                                g1T[:, fb, 128 * tt:128 * (tt + 1)],
                                                w2sb[:, fb, 512 * dh:512 * (dh + 1)],
                                                start=(fb == 0), stop=(fb == 15))
                            TA.inc(ins)

        @blk.scalar
        def _(s_):
            s = CE(s_, SC)
            for e in range(8):
                for tc in range(4):
                    for fb in range(16):
                        s.w(TA, ta_h1(e, tc, fb))
                        r = s.activation(g1T[:, fb, :], psH[:, fb % 2, :], AF.Gelu,
                                         bias=bcol_f1[:, e, fb:fb + 1])
                        assert r == sc_g1(e, tc, fb)

        @blk.vector
        def _(v_):
            v = CE(v_, VC)
            for tt in range(NT):
                for dh in range(2):
                    j = tt * 2 + dh
                    v.w(TA, ta_b2(j))
                    v.w(FQ[tt // 4], fq0[tt // 4] + 16)
                    r = v.scalar_tensor_tensor(
                        xful[:, tt, 512 * dh:512 * (dh + 1)],
                        psG[:, j % 2, :], 1.0,
                        xful[:, tt, 512 * dh:512 * (dh + 1)],
                        ALU.mult, ALU.add)
                    assert r == vc_b2(j)
            for e in range(8):
                for tc in range(4):
                    for tt in range(4):
                        for dh in range(2):
                            j = tt * 2 + dh
                            v.w(TA, ta_h2(e, tc, j))
                            t_g = 4 * tc + tt
                            r = v.scalar_tensor_tensor(
                                xful[:, t_g, 512 * dh:512 * (dh + 1)],
                                psG[:, j % 2, :], comb[:, t_g, e:e + 1],
                                xful[:, t_g, 512 * dh:512 * (dh + 1)],
                                ALU.mult, ALU.add)
                            assert r == vc_h2(e, tc, j)

        xo_val = [0] * NT
        sfree = [None] * NS

        @blk.sync
        def _(y):
            for t in range(NT):
                y.wait_ge(VC.h, vc_h2(7, t // 4, (t % 4) * 2 + 1))
                xo_val[t] = XO[t % NS].inc(
                    y.dma_start(out=xout[128 * t:128 * (t + 1), :], in_=xful[:, t, :]), 16)
        for t in range(NT):
            sfree[t % NS] = [(XO[t % NS], xo_val[t])]
        state["slot_free"] = sfree
        xow = [(XO[t % NS], xo_val[t]) for t in range(NT - NS, NT)]
        state["phase_done"] = [(TA, TA.n), (SC, SC.n), (VC, VC.n)] + xow
        fence(blk, [(TA, TA.n), (SC, SC.n), (VC, VC.n)] + xow,
              engines=("tensor", "scalar", "vector", "gpsimd"))


def emit_attn_b(p, blk, dr, cn, bcol_bq, bcol_bk, bcol_bv2, borep_b,
                state, xin, xmid, xout):
    nc = p.nc
    SC, VC, GC, TM, TA = p.sem("SC"), p.sem("VC"), p.sem("GC"), p.sem("TM"), p.sem("TA")
    XI, XO = state["XI"], state["XO"]
    xsl = cn["xsl"]
    onesf2, identb = cn["onesf2"], cn["identb"]
    CB = p.sem("CB")
    WK = p.sem("WK")
    WJ = [p.sem("WJA"), p.sem("WJB")]
    WR = [p.sem("WRA"), p.sem("WRB")]
    WM = [p.sem("WMA"), p.sem("WMB"), p.sem("WMC"), p.sem("WMD")]
    pdone = state["phase_done"]

    with ExitStack() as c:
        sb = lambda n, sh, dt=F32: c.enter_context(nc.sbuf_tensor(n, sh, dt))
        cn["wstage"] = sb("b_wstage", [128, 2, 2048], BF16)
        wstage = cn["wstage"]
        cosb = sb("b_cos", [128, TOK], F32)
        sinb = sb("b_sinm", [128, TOK], F32)
        maskb = sb("b_mask", [128, 4, 512], BF16)
        aoTb = sb("b_aoT", [128, 8, TOK], BF16)
        lnT = sb("b_lnT", [128, 8, TOK], BF16)

        @blk.sync
        def _(y):
            CB.inc(y.dma_start(out=cosb[:, :], in_=dr["c_cos"]), 16)
            CB.inc(y.dma_start(out=sinb[:, :], in_=dr["c_sinm"]), 16)
        cb_tab = CB.n

        @blk.gpsimd
        def _(g):
            for sem, val in pdone:
                g.wait_ge(sem.h, val)
            for o in range(4):
                WK.inc(g.dma_start(out=maskb[:, o, :], in_=dr["c_mask"][o, :, :]), 16)
        wk_msk = WK.n

        ln_all, _, ln_done = emit_ln(p, blk, cn, state, lnT, "b", xin, None,
                                     dup=0, add_hs=False)
        fence(blk, ln_done)

        cw = c.enter_context(ExitStack())
        wpT = cw.enter_context(nc.sbuf_tensor("b_wpT", [128, 3, 8, 128], BF16))
        Qf = cw.enter_context(nc.sbuf_tensor("b_Qf", [128, TOK], F32))
        Kf = cw.enter_context(nc.sbuf_tensor("b_Kf", [128, TOK], F32))
        rot = cw.enter_context(nc.sbuf_tensor("b_rot", [128, TOK], F32))
        rtm = cw.enter_context(nc.sbuf_tensor("b_rtm", [128, TOK], F32))
        Qr = cw.enter_context(nc.sbuf_tensor("b_Qr", [128, TOK], BF16))
        Kr = cw.enter_context(nc.sbuf_tensor("b_Kr", [128, TOK], BF16))
        Vb = cw.enter_context(nc.sbuf_tensor("b_Vb", [128, 16, 130], BF16))
        Eb = cw.enter_context(nc.sbuf_tensor("b_Eb", [128, 2, 16, 512], BF16))
        recRb = cw.enter_context(nc.sbuf_tensor("b_recR", [128, 512], F32))
        o64 = cw.enter_context(nc.sbuf_tensor("b_o64", [128, 2, 512], BF16))
        aostg = cw.enter_context(nc.sbuf_tensor("b_aostg", [128, 4, 512], BF16))
        psW = cw.enter_context(nc.psum_tensor("b_psW", [128, 2, DM], BF16))
        psP = cw.enter_context(nc.psum_tensor("b_psP", [128, 2, 512], F32))
        psS = cw.enter_context(nc.psum_tensor("b_psS", [128, 2, 512], F32))
        psA = cw.enter_context(nc.psum_tensor("b_psA", [128, 512], F32))
        psN = cw.enter_context(nc.psum_tensor("b_psN", [128, 512], F32))

        NKT = lambda qc: 4 * qc + 4
        ta0, sc0, vc0, gc0 = TA.n, SC.n, VC.n, GC.n
        wj0 = [WJ[0].n, WJ[1].n]
        wr0 = [WR[0].n, WR[1].n]
        wm0 = [WM[0].n, WM[1].n]

        TPP = 3 + 12 + 2 * sum(NKT(qc) + 2 for qc in range(4))
        def ta_tr(p_, w3):
            return ta0 + TPP * p_ + w3 + 1
        def ta_qk(p_, qk, tc):
            return ta0 + TPP * p_ + 3 + 4 * qk + tc + 1
        def ta_v(p_, sl):
            return ta0 + TPP * p_ + 11 + sl + 1
        def ta_att0(p_, w, qc):
            base = ta0 + TPP * p_ + 15
            for ww in range(2):
                for q2 in range(4):
                    if (ww, q2) == (w, qc):
                        return base
                    base += NKT(q2) + 2
            return base
        def ta_sc(p_, w, qc, kt):
            return ta_att0(p_, w, qc) + kt + 1
        def ta_av(p_, w, qc):
            return ta_att0(p_, w, qc) + NKT(qc) + 1
        def ta_bc(p_, w, qc):
            return ta_att0(p_, w, qc) + NKT(qc) + 2
        SPP = 3 + 8 + 2 * sum(NKT(qc) + 1 for qc in range(4))
        def sc_w(p_, w3):
            return sc0 + SPP * p_ + w3 + 1
        def sc_qk(p_, qk, tc):
            return sc0 + SPP * p_ + 3 + 4 * qk + tc + 1
        def sc_att0(p_, w, qc):
            base = sc0 + SPP * p_ + 11
            for ww in range(2):
                for q2 in range(4):
                    if (ww, q2) == (w, qc):
                        return base
                    base += NKT(q2) + 1
            return base
        def sc_exp(p_, w, qc, kt):
            return sc_att0(p_, w, qc) + kt + 1
        def sc_o64(p_, w, qc):
            return sc_att0(p_, w, qc) + NKT(qc) + 1
        VOFF = 2
        VPP = 16 + 4 + 2 * sum(4 + 3 for qc in range(4))
        def vc_vev(p_, sl, tt):
            return vc0 + VOFF + VPP * p_ + 4 * sl + tt + 1
        def vc_rope(p_, qk, ph):
            return vc0 + VOFF + VPP * p_ + 16 + 2 * qk + ph + 1
        def vc_att0(p_, w, qc):
            return vc0 + VOFF + VPP * p_ + 20 + 7 * (4 * w + qc)
        def vc_msk(p_, w, qc, j):
            return vc_att0(p_, w, qc) + j + 1
        def vc_rec(p_, w, qc):
            return vc_att0(p_, w, qc) + 5
        def vc_mul(p_, w, qc):
            return vc_att0(p_, w, qc) + 7
        GPP = 2
        def gc_rp(p_, qk):
            return gc0 + GPP * p_ + qk + 1
        wr_val = {}
        wm_val = {}
        wrn = [WR[0].n, WR[1].n]
        wmn = [sem.n for sem in WM]
        wm_prev = {}
        for p_ in range(8):
            for qk in range(2):
                wrn[qk] += 64
                wr_val[(p_, qk)] = wrn[qk]
            for qc in range(4):
                i = p_ * 4 + qc
                wm_prev[i] = wm_val.get((p_ - 1) * 4 + qc) if p_ > 0 else None
                wmn[qc] += 16
                wm_val[i] = wmn[qc]

        @blk.gpsimd
        def _(g_):
            g = CE(g_, GC)
            for p_ in range(8):
                for mp in ([p_ - 1] if p_ > 0 else []):
                    for qc in range(4):
                        i = mp * 4 + qc
                        g_.wait_ge(VC.h, vc_mul(mp, 1, qc))
                        vv = WM[qc].inc(
                            g_.dma_start(out=aoTb[64:128, mp, 512 * qc:512 * (qc + 1)],
                                         in_=aostg[0:64, qc, :]), 16)
                        assert vv == wm_val[i]
                if p_ == 0:
                    g_.wait_ge(VC.h, ln_done[0][1])
                else:
                    g_.wait_ge(TA.h, ta_tr(p_ - 1, 2))
                WJ[0].inc(g_.dma_start(out=wstage[:, 0, 0:1024],
                                       in_=dr["bw_q"][128 * p_:128 * (p_ + 1), :]), 16)
                WJ[0].inc(g_.dma_start(out=wstage[:, 0, 1024:2048],
                                       in_=dr["bw_k"][128 * p_:128 * (p_ + 1), :]), 16)
                WJ[1].inc(g_.dma_start(out=wstage[:, 1, 0:1024],
                                       in_=dr["bw_v"][128 * p_:128 * (p_ + 1), :]), 16)
                for qk in range(2):
                    src = Qf if qk == 0 else Kf
                    rr = rot if qk == 0 else rtm
                    g_.wait_ge(SC.h, sc_qk(p_, qk, 3))
                    if p_ > 0:
                        g_.wait_ge(VC.h, vc_rope(p_ - 1, qk, 1))
                    for piece in range(4):
                        sb_, db_ = [32, 0, 96, 64][piece], [0, 32, 64, 96][piece]
                        WR[qk % 2].inc(g_.dma_start(out=rr[db_:db_ + 32, :],
                                                    in_=src[sb_:sb_ + 32, :]), 16)
                    g.w(WR[qk % 2], wr_val[(p_, qk)])
                    g.w(CB, cb_tab)
                    r = g.tensor_mul(src[:, :], src[:, :], cosb[:, :])
                    assert r == gc_rp(p_, qk)
            for qc in range(4):
                i = 7 * 4 + qc
                g_.wait_ge(VC.h, vc_mul(7, 1, qc))
                vv = WM[qc].inc(
                    g_.dma_start(out=aoTb[64:128, 7, 512 * qc:512 * (qc + 1)],
                                 in_=aostg[0:64, qc, :]), 16)
                assert vv == wm_val[i]

        @blk.tensor
        def _(t_):
            for p_ in range(8):
                for w3 in range(3):
                    sl = 0 if w3 < 2 else 1
                    off = 1024 * (w3 % 2) if w3 < 2 else 0
                    t_.wait_ge(WJ[sl].h, wj0[sl] + (2 if sl == 0 else 1) * 16 * (p_ + 1))
                    if p_ >= 1 or w3 >= 2:
                        pk, pw = (p_, w3 - 2) if w3 >= 2 else (p_ - 1, w3 + 1)
                        t_.wait_ge(SC.h, sc_w(pk, pw))
                    for cc in range(8):
                        ins = t_.transpose(psW[:, w3 % 2, 128 * cc:128 * (cc + 1)],
                                           wstage[:, sl, off + 128 * cc:off + 128 * (cc + 1)],
                                           identb[:, :])
                    TA.inc(ins)
                for qk in range(2):
                    t_.wait_ge(SC.h, sc_w(p_, qk))
                    for tc in range(4):
                        j = 4 * qk + tc
                        if j >= 2:
                            t_.wait_ge(SC.h, sc_qk(p_, (j - 2) // 4, (j - 2) % 4))
                        for dc in range(8):
                            ins = t_.matmul(psP[:, j % 2, :],
                                            wpT[:, qk, dc, :],
                                            lnT[:, dc, 512 * tc:512 * (tc + 1)],
                                            start=(dc == 0), stop=(dc == 7))
                        TA.inc(ins)
                t_.wait_ge(SC.h, sc_w(p_, 2))
                for sl in range(4):
                    if sl >= 2:
                        t_.wait_ge(VC.h, vc_vev(p_, sl - 2, 3))
                    else:
                        if p_ > 0:
                            t_.wait_ge(VC.h, vc_vev(p_ - 1, 2 + sl, 3))
                        t_.wait_ge(SC.h, sc_qk(p_, 1, 2 + sl))
                    for tt in range(4):
                        kt = 4 * sl + tt
                        for dc in range(8):
                            ins = t_.matmul(psP[:, (8 + sl) % 2, 128 * tt:128 * (tt + 1)],
                                            lnT[:, dc, 128 * kt:128 * (kt + 1)],
                                            wpT[:, 2, dc, :],
                                            start=(dc == 0), stop=(dc == 7))
                    TA.inc(ins)
                for w in range(2):
                    b = 64 * w
                    for qc in range(4):
                        nkt = NKT(qc)
                        for kt in range(nkt):
                            pr = None
                            if kt >= 2:
                                pr = sc_exp(p_, w, qc, kt - 2)
                            elif (w, qc) != (0, 0):
                                pw, pq = (w, qc - 1) if qc > 0 else (0, 3)
                                pkt = NKT(pq) - 2 + kt
                                pr = sc_exp(p_, pw, pq, pkt)
                            elif p_ > 0:
                                pr = sc_exp(p_ - 1, 1, 3, 14 + kt)
                            if pr is not None:
                                t_.wait_ge(SC.h, pr)
                            if kt == 0:
                                t_.wait_ge(VC.h, vc_rope(p_, 1, 1))
                            ins = t_.matmul(psS[:, kt % 2, :],
                                            Kr[b:b + 64, 128 * kt:128 * (kt + 1)],
                                            Qr[b:b + 64, 512 * qc:512 * (qc + 1)],
                                            start=True, stop=True)
                            TA.inc(ins)
                        t_.wait_ge(SC.h, sc_exp(p_, w, qc, nkt - 1))
                        t_.wait_ge(VC.h, vc_msk(p_, w, qc, 3))
                        i = p_ * 4 + qc
                        if (w, qc) != (0, 0) or p_ > 0:
                            if qc > 0:
                                pp, pw, pq = p_, w, qc - 1
                            elif w == 1:
                                pp, pw, pq = p_, 0, 3
                            else:
                                pp, pw, pq = p_ - 1, 1, 3
                            t_.wait_ge(SC.h, sc_o64(pp, pw, pq))
                            t_.wait_ge(VC.h, vc_rec(pp, pw, pq))
                        ib = (8 * p_ + 4 * w + qc) % 2
                        for kt in range(nkt):
                            ins = t_.matmul(psA[0:65, :],
                                            Vb[:, kt, 65 * w:65 * w + 65],
                                            Eb[:, ib, kt, :],
                                            start=(kt == 0), stop=(kt == nkt - 1))
                        TA.inc(ins)
                        t_.wait_ge(VC.h, vc_rec(p_, w, qc))
                        TA.inc(t_.matmul(psN[0:64, :], onesf2[64:65, 0:64],
                                         recRb[64:65, :], start=True, stop=True))

        @blk.scalar
        def _(s_):
            s = CE(s_, SC)
            for p_ in range(8):
                for w3 in range(3):
                    s.w(TA, ta_tr(p_, w3))
                    r = s.copy(wpT[:, w3, :, :],
                               psW[:, w3 % 2, :].rearrange("p (c i) -> p c i", i=128))
                    assert r == sc_w(p_, w3)
                for qk in range(2):
                    dst = Qf if qk == 0 else Kf
                    bcol = bcol_bq if qk == 0 else bcol_bk
                    for tc in range(4):
                        s.w(TA, ta_qk(p_, qk, tc))
                        if tc == 0 and p_ > 0:
                            s.w(WR[qk % 2], wr_val[(p_ - 1, qk)])
                            s.w(GC, gc_rp(p_ - 1, qk))
                            s.w(VC, vc_rope(p_ - 1, qk, 1))
                        r = s.activation(dst[:, 512 * tc:512 * (tc + 1)],
                                         psP[:, (4 * qk + tc) % 2, :], AF.Identity,
                                         bias=bcol[:, p_:p_ + 1])
                        assert r == sc_qk(p_, qk, tc)
                for w in range(2):
                    for qc in range(4):
                        nkt = NKT(qc)
                        ib = (8 * p_ + 4 * w + qc) % 2
                        for kt in range(nkt):
                            s.w(TA, ta_sc(p_, w, qc, kt))
                            if kt == 0:
                                G = 8 * p_ + 4 * w + qc
                                if G >= 2:
                                    Gp = G - 2
                                    s.w(TA, ta_av(Gp // 8, (Gp % 8) // 4, Gp % 4))
                            r = s.activation(Eb[:, ib, kt, :], psS[:, kt % 2, :],
                                             AF.Exp, scale=ISR64)
                            assert r == sc_exp(p_, w, qc, kt)
                        s.w(TA, ta_av(p_, w, qc))
                        r = s.copy(o64[0:64, (4 * w + qc) % 2, :], psA[0:64, :])
                        assert r == sc_o64(p_, w, qc)

        @blk.vector
        def _(v_):
            v = CE(v_, VC)
            v.memset(Vb[:, :, 64:65], 1.0)
            v.memset(Vb[:, :, 129:130], 1.0)
            for p_ in range(8):
                for sl in range(4):
                    for tt in range(4):
                        kt = 4 * sl + tt
                        v.w(TA, ta_v(p_, sl))
                        r = v.tensor_copy(
                            Vb[:, kt, :].rearrange("p (w u) -> p w u", u=65)[:, :, 0:64],
                            psP[:, (8 + sl) % 2, 128 * tt:128 * (tt + 1)]
                            .rearrange("p (w u) -> p w u", u=64))
                        assert r == vc_vev(p_, sl, tt)
                for qk in range(2):
                    src = Qf if qk == 0 else Kf
                    rr = rot if qk == 0 else rtm
                    dst = Qr if qk == 0 else Kr
                    v.w(WR[qk % 2], wr_val[(p_, qk)])
                    v.w(GC, gc_rp(p_, qk))
                    r = v.tensor_mul(rr[:, :], rr[:, :], sinb[:, :])
                    assert r == vc_rope(p_, qk, 0)
                    r = v.tensor_add(dst[:, :], src[:, :], rr[:, :])
                    assert r == vc_rope(p_, qk, 1)
                for w in range(2):
                    for qc in range(4):
                        i = p_ * 4 + qc
                        ib = (8 * p_ + 4 * w + qc) % 2
                        for j in range(4):
                            kt = 4 * qc + j
                            v.w(SC, sc_exp(p_, w, qc, kt))
                            v.w(WK, wk_msk)
                            r = v.tensor_mul(Eb[:, ib, kt, :], Eb[:, ib, kt, :],
                                             maskb[:, j, :])
                            assert r == vc_msk(p_, w, qc, j)
                        v.w(TA, ta_av(p_, w, qc))
                        r = v.reciprocal(recRb[64:65, :], psA[64:65, :])
                        assert r == vc_rec(p_, w, qc)
                        v.w(TA, ta_bc(p_, w, qc))
                        v.w(SC, sc_o64(p_, w, qc))
                        if w == 0:
                            v.tensor_mul(aoTb[0:64, p_, 512 * qc:512 * (qc + 1)],
                                         o64[0:64, (4 * w + qc) % 2, :],
                                         psN[0:64, :])
                            r = v.tensor_scalar_add(
                                aoTb[0:64, p_, 512 * qc:512 * (qc + 1)],
                                aoTb[0:64, p_, 512 * qc:512 * (qc + 1)],
                                bcol_bv2[0][0:64, p_:p_ + 1])
                        else:
                            if wm_prev.get(i) is not None:
                                v.w(WM[qc], wm_prev[i])
                            v.tensor_mul(aostg[0:64, qc, :],
                                         o64[0:64, (4 * w + qc) % 2, :],
                                         psN[0:64, :])
                            r = v.tensor_scalar_add(aostg[0:64, qc, :],
                                                    aostg[0:64, qc, :],
                                                    bcol_bv2[1][0:64, p_:p_ + 1])
                        assert r == vc_mul(p_, w, qc)

        fence(blk, [(TA, TA.n), (SC, SC.n), (VC, VC.n), (GC, GC.n)] +
              [(sem, sem.n) for sem in WM])
        cw.close()

        with ExitStack() as co:
            WoT = co.enter_context(nc.sbuf_tensor("b_WoT", [128, 8, DM], BF16))
            psW2 = co.enter_context(nc.psum_tensor("b_psWo", [128, 2, DM], BF16))
            psY = co.enter_context(nc.psum_tensor("b_psY", [128, 2, 512], F32))

            state["wfree"] = None
            state["phase_done"] = [(TA, TA.n), (SC, SC.n), (VC, VC.n)]
            ta_w0, scw = jit_w(p, blk, dr["bw_o"], 0, 8, cn, state, psW2, WoT)

            ta0b = TA.n
            vc0b = VC.n
            gc0b = GC.n
            vc_stt = lambda t: vc0b + 2 * (t + 1)
            gc_add = lambda t: gc0b + t + 1
            xi_val, xo_val, gate = plan_stream(state)

            @blk.sync
            def _(y):
                for t in range(NT):
                    sl = t % NS
                    for sem, val in gate[t]:
                        y.wait_ge(sem.h, val)
                    vv = XI[sl].inc(y.dma_start(out=xsl[:, sl, :],
                                                in_=xin[128 * t:128 * (t + 1), :]), 16)
                    assert vv == xi_val[t]

            @blk.tensor
            def _(t_):
                t_.wait_ge(SC.h, scw(7))
                i = 0
                for t in range(NT):
                    for vch in range(2):
                        if i >= 2:
                            t_.wait_ge(VC.h, vc0b + (i - 1))
                        for hp in range(8):
                            ins = t_.matmul(psY[:, i % 2, :],
                                            aoTb[:, hp, 128 * t:128 * (t + 1)],
                                            WoT[:, hp, 512 * vch:512 * (vch + 1)],
                                            start=(hp == 0), stop=(hp == 7))
                        TA.inc(ins)
                        i += 1

            @blk.vector
            def _(v_):
                v = CE(v_, VC)
                i = 0
                for t in range(NT):
                    v.w(XI[t % NS], xi_val[t])
                    for vch in range(2):
                        v.w(TA, ta0b + i + 1)
                        r = v.scalar_tensor_tensor(
                            xsl[:, t % NS, 512 * vch:512 * (vch + 1)],
                            psY[:, i % 2, :], 1.0,
                            xsl[:, t % NS, 512 * vch:512 * (vch + 1)],
                            ALU.mult, ALU.add)
                        i += 1
                    assert r == vc_stt(t)

            @blk.gpsimd
            def _(g_):
                g = CE(g_, GC)
                for t in range(NT):
                    g.w(VC, vc_stt(t))
                    r = g.tensor_add(xsl[:, t % NS, :], xsl[:, t % NS, :],
                                     borep_b[:, :])
                    assert r == gc_add(t)

            @blk.scalar
            def _(s_):
                for t in range(NT):
                    s_.wait_ge(GC.h, gc_add(t))
                    vv = XO[t % NS].inc(s_.dma_start(out=xout[128 * t:128 * (t + 1), :],
                                                     in_=xsl[:, t % NS, :]), 16)
                    assert vv == xo_val[t]

            xow = [(XO[t % NS], xo_val[t]) for t in range(NT - NS, NT)]
            state["phase_done"] = [(TA, TA.n), (VC, VC.n), (GC, GC.n), (SC, SC.n)] + xow
            fence(blk, [(TA, TA.n), (VC, VC.n), (GC, GC.n), (SC, SC.n)] + xow,
                  engines=("tensor", "scalar", "vector", "gpsimd"))


def make_consts():
    ident = np.eye(128, dtype=np.float32)
    hd = 64
    theta = 1.0 / (10000.0 ** (np.arange(0, hd, 2, dtype=np.float64) / hd))
    pos = np.arange(1, TOK + 1, dtype=np.float64)
    ang = pos[None, :] * theta[:, None]
    cos = np.cos(ang).astype(np.float32)
    sin = np.sin(ang).astype(np.float32)
    cos64 = np.concatenate([cos, cos], axis=0)
    sinm64 = np.concatenate([-sin, sin], axis=0)
    c_cos = np.concatenate([cos64, cos64], axis=0)
    c_sinm = np.concatenate([sinm64, sinm64], axis=0)
    masks = np.zeros((4, 128, 512), dtype=np.float32)
    for o in range(4):
        kk = np.arange(128)[:, None] + 128 * o
        qq = np.arange(512)[None, :]
        masks[o] = (kk <= qq).astype(np.float32)
    sel4 = np.zeros((4, 128), dtype=np.float32)
    for k in range(4):
        sel4[k, 32 * k:32 * k + 32] = 1.0
    return ident, c_cos, c_sinm, masks, sel4


def kernel(L0_attn_a_x, L0_attn_a_in_proj_w, L0_attn_a_in_proj_b, L0_attn_a_out_w,
           L0_attn_a_out_b, L0_attn_b_Wq, L0_attn_b_bq, L0_attn_b_Wk, L0_attn_b_bk,
           L0_attn_b_Wv, L0_attn_b_bv, L0_attn_b_Wo, L0_attn_b_bo, L0_ffn_q_w,
           L0_ffn_q_b, L0_ffn_k_w, L0_ffn_k_b, L0_ffn_v_w, L0_ffn_v_b, L0_ffn_o_w,
           L0_ffn_o_b, L0_ffn_router_w, L0_ffn_router_b, L0_ffn_w1, L0_ffn_b1,
           L0_ffn_w2, L0_ffn_b2, _upto=4):
    x = np.ascontiguousarray(np.asarray(L0_attn_a_x, dtype=np.float32))
    ident, c_cos, c_sinm, masks, sel4 = make_consts()
    f32c = lambda a: np.ascontiguousarray(np.asarray(a, dtype=np.float32))
    row = lambda a: np.ascontiguousarray(np.asarray(a, dtype=np.float32).reshape(1, -1))
    shared = {
        "aw_inproj": f32c(L0_attn_a_in_proj_w), "aw_inb": row(L0_attn_a_in_proj_b),
        "aw_outw": f32c(L0_attn_a_out_w), "aw_outb": row(L0_attn_a_out_b),
        "bw_q": f32c(L0_attn_b_Wq), "bw_k": f32c(L0_attn_b_Wk),
        "bw_v": f32c(L0_attn_b_Wv), "bw_o": f32c(L0_attn_b_Wo),
        "bb_q": row(L0_attn_b_bq), "bb_k": row(L0_attn_b_bk),
        "bb_v": row(L0_attn_b_bv), "bb_o": row(L0_attn_b_bo),
        "fw_q": f32c(L0_ffn_q_w), "fw_k": f32c(L0_ffn_k_w),
        "fw_v": f32c(L0_ffn_v_w), "fw_o": f32c(L0_ffn_o_w),
        "fb_q": row(L0_ffn_q_b), "fb_k": row(L0_ffn_k_b),
        "fb_v": row(L0_ffn_v_b), "fb_o": row(L0_ffn_o_b),
        "fw_r": f32c(L0_ffn_router_w), "fb_r": row(L0_ffn_router_b),
        "fw_1": f32c(L0_ffn_w1), "fb_1": f32c(L0_ffn_b1),
        "fw_2": f32c(L0_ffn_w2), "fb_2": f32c(L0_ffn_b2),
        "c_ident": ident, "c_cos": c_cos, "c_sinm": c_sinm, "c_mask": masks,
        "c_sel4": sel4,
    }
    nc = build_program(upto=_upto)
    in_maps = []
    for i in range(NC8):
        m = dict(shared)
        m["x"] = np.ascontiguousarray(x[i])
        in_maps.append(m)
    res = run_bass_kernel_spmd(nc, in_maps, core_ids=list(range(NC8)))
    return np.stack([np.asarray(r["out"], dtype=np.float32) for r in res.results], axis=0)


# version 05 (final)
# speedup vs baseline: 1.1944x; 1.0088x over previous
import numpy as np
import ml_dtypes
from contextlib import ExitStack
import concourse.bass as bass
import concourse.mybir as mybir
from concourse.bass_utils import run_bass_kernel_spmd

F32 = mybir.dt.float32
BF16 = mybir.dt.bfloat16
AF = mybir.ActivationFunctionType
ALU = mybir.AluOpType
AX = mybir.AxisListType

TOK = 2048
NT = TOK // 128
DM = 1024
NC8 = 8
NS = 4

ISR128 = float(1.0 / np.sqrt(128.0))
ISR32 = float(1.0 / np.sqrt(32.0))
ISR64 = float(1.0 / np.sqrt(64.0))
EPS_LN = 1e-5
_DBG_MODE = "full"


class Sem:
    __slots__ = ("h", "n")

    def __init__(self, handle):
        self.h = handle
        self.n = 0

    def inc(self, instr, k=1):
        instr.then_inc(self.h, k)
        self.n += k
        return self.n


class CE:

    _RAW = {"wait_ge", "dma_start", "nop"}

    def __init__(self, eng, chain):
        self._e = eng
        self._c = chain

    @property
    def n(self):
        return self._c.n

    def w(self, sem, val):
        self._e.wait_ge(sem.h, val)

    def raw(self):
        return self._e

    def __getattr__(self, name):
        f = getattr(self._e, name)
        if name in CE._RAW:
            return f

        def g(*a, **k):
            self._e.wait_ge(self._c.h, self._c.n)
            ins = f(*a, **k)
            return self._c.inc(ins)
        return g


class P:
    def __init__(self, nc, ctx):
        self.nc = nc
        self.ctx = ctx
        self._sems = {}

    def sem(self, name):
        if name not in self._sems:
            self._sems[name] = Sem(self.ctx.enter_context(self.nc.semaphore(name)))
        return self._sems[name]


def fence(blk, waits, engines=("tensor", "scalar", "vector")):
    def body(e):
        for sem, val in waits:
            e.wait_ge(sem.h, val)
    for name in engines:
        getattr(blk, name)(body)


def plan_stream(state, nt=NT, store=True):
    XI, XO = state["XI"], state["XO"]
    xi_n = [s.n for s in XI]
    xi_val, gate = [], []
    prev = [list(fr) for fr in state["slot_free"]]
    xo_val = [None] * nt
    if store:
        xo_n = [s.n for s in XO]
    for t in range(nt):
        s = t % NS
        gate.append(list(prev[s]))
        xi_n[s] += 16
        xi_val.append(xi_n[s])
        if store:
            xo_n[s] += 16
            xo_val[t] = xo_n[s]
            prev[s] = [(XO[s], xo_n[s])]
        else:
            prev[s] = []
    if store:
        state["slot_free"] = prev
    return xi_val, xo_val, gate


def build_program(upto=4):
    nc = bass.Bass("TRN2", target_bir_lowering=False)
    ctx = ExitStack()
    p = P(nc, ctx)

    dr = {}
    def din(name, shape):
        dr[name] = nc.dram_tensor(name, list(shape), F32, kind="ExternalInput").ap()

    din("x", (TOK, DM))
    din("aw_inproj", (3072, DM)); din("aw_inb", (1, 3072))
    din("aw_outw", (DM, DM)); din("aw_outb", (1, DM))
    din("bw_q", (DM, DM)); din("bw_k", (DM, DM)); din("bw_v", (DM, DM)); din("bw_o", (DM, DM))
    din("bb_q", (1, DM)); din("bb_k", (1, DM)); din("bb_v", (1, DM)); din("bb_o", (1, DM))
    din("fw_q", (DM, DM)); din("fw_k", (DM, 512)); din("fw_v", (DM, 512)); din("fw_o", (DM, DM))
    din("fb_q", (1, DM)); din("fb_k", (1, 512)); din("fb_v", (1, 512)); din("fb_o", (1, DM))
    din("fw_r", (DM, 8)); din("fb_r", (1, 8))
    din("fw_1", (8, DM, 2048)); din("fb_1", (8, 2048))
    din("fw_2", (8, 2048, DM)); din("fb_2", (8, DM))
    din("c_ident", (128, 128))
    din("c_cos", (128, TOK)); din("c_sinm", (128, TOK))
    din("c_mask", (4, 128, 512)); din("c_sel4", (4, 128))
    out_hbm = nc.dram_tensor("out", [TOK, DM], F32, kind="ExternalOutput").ap()
    xsA = nc.dram_tensor("xsA", [TOK, DM], F32, kind="Internal").ap()
    xsB = nc.dram_tensor("xsB", [TOK, DM], F32, kind="Internal").ap()

    with ctx:
        emit(p, dr, out_hbm, xsA, xsB, upto)
    return nc


def emit(p, dr, out_hbm, xsA, xsB, upto):
    nc = p.nc
    blk = p.ctx.enter_context(nc.Block())
    alloc = lambda n, sh, dt=F32: p.ctx.enter_context(nc.sbuf_tensor(n, sh, dt))

    xsl = alloc("xsl", [128, NS, DM], F32)
    identb = alloc("identb", [128, 128], BF16)
    identf = alloc("identf", [128, 128], F32)
    onesf = alloc("onesf", [1, 128], F32)
    onesf2 = alloc("onesf2", [128, 64], F32)
    onesb = alloc("onesb", [128, 1], BF16)

    CD = p.sem("CD")
    SC = p.sem("SC"); VC = p.sem("VC"); GC = p.sem("GC")
    TM = p.sem("TM")
    XI = [p.sem(f"XI{s}") for s in range(NS)]
    XO = [p.sem(f"XO{s}") for s in range(NS)]
    WID = p.sem("WID")

    @blk.sync
    def _(y):
        CD.inc(y.dma_start(out=identf[:, :], in_=dr["c_ident"]), 16)

    ge_ones = [None]

    @blk.gpsimd
    def _(g_):
        g = CE(g_, GC)
        WID.inc(g_.dma_start(out=identb[:, :], in_=dr["c_ident"]), 16)
        g.memset(onesf[:, :], 1.0)
        g.memset(onesf2[:, :], 1.0)
        ge_ones[0] = g.memset(onesb[:, :], 1.0)
    ge_ones = ge_ones[0]

    bcol_a = alloc("bcol_a", [128, 16], F32)
    bcol_fo = alloc("bcol_fo", [128, 8], F32)
    bcol_fq = alloc("bcol_fq", [128, 4], F32)
    bcol_fk = alloc("bcol_fk", [128, 4], F32)
    bcol_fv2 = alloc("bcol_fv2", [128, 4], F32)
    bcol_bq = alloc("bcol_bq", [128, 8], F32)
    bcol_bk = alloc("bcol_bk", [128, 8], F32)
    bcol_f1 = alloc("bcol_f1", [128, 8, 16], F32)
    bcol_bve = alloc("bcol_bve", [64, 8], F32)
    bcol_bvo = alloc("bcol_bvo", [64, 8], F32)
    sel4f = alloc("sel4f", [128, 128], F32)
    borep_b = alloc("borep_b", [128, DM], F32)
    rbrep = alloc("rbrep", [128, 8], F32)
    fb2n = alloc("fb2n", [8, DM], BF16)
    arep = ExitStack()
    bvrep_a = arep.enter_context(nc.sbuf_tensor("bvrep_a", [128, DM], F32))
    borep_a2 = arep.enter_context(nc.sbuf_tensor("borep_a2", [128, DM], F32))

    with ExitStack() as pc:
        brows = [("aw_inb", 3072), ("aw_outb", DM), ("bb_q", DM), ("bb_k", DM),
                 ("bb_v", DM), ("bb_o", DM), ("fb_q", DM), ("fb_k", 512),
                 ("fb_v", 512), ("fb_o", DM), ("fb_r", 8)]
        brow = {}
        for name, n in brows:
            brow[name] = pc.enter_context(nc.sbuf_tensor("brow_" + name, [1, n], F32))
        fb1row = pc.enter_context(nc.sbuf_tensor("fb1row", [1, 16384], F32))
        fb2n32 = pc.enter_context(nc.sbuf_tensor("fb2n32", [8, DM], F32))
        qbrow = pc.enter_context(nc.sbuf_tensor("qbrow", [1, 512], F32))
        pbias = pc.enter_context(nc.psum_tensor("pbias", [128, 2, 512], F32))
        prep = pc.enter_context(nc.psum_tensor("prep", [128, 2, 512], F32))

        @blk.sync
        def _(y):
            for name, n in brows:
                CD.inc(y.dma_start(out=brow[name][:, :], in_=dr[name]), 16)
            CD.inc(y.dma_start(out=fb1row[:, :],
                               in_=dr["fb_1"].rearrange("e f -> (e f)")
                               .rearrange("(o f) -> o f", o=1)), 16)
            CD.inc(y.dma_start(out=fb2n32[:, :], in_=dr["fb_2"]), 16)
            CD.inc(y.dma_start(out=sel4f[0:4, :], in_=dr["c_sel4"]), 16)

        ve_pro = [None]

        @blk.vector
        def _(v_):
            v = CE(v_, VC)
            v.w(CD, CD.n)
            v.tensor_copy(fb2n[:, :], fb2n32[:, :])
            ve_pro[0] = v.tensor_copy(
                qbrow[:, :].rearrange("o (h s) -> o h s", s=32),
                brow["fb_q"][:, :].rearrange("o (h s) -> o h s", s=64)[:, :, 0:32])
        ve_pro = ve_pro[0]

        jobs = []
        for c in range(16):
            jobs.append((bcol_a[:, c:c + 1], brow["aw_inb"][0:1, 128 * c:128 * (c + 1)], 128))
        for c in range(8):
            jobs.append((bcol_fo[:, c:c + 1], brow["fb_o"][0:1, 128 * c:128 * (c + 1)], 128))
        for cc in range(4):
            jobs.append((bcol_fq[0:64, cc:cc + 1], qbrow[0:1, 128 * cc:128 * cc + 64], 64))
            jobs.append((bcol_fq[64:128, cc:cc + 1], qbrow[0:1, 128 * cc + 64:128 * (cc + 1)], 64))
            jobs.append((bcol_fk[0:64, cc:cc + 1], brow["fb_k"][0:1, 128 * cc:128 * cc + 64], 64))
            jobs.append((bcol_fk[64:128, cc:cc + 1], brow["fb_k"][0:1, 128 * cc + 64:128 * (cc + 1)], 64))
            jobs.append((bcol_fv2[0:64, cc:cc + 1], brow["fb_v"][0:1, 128 * cc:128 * cc + 64], 64))
            jobs.append((bcol_fv2[64:128, cc:cc + 1], brow["fb_v"][0:1, 128 * cc + 64:128 * (cc + 1)], 64))
        for c in range(8):
            jobs.append((bcol_bq[:, c:c + 1], brow["bb_q"][0:1, 128 * c:128 * (c + 1)], 128))
            jobs.append((bcol_bk[:, c:c + 1], brow["bb_k"][0:1, 128 * c:128 * (c + 1)], 128))
        for g in range(8):
            jobs.append((bcol_bve[0:64, g:g + 1], brow["bb_v"][0:1, 128 * g:128 * g + 64], 64))
            jobs.append((bcol_bvo[0:64, g:g + 1], brow["bb_v"][0:1, 128 * g + 64:128 * (g + 1)], 64))
        for e in range(8):
            for c in range(16):
                jobs.append((bcol_f1[:, e, c:c + 1],
                             fb1row[0:1, 2048 * e + 128 * c:2048 * e + 128 * (c + 1)], 128))

        tm0 = TM.n
        sc0 = SC.n
        if _DBG_MODE == "nobias":
            jobs = []
        elif _DBG_MODE.startswith("jobs:"):
            a, b = _DBG_MODE[5:].split(":")
            jobs = jobs[int(a):int(b)]

        @blk.tensor
        def _(t_):
            t_.wait_ge(CD.h, CD.n)
            t_.wait_ge(GC.h, ge_ones)
            t_.wait_ge(VC.h, ve_pro)
            for i, (dst, src, m) in enumerate(jobs):
                g = i // 16
                if g >= 2 and i % 16 == 0:
                    t_.wait_ge(SC.h, sc0 + 16 * (g - 1))
                TM.inc(t_.matmul(pbias[0:m, g % 2, (i % 16):(i % 16) + 1],
                                 src, onesf[0:1, 0:1], start=True, stop=True))

        @blk.scalar
        def _(s_):
            s = CE(s_, SC)
            for i, (dst, src, m) in enumerate(jobs):
                g = i // 16
                s.w(TM, tm0 + min(len(jobs), (g + 1) * 16))
                s.copy(dst, pbias[0:m, g % 2, (i % 16):(i % 16) + 1])
        assert SC.n == sc0 + len(jobs)

        reps = []
        for j in range(2):
            reps.append((bvrep_a[:, 512 * j:512 * (j + 1)],
                         brow["aw_inb"][0:1, 2048 + 512 * j:2048 + 512 * (j + 1)], 512, 1.0))
        for j in range(2):
            reps.append((borep_a2[:, 512 * j:512 * (j + 1)],
                         brow["aw_outb"][0:1, 512 * j:512 * (j + 1)], 512, 2.0))

        for j in range(2):
            reps.append((borep_b[:, 512 * j:512 * (j + 1)],
                         brow["bb_o"][0:1, 512 * j:512 * (j + 1)], 512, 1.0))
        reps.append((rbrep[:, :], brow["fb_r"][0:1, :], 8, 1.0))

        tr0 = TM.n
        sr0 = SC.n
        if _DBG_MODE in ("nobias", "norep"):
            reps = []

        @blk.tensor
        def _(t_):
            for i, (dst, src, w, sc) in enumerate(reps):
                if i >= 2:
                    t_.wait_ge(SC.h, sr0 + (i - 1))
                TM.inc(t_.matmul(prep[:, i % 2, 0:w], onesf[0:1, :], src,
                                 start=True, stop=True))

        @blk.scalar
        def _(s_):
            s = CE(s_, SC)
            for i, (dst, src, w, sc) in enumerate(reps):
                s.w(TM, tr0 + i + 1)
                if sc == 1.0:
                    s.copy(dst, prep[:, i % 2, 0:w])
                else:
                    s.activation(dst, prep[:, i % 2, 0:w], AF.Copy, scale=sc)

        fence(blk, [(SC, SC.n), (TM, TM.n), (VC, ve_pro), (WID, 16)])

    cn = {"xsl": xsl, "wstage": None, "identb": identb, "identf": identf,
          "onesf": onesf, "onesf2": onesf2, "onesb": onesb, "sel4f": sel4f}
    state = {"wfree": None, "XI": XI, "XO": XO,
             "slot_free": [[] for _ in range(NS)]}
    state["phase_done"] = [(SC, SC.n), (TM, TM.n), (VC, VC.n)]

    seq = []
    if upto >= 1:
        seq.append(("a", xsA, xsB if upto > 1 else out_hbm))
    if upto >= 2:
        seq.append(("f1", xsA, xsB if upto > 2 else out_hbm))
    if upto >= 3:
        seq.append(("b", xsA, xsB if upto > 3 else out_hbm))
    if upto >= 4:
        seq.append(("f2", xsA, out_hbm))

    xin = dr["x"]
    for tag, xmid, xout in seq:
        if tag == "a":
            emit_attn_a(p, blk, dr, cn, bcol_a, bvrep_a, borep_a2,
                        state, xin, xmid, xout)
            arep.close()
        elif tag in ("f1", "f2"):
            emit_ffn(p, blk, dr, cn, bcol_fq, bcol_fk, bcol_fo, bcol_f1,
                     bcol_fv2, rbrep, fb2n, state, tag, xin, xmid, xout)
        else:
            emit_attn_b(p, blk, dr, cn, bcol_bq, bcol_bk, (bcol_bve, bcol_bvo), borep_b,
                        state, xin, xmid, xout)
        xin = xout

    if upto == 0:
        xi_val, xo_val, gate = plan_stream(state)

        @blk.sync
        def _(y):
            for t in range(NT):
                s = t % NS
                for sem, val in gate[t]:
                    y.wait_ge(sem.h, val)
                v = XI[s].inc(y.dma_start(out=xsl[:, s, :],
                                          in_=dr["x"][128 * t:128 * (t + 1), :]), 16)
                assert v == xi_val[t]
                y.wait_ge(XI[s].h, xi_val[t])
                v = XO[s].inc(y.dma_start(out=out_hbm[128 * t:128 * (t + 1), :],
                                          in_=xsl[:, s, :]), 16)
                assert v == xo_val[t]

    @blk.sync
    def _(y):
        for fr in state["slot_free"]:
            for sem, val in fr:
                y.wait_ge(sem.h, val)


def emit_ln(p, blk, cn, state, lnT, tag, xin, xmid, dup=0, add_hs=False):
    nc = p.nc
    SC, VC, TM = p.sem("SC"), p.sem("VC"), p.sem("TM")
    XI, XO = state["XI"], state["XO"]
    xsl, identb = cn["xsl"], cn["identb"]

    with ExitStack() as c:
        sums = c.enter_context(nc.sbuf_tensor(f"ln_sums_{tag}", [128, NT], F32))
        sumsq = c.enter_context(nc.sbuf_tensor(f"ln_sumsq_{tag}", [128, NT], F32))
        means = c.enter_context(nc.sbuf_tensor(f"ln_means_{tag}", [128, NT], F32))
        rstds = c.enter_context(nc.sbuf_tensor(f"ln_rstds_{tag}", [128, NT], F32))
        s1p = c.enter_context(nc.sbuf_tensor(f"ln_s1p_{tag}", [128, NT], F32))
        s2p = c.enter_context(nc.sbuf_tensor(f"ln_s2p_{tag}", [128, NT], F32))
        vscr = c.enter_context(nc.sbuf_tensor(f"ln_vscr_{tag}", [128, 4, NT], F32))
        sqscr = c.enter_context(nc.sbuf_tensor(f"ln_sqscr_{tag}", [128, DM], F32))
        lnscr = c.enter_context(nc.sbuf_tensor(f"ln_lnscr_{tag}", [128, 1], F32))
        lnbf = c.enter_context(nc.sbuf_tensor(f"ln_lnbf_{tag}", [128, 2, DM], BF16))
        psT = c.enter_context(nc.psum_tensor(f"ln_psT_{tag}", [128, 2, DM], BF16))

        sc0, vc0, tm0 = SC.n, VC.n, TM.n
        xi_val, xo_val, gate = plan_stream(state, store=add_hs)

        nv3 = 4 if add_hs else 1
        sc_sq, sc_rstd, sc_evac = {}, {}, {}
        vc_sums, vc_veps, vc_xup = {}, {}, {}
        n = sc0
        for u in range(NT + 2):
            if u < NT:
                n += 1; sc_sq[u] = n
            if 1 <= u <= NT:
                n += 2; sc_rstd[u - 1] = n
            if 2 <= u:
                n += 1; sc_evac[u - 2] = n
        m = vc0
        for u in range(NT + 2):
            if u < NT:
                m += 1; vc_sums[u] = m
            if 1 <= u <= NT:
                m += 4; vc_veps[u - 1] = m
            if 2 <= u:
                m += nv3; vc_xup[u - 2] = m
        tm_tr = lambda t: tm0 + t + 1
        if not add_hs:
            fr = [[] for _ in range(NS)]
            for t in range(NT):
                fr[t % NS] = [(VC, vc_xup[t]), (SC, sc_sq[t])]
            state["slot_free"] = fr

        @blk.sync
        def _(y):
            for t in range(NT):
                s = t % NS
                if t >= NS and not add_hs:
                    y.wait_ge(VC.h, vc_xup[t - NS])
                    y.wait_ge(SC.h, sc_sq[t - NS])
                for sem, val in gate[t]:
                    y.wait_ge(sem.h, val)
                v = XI[s].inc(y.dma_start(out=xsl[:, s, :],
                                          in_=xin[128 * t:128 * (t + 1), :]), 16)
                assert v == xi_val[t]

        @blk.scalar
        def _(s_):
            s = CE(s_, SC)
            for u in range(NT + 2):
                if u < NT:
                    s.w(XI[u % NS], xi_val[u])
                    r = s.activation(sqscr[:, :], xsl[:, u % NS, :], AF.Square,
                                     accum_out=sumsq[:, u:u + 1])
                    assert r == sc_sq[u]
                if 1 <= u <= NT:
                    t = u - 1
                    s.w(VC, vc_veps[t])
                    s.activation(lnscr[:, 0:1], vscr[:, 2, t:t + 1], AF.Ln)
                    r = s.activation(rstds[:, t:t + 1], lnscr[:, 0:1], AF.Exp, scale=-0.5)
                    assert r == sc_rstd[t]
                if 2 <= u:
                    t = u - 2
                    s.w(TM, tm_tr(t))
                    r = s.copy(lnT[:, :, 128 * t:128 * (t + 1)],
                               psT[:, t % 2, :].rearrange("p (c t) -> p c t", t=128))
                    assert r == sc_evac[t]
                    if add_hs:
                        s_.wait_ge(VC.h, vc_xup[t])
                        v = XO[t % NS].inc(s_.dma_start(out=xmid[128 * t:128 * (t + 1), :],
                                                        in_=xsl[:, t % NS, :]), 16)
                        assert v == xo_val[t]

        @blk.vector
        def _(v_):
            v = CE(v_, VC)
            for u in range(NT + 2):
                if u < NT:
                    v.w(XI[u % NS], xi_val[u])
                    r = v.reduce_sum(sums[:, u:u + 1], xsl[:, u % NS, :], axis=AX.X)
                    assert r == vc_sums[u]
                if 1 <= u <= NT:
                    t = u - 1
                    v.w(SC, sc_sq[t])
                    v.tensor_scalar(means[:, t:t + 1], sums[:, t:t + 1], 1.0 / DM, None, ALU.mult)
                    v.tensor_scalar(vscr[:, 0, t:t + 1], means[:, t:t + 1], means[:, t:t + 1],
                                    None, ALU.mult)
                    v.tensor_scalar(vscr[:, 1, t:t + 1], sumsq[:, t:t + 1], 1.0 / DM, EPS_LN,
                                    ALU.mult, ALU.add)
                    r = v.tensor_sub(vscr[:, 2, t:t + 1], vscr[:, 1, t:t + 1], vscr[:, 0, t:t + 1])
                    assert r == vc_veps[t]
                if 2 <= u:
                    t = u - 2
                    v.w(SC, sc_rstd[t])
                    if add_hs:
                        v.tensor_scalar(s1p[:, t:t + 1], rstds[:, t:t + 1], 1.0, None, ALU.add)
                        v.tensor_scalar(s2p[:, t:t + 1], means[:, t:t + 1], rstds[:, t:t + 1],
                                        -1.0, ALU.mult, ALU.mult)
                    if t >= 2:
                        v.w(TM, tm_tr(t - 2))
                    r = v.tensor_scalar(lnbf[:, t % 2, :], xsl[:, t % NS, :],
                                        means[:, t:t + 1], rstds[:, t:t + 1],
                                        ALU.subtract, ALU.mult)
                    if add_hs:
                        r = v.tensor_scalar(xsl[:, t % NS, :], xsl[:, t % NS, :],
                                            s1p[:, t:t + 1], s2p[:, t:t + 1],
                                            ALU.mult, ALU.add)
                    assert r == vc_xup[t]

        @blk.tensor
        def _(t_):
            for t in range(NT):
                t_.wait_ge(VC.h, vc_xup[t])
                if t >= 2:
                    t_.wait_ge(SC.h, sc_evac[t - 2])
                for cc in range(8):
                    ins = t_.transpose(psT[:, t % 2, 128 * cc:128 * (cc + 1)],
                                       lnbf[:, t % 2, 128 * cc:128 * (cc + 1)],
                                       identb[:, :])
                TM.inc(ins)
        assert TM.n == tm_tr(NT - 1)

        dup_dep = None
        if dup:
            vdup = [None]

            @blk.vector
            def _(v_):
                v = CE(v_, VC)
                v.w(SC, sc_evac[NT - 1])
                vdup[0] = v.tensor_copy(lnT[:, :, TOK:TOK + dup], lnT[:, :, 0:dup])
            dup_dep = (VC, vdup[0])

        ln_done = [(VC, VC.n), (SC, SC.n), (TM, TM.n)]
        return (SC, sc_evac[NT - 1]), dup_dep, ln_done


def jit_w(p, blk, dr_w, row0, nchunk, cn, state, psW, dst):
    TA, SC = p.sem("TA"), p.sem("SC")
    WJ = [p.sem("WJA"), p.sem("WJB")]
    wstage, identb = cn["wstage"], cn["identb"]
    ta0 = TA.n
    sc0 = SC.n
    sc_evac = lambda k: sc0 + k + 1
    wfree = state["wfree"]
    pdone = state.get("phase_done") or []
    wjv = [None] * nchunk

    @blk.gpsimd
    def _(g):
        for k in range(nchunk):
            if k < 2:
                if wfree is not None:
                    g.wait_ge(wfree[0].h, wfree[1])
                else:
                    for sem, val in pdone:
                        g.wait_ge(sem.h, val)
            else:
                g.wait_ge(TA.h, ta0 + (k - 1))
            wjv[k] = WJ[k % 2].inc(
                g.dma_start(out=wstage[:, k % 2, 0:DM],
                            in_=dr_w[row0 + 128 * k:row0 + 128 * (k + 1), :]), 16)

    @blk.tensor
    def _(t_):
        for k in range(nchunk):
            t_.wait_ge(WJ[k % 2].h, wjv[k])
            if k >= 2:
                t_.wait_ge(SC.h, sc_evac(k - 2))
            for cc in range(8):
                ins = t_.transpose(psW[:, k % 2, 128 * cc:128 * (cc + 1)],
                                   wstage[:, k % 2, 128 * cc:128 * (cc + 1)],
                                   identb[:, :])
            TA.inc(ins)

    @blk.scalar
    def _(s_):
        s = CE(s_, SC)
        for k in range(nchunk):
            s.w(TA, ta0 + k + 1)
            r = s.copy(dst[:, :, 128 * k:128 * (k + 1)],
                       psW[:, k % 2, :].rearrange("p (c i) -> p c i", i=128))
            assert r == sc_evac(k)
    state["wfree"] = (TA, ta0 + nchunk)
    return ta0, sc_evac


def emit_attn_a(p, blk, dr, cn, bcol_a, bvrep_a, borep_a2, state, xin, xmid, xout):
    nc = p.nc
    SC, VC, GC, TM, TA = p.sem("SC"), p.sem("VC"), p.sem("GC"), p.sem("TM"), p.sem("TA")
    XI, XO = state["XI"], state["XO"]
    xsl = cn["xsl"]
    onesf, onesb, identb = cn["onesf"], cn["onesb"], cn["identb"]
    TW = TOK + 64

    with ExitStack() as c:
        sb = lambda n, sh, dt=F32: c.enter_context(nc.sbuf_tensor(n, sh, dt))
        cn["wstage"] = sb("a_wstage", [128, 2, 2048], BF16)
        Vwa = sb("a_Vwa", [128, NT, DM], BF16)
        QKT = sb("a_QKT", [128, 16, TW], BF16)
        cl = c.enter_context(ExitStack())
        lnT = cl.enter_context(nc.sbuf_tensor("a_lnT", [128, 8, TW], BF16))

        ln_all, dup_dep, ln_done = emit_ln(p, blk, cn, state, lnT, "a", xin, None,
                                           dup=64, add_hs=False)
        fence(blk, ln_done)
        if _DBG_MODE == "a:ln":
            _dbg_copy_out(p, blk, cn, state, xin, xout)
            return

        with ExitStack() as cv:
            WvT = cv.enter_context(nc.sbuf_tensor("a_WvT", [128, 8, DM], BF16))
            psW = cv.enter_context(nc.psum_tensor("a_psWv", [128, 2, DM], BF16))
            psV = cv.enter_context(nc.psum_tensor("a_psV", [128, 2, 512], F32))

            ta_w0, scw = jit_w(p, blk, dr["aw_inproj"], 2048, 8, cn, state, psW, WvT)

            ta_v0 = TA.n
            vc0 = VC.n
            vc_vadd = lambda i: vc0 + i + 1

            @blk.tensor
            def _(t_):
                t_.wait_ge(SC.h, scw(7))
                t_.wait_ge(SC.h, ln_all[1])
                t_.wait_ge(dup_dep[0].h, dup_dep[1])
                i = 0
                for wp in range(NT):
                    t0 = 32 + 128 * wp
                    for vch in range(2):
                        if i >= 2:
                            t_.wait_ge(VC.h, vc_vadd(i - 2))
                        for cc in range(8):
                            ins = t_.matmul(psV[:, i % 2, :],
                                            lnT[:, cc, t0:t0 + 128],
                                            WvT[:, cc, 512 * vch:512 * (vch + 1)],
                                            start=(cc == 0), stop=(cc == 7))
                        TA.inc(ins)
                        i += 1

            @blk.vector
            def _(v_):
                v = CE(v_, VC)
                i = 0
                for wp in range(NT):
                    for vch in range(2):
                        v.w(TA, ta_v0 + i + 1)
                        r = v.tensor_add(Vwa[:, wp, 512 * vch:512 * (vch + 1)],
                                         psV[:, i % 2, :],
                                         bvrep_a[:, 512 * vch:512 * (vch + 1)])
                        assert r == vc_vadd(i)
                        i += 1
            vc_v = vc_vadd
            fence(blk, [(VC, VC.n), (TA, TA.n), (SC, SC.n)])
        if _DBG_MODE == "a:v":
            _dbg_copy_out(p, blk, cn, state, xin, xout)
            return

        with ExitStack() as cq:
            WqkTr = cq.enter_context(nc.sbuf_tensor("a_WqkTr", [128, 2, 8, 128], BF16))
            psW = cq.enter_context(nc.psum_tensor("a_psWq", [128, 2, DM], BF16))
            psQ = cq.enter_context(nc.psum_tensor("a_psQ", [128, 2, 512], F32))

            WJ = [p.sem("WJA"), p.sem("WJB")]
            wstage = cn["wstage"]
            ta0, sc0 = TA.n, SC.n
            NJ = TOK // 512
            wfree = state["wfree"]
            wjv = [None] * 16
            sc_w = lambda k: sc0 + k * (1 + NJ) + 1
            sc_q = lambda k, j: sc0 + k * (1 + NJ) + 1 + (j + 1)

            @blk.gpsimd
            def _(g):
                for k in range(16):
                    if k < 2:
                        g.wait_ge(wfree[0].h, wfree[1])
                    else:
                        g.wait_ge(TA.h, ta0 + (k - 1) * (1 + NJ) + 1)
                    wjv[k] = WJ[k % 2].inc(
                        g.dma_start(out=wstage[:, k % 2, 0:DM],
                                    in_=dr["aw_inproj"][128 * k:128 * (k + 1), :]), 16)

            @blk.tensor
            def _(t_):
                for k in range(16):
                    t_.wait_ge(WJ[k % 2].h, wjv[k])
                    if k >= 2:
                        t_.wait_ge(SC.h, sc_w(k - 2))
                    for cc in range(8):
                        ins = t_.transpose(psW[:, k % 2, 128 * cc:128 * (cc + 1)],
                                           wstage[:, k % 2, 128 * cc:128 * (cc + 1)],
                                           identb[:, :])
                    TA.inc(ins)
                    t_.wait_ge(SC.h, sc_w(k))
                    for j in range(NJ):
                        gi = k * NJ + j
                        if gi >= 2:
                            pk, pj = (gi - 2) // NJ, (gi - 2) % NJ
                            t_.wait_ge(SC.h, sc_q(pk, pj))
                        for cc in range(8):
                            ins = t_.matmul(psQ[:, gi % 2, :],
                                            WqkTr[:, k % 2, cc, :],
                                            lnT[:, cc, 512 * j:512 * (j + 1)],
                                            start=(cc == 0), stop=(cc == 7))
                        TA.inc(ins)

            @blk.scalar
            def _(s_):
                s = CE(s_, SC)
                for k in range(16):
                    s.w(TA, ta0 + k * (1 + NJ) + 1)
                    r = s.copy(WqkTr[:, k % 2, :, :],
                               psW[:, k % 2, :].rearrange("p (c i) -> p c i", i=128))
                    assert r == sc_w(k)
                    for j in range(NJ):
                        s.w(TA, ta0 + k * (1 + NJ) + 1 + (j + 1))
                        r = s.activation(QKT[:, k, 512 * j:512 * (j + 1)],
                                         psQ[:, (k * NJ + j) % 2, :], AF.Identity,
                                         bias=bcol_a[:, k:k + 1])
                        assert r == sc_q(k, j)

            vdup = [None]

            @blk.vector
            def _(v_):
                v = CE(v_, VC)
                v.w(SC, SC.n)
                vdup[0] = v.tensor_copy(QKT[:, :, TOK:TW], QKT[:, :, 0:64])
            vc_qkdup = vdup[0]

            state["wfree"] = (TA, ta0 + 15 * (1 + NJ) + 1)
            fence(blk, [(SC, SC.n), (TA, TA.n), (VC, VC.n)])

        cl.close()
        if _DBG_MODE == "a:qk":
            _dbg_copy_out(p, blk, cn, state, xin, xout)
            return

        OT = sb("a_OT", [128, 8, TOK], BF16)
        with ExitStack() as ca:
            EA = ca.enter_context(nc.sbuf_tensor("a_EA", [128, 512], BF16))
            Asb = ca.enter_context(nc.sbuf_tensor("a_Asb", [128, 512], BF16))
            recD = ca.enter_context(nc.sbuf_tensor("a_recD", [1, 1024], F32))
            psS = ca.enter_context(nc.psum_tensor("a_psS", [128, 512], F32))
            psDa = ca.enter_context(nc.psum_tensor("a_psDa", [1, 512], F32))
            psDb = ca.enter_context(nc.psum_tensor("a_psDb", [1, 512], F32))
            psB = ca.enter_context(nc.psum_tensor("a_psB", [128, 512], F32))
            psO = ca.enter_context(nc.psum_tensor("a_psO", [128, 2, 1024], F32))

            tb0, sc0, vc0 = TA.n, SC.n, VC.n
            sc_exp, sc_ot = [], []
            n = sc0
            for wp in range(NT):
                n += 1
                sc_exp.append(n)
                n += 3 if wp == NT - 1 else 1
                sc_ot.append(n)
            vc_rec = lambda wp: vc0 + 3 * wp + 2
            vc_mul = lambda wp: vc0 + 3 * wp + 3

            if _DBG_MODE.startswith("a:att:"):
                _stage = _DBG_MODE.split(":")[2]
                _nwp = int(_DBG_MODE.split(":")[3])
            else:
                _stage, _nwp = "av", NT
            _so = {"s": 0, "d": 1, "b": 2, "m": 3, "av": 4}[_stage]

            @blk.tensor
            def _(t_):
                t_.wait_ge(VC.h, vc_qkdup)
                for wp in range(_nwp):
                    t0 = 32 + 128 * wp
                    if wp >= 1:
                        t_.wait_ge(SC.h, sc_exp[wp - 1])
                    for h in range(8):
                        for w in range(2):
                            b = 64 * w
                            ins = t_.matmul(psS[b:b + 64, 64 * h:64 * h + 64],
                                            QKT[:, 8 + h, t0 + b:t0 + b + 64],
                                            QKT[:, h, t0 + b:t0 + b + 64],
                                            start=True, stop=True)
                    TA.inc(ins)
                    if _so < 1:
                        [TA.inc(t_.nop()) for _ in range(3)]
                        continue
                    t_.wait_ge(SC.h, sc_exp[wp])
                    if wp >= 1:
                        t_.wait_ge(VC.h, vc_rec(wp - 1))
                    t_.matmul(psDa[0:1, :], onesb[0:64, 0:1], EA[0:64, :],
                              start=True, stop=True)
                    ins = t_.matmul(psDb[0:1, :], onesb[64:128, 0:1], EA[64:128, :],
                                    start=True, stop=True)
                    TA.inc(ins)
                    if _so < 2:
                        [TA.inc(t_.nop()) for _ in range(2)]
                        continue
                    t_.wait_ge(VC.h, vc_rec(wp))
                    t_.matmul(psB[0:64, :], onesf[0:1, 0:64], recD[0:1, 0:512],
                              start=True, stop=True)
                    ins = t_.matmul(psB[64:128, :], onesf[0:1, 0:64], recD[0:1, 512:1024],
                                    start=True, stop=True)
                    TA.inc(ins)
                    if _so < 4:
                        TA.inc(t_.nop())
                        continue
                    t_.wait_ge(VC.h, vc_mul(wp))
                    t_.wait_ge(VC.h, vc_v(2 * wp + 1))
                    if wp >= 2:
                        t_.wait_ge(SC.h, sc_ot[wp - 2])
                    for h in range(8):
                        for w in range(2):
                            ins = t_.matmul(
                                psO[:, wp % 2, 512 * w + 64 * h:512 * w + 64 * h + 64],
                                Vwa[64 * w:64 * w + 64, wp, 128 * h:128 * (h + 1)],
                                Asb[64 * w:64 * w + 64, 64 * h:64 * h + 64],
                                start=True, stop=True)
                    TA.inc(ins)

            @blk.scalar
            def _(s_):
                s = CE(s_, SC)
                for wp in range(_nwp):
                    t0 = 32 + 128 * wp
                    s.w(TA, tb0 + 4 * wp + 1)
                    if wp >= 1 and _so >= 3:
                        s.w(VC, vc_mul(wp - 1))
                    r = s.activation(EA[:, :], psS[:, :], AF.Exp, scale=ISR128)
                    assert r == sc_exp[wp]
                    if _so < 4:
                        [SC.inc(s.nop()) for _ in range(sc_ot[wp] - r)]
                        continue
                    s.w(TA, tb0 + 4 * wp + 4)
                    src = psO[:, wp % 2, :].rearrange("p (w h t) -> p w h t", w=2, t=64)
                    if t0 + 128 <= TOK:
                        dst = OT[:, :, t0:t0 + 128].rearrange("p h (w t) -> p w h t", w=2)
                        r = s.copy(dst, src)
                    else:
                        s.copy(OT[:, :, t0:t0 + 64], src[:, 0, :, :])
                        s.copy(OT[:, :, t0 + 64:TOK], src[:, 1, :, 0:32])
                        r = s.copy(OT[:, :, 0:32], src[:, 1, :, 32:64])
                    assert r == sc_ot[wp]

            @blk.vector
            def _(v_):
                v = CE(v_, VC)
                for wp in range(_nwp):
                    if _so < 1:
                        continue
                    v.w(TA, tb0 + 4 * wp + 2)
                    v.reciprocal(recD[0:1, 0:512], psDa[0:1, :])
                    r = v.reciprocal(recD[0:1, 512:1024], psDb[0:1, :])
                    assert r == vc_rec(wp)
                    if _so < 3:
                        [VC.inc(v.nop()) for _ in range(vc_mul(wp) - r)]
                        continue
                    v.w(TA, tb0 + 4 * wp + 3)
                    r = v.tensor_mul(Asb[:, :], EA[:, :], psB[:, :])
                    assert r == vc_mul(wp)

            fence(blk, [(SC, SC.n), (TA, TA.n), (VC, VC.n)])
        if _DBG_MODE.startswith("a:att"):
            _dbg_copy_out(p, blk, cn, state, xin, xout)
            return

        with ExitStack() as co:
            WoT = co.enter_context(nc.sbuf_tensor("a_WoT", [128, 8, DM], BF16))
            psW = co.enter_context(nc.psum_tensor("a_psWo", [128, 2, DM], BF16))
            psY = co.enter_context(nc.psum_tensor("a_psY", [128, 2, 512], F32))

            ta_w0, scw = jit_w(p, blk, dr["aw_outw"], 0, 8, cn, state, psW, WoT)

            ta0 = TA.n
            vc0 = VC.n
            gc0 = GC.n
            vc_stt = lambda t: vc0 + 2 * (t + 1)
            gc_add = lambda t: gc0 + t + 1
            xi_val, xo_val, gate = plan_stream(state)

            @blk.sync
            def _(y):
                for t in range(NT):
                    s = t % NS
                    for sem, val in gate[t]:
                        y.wait_ge(sem.h, val)
                    v = XI[s].inc(y.dma_start(out=xsl[:, s, :],
                                              in_=xin[128 * t:128 * (t + 1), :]), 16)
                    assert v == xi_val[t]

            @blk.tensor
            def _(t_):
                t_.wait_ge(SC.h, scw(7))
                i = 0
                for t in range(NT):
                    for vch in range(2):
                        if i >= 2:
                            t_.wait_ge(VC.h, vc0 + (i - 1))
                        for h in range(8):
                            ins = t_.matmul(psY[:, i % 2, :],
                                            OT[:, h, 128 * t:128 * (t + 1)],
                                            WoT[:, h, 512 * vch:512 * (vch + 1)],
                                            start=(h == 0), stop=(h == 7))
                        TA.inc(ins)
                        i += 1

            @blk.vector
            def _(v_):
                v = CE(v_, VC)
                i = 0
                for t in range(NT):
                    v.w(XI[t % NS], xi_val[t])
                    for vch in range(2):
                        v.w(TA, ta0 + i + 1)
                        r = v.scalar_tensor_tensor(
                            xsl[:, t % NS, 512 * vch:512 * (vch + 1)],
                            psY[:, i % 2, :], 2.0,
                            xsl[:, t % NS, 512 * vch:512 * (vch + 1)],
                            ALU.mult, ALU.add)
                        i += 1
                    assert r == vc_stt(t)

            @blk.gpsimd
            def _(g_):
                g = CE(g_, GC)
                for t in range(NT):
                    g.w(VC, vc_stt(t))
                    r = g.tensor_add(xsl[:, t % NS, :], xsl[:, t % NS, :],
                                     borep_a2[:, :])
                    assert r == gc_add(t)

            @blk.scalar
            def _(s_):
                for t in range(NT):
                    s_.wait_ge(GC.h, gc_add(t))
                    v = XO[t % NS].inc(s_.dma_start(out=xout[128 * t:128 * (t + 1), :],
                                                    in_=xsl[:, t % NS, :]), 16)
                    assert v == xo_val[t]

            xow = [(XO[t % NS], xo_val[t]) for t in range(NT - NS, NT)]
            state["phase_done"] = [(TA, TA.n), (VC, VC.n), (GC, GC.n), (SC, SC.n)] + xow
            fence(blk, [(TA, TA.n), (VC, VC.n), (GC, GC.n), (SC, SC.n)] + xow,
                  engines=("tensor", "scalar", "vector", "gpsimd"))


def _dbg_copy_out(p, blk, cn, state, xin, xout):
    XI, XO = state["XI"], state["XO"]
    xsl = cn["xsl"]
    xi_val, xo_val, gate = plan_stream(state)

    @blk.sync
    def _(y):
        for t in range(NT):
            s = t % NS
            for sem, val in gate[t]:
                y.wait_ge(sem.h, val)
            v = XI[s].inc(y.dma_start(out=xsl[:, s, :],
                                      in_=xin[128 * t:128 * (t + 1), :]), 16)
            assert v == xi_val[t]
            y.wait_ge(XI[s].h, xi_val[t])
            v = XO[s].inc(y.dma_start(out=xout[128 * t:128 * (t + 1), :],
                                      in_=xsl[:, s, :]), 16)
            assert v == xo_val[t]


def emit_ffn(p, blk, dr, cn, bcol_fq, bcol_fk, bcol_fo, bcol_f1,
             bcol_fv2, rbrep, fb2n, state, tag, xin, xmid, xout):
    nc = p.nc
    SC, VC, GC, TM, TA = p.sem("SC"), p.sem("VC"), p.sem("GC"), p.sem("TM"), p.sem("TA")
    XI, XO = state["XI"], state["XO"]
    xsl = cn["xsl"]
    onesf2, identf, sel4f = cn["onesf2"], cn["identf"], cn["sel4f"]
    WJ = [p.sem("WJA"), p.sem("WJB")]
    W1 = [p.sem("W1A"), p.sem("W1B")]
    W2 = [p.sem("W2A"), p.sem("W2B")]
    RD = p.sem("RDF")
    FQ = [p.sem(f"FQ{q}") for q in range(4)]
    pdone = state["phase_done"]

    with ExitStack() as c:
        sb = lambda n, sh, dt=F32: c.enter_context(nc.sbuf_tensor(n + "_" + tag, sh, dt))
        mlaT = sb("f_mlaT", [128, 8, 2048], BF16)
        comb = sb("f_comb", [128, NT, 8], F32)
        combT = sb("f_combT", [8, 2048], BF16)
        cA = c.enter_context(ExitStack())
        QuT = cA.enter_context(nc.sbuf_tensor("f_QuT_" + tag, [128, 4, 2048], BF16, side="right"))
        KuT = cA.enter_context(nc.sbuf_tensor("f_KuT_" + tag, [128, 4, 2048], BF16, side="right"))
        Vn = cA.enter_context(nc.sbuf_tensor("f_Vn_" + tag, [128, 16, 528], BF16, side="right"))
        cB = c.enter_context(ExitStack())
        lnT = cB.enter_context(nc.sbuf_tensor("f_lnT_" + tag, [128, 8, TOK], BF16))
        qwu = cB.enter_context(nc.sbuf_tensor("f_qwu_" + tag, [128, 8, 512], BF16))
        cqw = ExitStack()
        qw = cqw.enter_context(nc.sbuf_tensor("f_qw_" + tag, [128, 8, 512], BF16))

        wja0, wjb0 = WJ[0].n, WJ[1].n

        cqw.close()
        kw = cB.enter_context(nc.sbuf_tensor("f_kw_" + tag, [128, 8, 512], BF16))
        vw = cB.enter_context(nc.sbuf_tensor("f_vw_" + tag, [128, 8, 512], BF16))
        psP = cB.enter_context(nc.psum_tensor("f_psP_" + tag, [128, 2, 512], F32))

        ln_all, _, ln_done = emit_ln(p, blk, cn, state, lnT, "ln" + tag, xin, xmid,
                                     dup=0, add_hs=True)
        ln_slotfree = [list(fr) for fr in state["slot_free"]]
        fence(blk, ln_done)

        vc0 = VC.n
        vc_ones = vc0 + 1
        vc_rep = lambda hf, dc: vc0 + 1 + 8 * hf + dc + 1
        vc_vev = lambda kt: vc0 + 17 + kt + 1
        wja_h = [None, None]

        @blk.gpsimd
        def _(g):
            for sem, val in pdone:
                g.wait_ge(sem.h, val)
            for hf in range(2):
                if hf == 1:
                    g.wait_ge(VC.h, vc_rep(0, 7))
                for dc in range(8):
                    WJ[0].inc(g.dma_start(
                        out=qw[:, dc, :],
                        in_=dr["fw_q"][128 * dc:128 * (dc + 1), 512 * hf:512 * (hf + 1)]), 16)
                wja_h[hf] = WJ[0].n

        @blk.gpsimd
        def _(g):
            g.wait_ge(VC.h, vc_rep(1, 7))
            for dc in range(8):
                WJ[1].inc(g.dma_start(out=kw[:, dc, :],
                                      in_=dr["fw_k"][128 * dc:128 * (dc + 1), :]), 16)
                WJ[1].inc(g.dma_start(out=vw[:, dc, :],
                                      in_=dr["fw_v"][128 * dc:128 * (dc + 1), :]), 16)
        wjb_kv = WJ[1].n

        tb0 = TA.n
        ta_qu = lambda j: tb0 + j + 1
        ta_ku = lambda j: tb0 + 16 + j + 1
        ta_v = lambda kt: tb0 + 32 + kt + 1
        sc0 = SC.n
        sc_qu = lambda j: sc0 + j + 1
        sc_ku = lambda j: sc0 + 16 + j + 1

        @blk.vector
        def _(v_):
            v = CE(v_, VC)
            r = v.memset(Vn[:, :, :].rearrange("p k (h u) -> p k h u", u=33)[:, :, :, 32:33], 1.0)
            assert r == vc_ones
            for hf in range(2):
                v.w(WJ[0], wja_h[hf])
                for dc in range(8):
                    r = v.tensor_copy(
                        qwu[:, dc, 256 * hf:256 * (hf + 1)].rearrange("p (h s) -> p h s", s=32),
                        qw[:, dc, :].rearrange("p (h s) -> p h s", s=64)[:, :, 0:32])
                    assert r == vc_rep(hf, dc)
            for kt in range(16):
                v.w(TA, ta_v(kt))
                r = v.tensor_copy(Vn[:, kt, :].rearrange("p (h u) -> p h u", u=33)[:, :, 0:32],
                                  psP[:, kt % 2, :].rearrange("p (h u) -> p h u", u=32))
                assert r == vc_vev(kt)

        @blk.tensor
        def _(t_):
            t_.wait_ge(VC.h, vc_rep(1, 7))
            t_.wait_ge(SC.h, ln_all[1])
            for cc in range(4):
                for tc in range(4):
                    j = cc * 4 + tc
                    if j >= 2:
                        t_.wait_ge(SC.h, sc_qu(j - 2))
                    for dc in range(8):
                        ins = t_.matmul(psP[:, j % 2, :],
                                        qwu[:, dc, 128 * cc:128 * (cc + 1)],
                                        lnT[:, dc, 512 * tc:512 * (tc + 1)],
                                        start=(dc == 0), stop=(dc == 7))
                    TA.inc(ins)
            t_.wait_ge(WJ[1].h, wjb_kv)
            for cc in range(4):
                for tc in range(4):
                    j = cc * 4 + tc
                    t_.wait_ge(SC.h, sc_ku(j - 2) if j >= 2 else sc_qu(15))
                    for dc in range(8):
                        ins = t_.matmul(psP[:, j % 2, :],
                                        kw[:, dc, 128 * cc:128 * (cc + 1)],
                                        lnT[:, dc, 512 * tc:512 * (tc + 1)],
                                        start=(dc == 0), stop=(dc == 7))
                    TA.inc(ins)
            t_.wait_ge(VC.h, vc_ones)
            for kt in range(16):
                if kt >= 2:
                    t_.wait_ge(VC.h, vc_vev(kt - 2))
                else:
                    t_.wait_ge(SC.h, sc_ku(14 + kt))
                for dc in range(8):
                    ins = t_.matmul(psP[:, kt % 2, 0:512],
                                    lnT[:, dc, 128 * kt:128 * (kt + 1)],
                                    vw[:, dc, :],
                                    start=(dc == 0), stop=(dc == 7))
                TA.inc(ins)

        @blk.scalar
        def _(s_):
            s = CE(s_, SC)
            for cc in range(4):
                for tc in range(4):
                    j = cc * 4 + tc
                    s.w(TA, ta_qu(j))
                    r = s.activation(QuT[:, cc, 512 * tc:512 * (tc + 1)],
                                     psP[:, j % 2, :], AF.Identity,
                                     bias=bcol_fq[:, cc:cc + 1])
                    assert r == sc_qu(j)
            for cc in range(4):
                for tc in range(4):
                    j = cc * 4 + tc
                    s.w(TA, ta_ku(j))
                    r = s.activation(KuT[:, cc, 512 * tc:512 * (tc + 1)],
                                     psP[:, j % 2, :], AF.Identity,
                                     bias=bcol_fk[:, cc:cc + 1])
                    assert r == sc_ku(j)

        fence(blk, [(TA, TA.n), (SC, SC.n), (VC, VC.n)])
        cB.close()

        cAO = c.enter_context(ExitStack())
        aoT = cAO.enter_context(nc.sbuf_tensor("f_aoT_" + tag, [128, 4, 2048], BF16))
        cE = cAO.enter_context(ExitStack())
        E2 = cE.enter_context(nc.sbuf_tensor("f_E2_" + tag, [128, 2, 4, 2, 1024], BF16))
        rtmp = cE.enter_context(nc.sbuf_tensor("f_rtmp_" + tag, [128, 2, 512], F32))
        recD = cE.enter_context(nc.sbuf_tensor("f_recD_" + tag, [128, 512], F32))
        o32d = cE.enter_context(nc.sbuf_tensor("f_o32_" + tag, [128, 512], BF16))
        psS = cE.enter_context(nc.psum_tensor("f_psS_" + tag, [128, 6, 512], F32))
        psA = cE.enter_context(nc.psum_tensor("f_psA_" + tag, [128, 2, 512], F32))
        psN = psS[:, 4, :]

        ta0 = TA.n
        sc0 = SC.n
        vc0 = VC.n
        rd0 = RD.n
        ta_sc = {}
        ta_av = {}
        ta_bc = {}
        tn = ta0
        for j in range(16):
            for s4 in range(4):
                gs = 4 * j + s4
                for ktl in range(4):
                    for r in range(4):
                        tn += 1
                        ta_sc[(j, s4, ktl, r)] = tn
                if gs > 0:
                    ja, sa = (gs - 1) // 4, (gs - 1) % 4
                    for ktl in range(4):
                        for pp in range(2):
                            for q2 in range(2):
                                tn += 1
                                ta_av[(ja, sa, ktl, pp, q2)] = tn
                if s4 == 0 and j > 0:
                    tn += 1
                    ta_bc[j - 1] = tn
        for ktl in range(4):
            for pp in range(2):
                for q2 in range(2):
                    tn += 1
                    ta_av[(15, 3, ktl, pp, q2)] = tn
        tn += 1
        ta_bc[15] = tn
        ta_av_last = lambda gs: ta_av[(gs // 4, gs % 4, 3, 1, 1)]
        sc_exp = lambda gk, pp: sc0 + 2 * gk + pp + 1
        vc_it = lambda j, k: vc0 + 10 * j + k
        rd_tot = lambda j: rd0 + 64 * (j + 1)
        slot = lambda gk, pp: (2 * gk + pp) % 3
        s2_sub0_first = {}
        s2_sub0_last = {}
        s2_post_first = {}
        s2_g_last = None
        for j in range(16):
            for kt in range(16):
                for pp2 in range(2):
                    gk2 = 16 * j + kt
                    if slot(gk2, pp2) == 2:
                        if kt < 4:
                            if j not in s2_sub0_first:
                                s2_sub0_first[j] = gk2
                            s2_sub0_last[j] = (gk2, pp2)
                        elif j not in s2_post_first:
                            s2_post_first[j] = gk2
                        if j == 15:
                            s2_g_last = (gk2, pp2)

        @blk.tensor
        def _(t_):
            def em_av(ag):
                ja, sa = ag // 4, ag % 4
                ach, aqc = ja // 4, ja % 4
                for ktl in range(4):
                    kt = 4 * sa + ktl
                    t_.wait_ge(SC.h, sc_exp(16 * ja + kt, 1))
                    if sa == 0 and ktl == 0 and ja > 0:
                        t_.wait_ge(VC.h, vc_it(ja - 1, 8))
                    for pp in range(2):
                        for q2 in range(2):
                            h = 4 * ach + 2 * pp + q2
                            ins = t_.matmul(
                                psA[64 * q2:64 * q2 + 33, pp, :],
                                Vn[:, kt, 33 * h:33 * h + 33],
                                E2[:, ag % 2, ktl, pp, 512 * q2:512 * (q2 + 1)],
                                start=(sa == 0 and ktl == 0),
                                stop=(sa == 3 and ktl == 3),
                                tile_position=(0, 64 * q2),
                                skip_group_check=True)
                            assert TA.inc(ins) == ta_av[(ja, sa, ktl, pp, q2)]

            def em_bc(jb):
                t_.wait_ge(RD.h, rd_tot(jb))
                if jb < 15:
                    t_.wait_ge(SC.h, sc_exp(*s2_sub0_last[jb + 1]))
                else:
                    t_.wait_ge(SC.h, sc_exp(*s2_g_last))
                if jb >= 1:
                    t_.wait_ge(VC.h, vc_it(jb - 1, 9))
                ins = t_.matmul(psN, sel4f[0:4, :], recD[0:4, :],
                                start=True, stop=True)
                assert TA.inc(ins) == ta_bc[jb]

            for j in range(16):
                cch, qc = j // 4, j % 4
                for s4 in range(4):
                    gs = 4 * j + s4
                    for ktl in range(4):
                        kt = 4 * s4 + ktl
                        gk = 16 * j + kt
                        for pp in range(2):
                            prev = 2 * gk + pp - 3
                            if prev >= 0:
                                t_.wait_ge(SC.h, sc_exp(prev // 2, prev % 2))
                        if j > 0 and s2_post_first.get(j) == gk:
                            t_.wait_ge(VC.h, vc_it(j - 1, 9))
                        if j >= 2 and s2_sub0_first.get(j) == gk:
                            t_.wait_ge(VC.h, vc_it(j - 2, 9))
                        for r in range(4):
                            sl = slot(gk, r // 2)
                            ins = t_.matmul(
                                psS[:, 2 * sl + (r % 2), :],
                                KuT[32 * r:32 * r + 32, cch, 128 * kt:128 * (kt + 1)],
                                QuT[32 * r:32 * r + 32, cch, 512 * qc:512 * (qc + 1)],
                                start=True, stop=True, tile_position=(32 * r, 0))
                            assert TA.inc(ins) == ta_sc[(j, s4, ktl, r)]
                    if gs > 0:
                        em_av(gs - 1)
                    if s4 == 0 and j > 0:
                        em_bc(j - 1)
            em_av(63)
            em_bc(15)

        @blk.scalar
        def _(s_):
            s = CE(s_, SC)
            for gs in range(64):
                j, s4 = gs // 4, gs % 4
                for ktl in range(4):
                    gk = 16 * j + 4 * s4 + ktl
                    for pp in range(2):
                        s.w(TA, ta_sc[(j, s4, ktl, 2 * pp + 1)])
                        if gs >= 2 and ktl == 0 and pp == 0:
                            s.w(TA, ta_av_last(gs - 2))
                        sl = slot(gk, pp)
                        r = s.activation(E2[:, gs % 2, ktl, pp, :],
                                         psS[:, 2 * sl:2 * sl + 2, :],
                                         AF.Exp, scale=ISR32)
                        assert r == sc_exp(gk, pp)

        @blk.vector
        def _(v_):
            v = CE(v_, VC)
            for j in range(16):
                cch, qc = j // 4, j % 4
                v.w(TA, ta_av_last(4 * j + 3))
                if j > 0:
                    v.w(RD, rd_tot(j - 1))
                for k in range(4):
                    pp, rr = k // 2, k % 2
                    r = v.reciprocal(rtmp[32 + 64 * rr:33 + 64 * rr, pp, :],
                                     psA[32 + 64 * rr:33 + 64 * rr, pp, :])
                    assert r == vc_it(j, k + 1)
                for k in range(4):
                    r = v.tensor_copy(o32d[32 * k:32 * (k + 1), :],
                                      psA[64 * (k % 2):64 * (k % 2) + 32, k // 2, :])
                    assert r == vc_it(j, k + 5)
                v.w(TA, ta_bc[j])
                r = v.tensor_mul(aoT[:, cch, 512 * qc:512 * (qc + 1)],
                                 o32d[:, :], psN)
                assert r == vc_it(j, 9)
                r = v.tensor_scalar_add(aoT[:, cch, 512 * qc:512 * (qc + 1)],
                                        aoT[:, cch, 512 * qc:512 * (qc + 1)],
                                        bcol_fv2[:, cch:cch + 1])
                assert r == vc_it(j, 10)

        @blk.gpsimd
        def _(g_):
            for j in range(16):
                if j > 0:
                    g_.wait_ge(TA.h, ta_bc[j - 1])
                for k in range(4):
                    pp, rr = k // 2, k % 2
                    g_.wait_ge(VC.h, vc_it(j, k + 1))
                    RD.inc(g_.dma_start(
                        out=recD[k:k + 1, :],
                        in_=rtmp[32 + 64 * rr:33 + 64 * rr, pp, :]), 16)
                assert RD.n == rd_tot(j)

        fence(blk, [(TA, TA.n), (SC, SC.n), (VC, VC.n)])
        cE.close()
        cA.close()

        c4 = c.enter_context(ExitStack())
        ow = c4.enter_context(nc.sbuf_tensor("f_ow_" + tag, [128, 4, 1024], BF16))
        rw = c4.enter_context(nc.sbuf_tensor("f_rw_" + tag, [128, 8, 8], BF16))
        lgT = c4.enter_context(nc.sbuf_tensor("f_lgT_" + tag, [8, 2048], F32))
        lg2 = c4.enter_context(nc.sbuf_tensor("f_lg2_" + tag, [128, 2, 8], F32))
        E8 = c4.enter_context(nc.sbuf_tensor("f_E8_" + tag, [128, 2, 8], F32))
        rsc = c4.enter_context(nc.sbuf_tensor("f_rsc_" + tag, [128, 8], F32))
        psM = c4.enter_context(nc.psum_tensor("f_psM_" + tag, [128, 2, 512], F32))
        psR = c4.enter_context(nc.psum_tensor("f_psR_" + tag, [8, 2, 512], F32))

        wja0 = WJ[0].n

        @blk.gpsimd
        def _(g):
            g.wait_ge(TA.h, TA.n)
            g.wait_ge(VC.h, VC.n)
            for cc in range(4):
                WJ[0].inc(g.dma_start(out=ow[:, cc, :],
                                      in_=dr["fw_o"][128 * cc:128 * (cc + 1), :]), 16)
            WJ[0].inc(g.dma_start(
                out=rw[:, :, :],
                in_=dr["fw_r"].rearrange("(c p) e -> p c e", p=128)), 16)
        wja_o = WJ[0].n

        ta0 = TA.n
        sc0 = SC.n
        vc0 = VC.n
        ta_op = lambda j: ta0 + j + 1
        sc_op = lambda j: sc0 + j + 1
        ta_rt = lambda tc: ta0 + 32 + tc + 1
        sc_rt = lambda tc: sc0 + 32 + tc + 1
        ta_tr = lambda tt: ta0 + 36 + tt + 1
        ta_ct = lambda tt: ta0 + 52 + tt + 1
        sc_e8 = lambda tt: sc0 + 36 + tt + 1
        sc_ct = lambda tt: sc0 + 52 + tt + 1

        @blk.tensor
        def _(t_):
            t_.wait_ge(WJ[0].h, wja_o)
            for dc in range(8):
                for tc in range(4):
                    j = dc * 4 + tc
                    if j >= 2:
                        t_.wait_ge(SC.h, sc_op(j - 2))
                    for cc in range(4):
                        ins = t_.matmul(psM[:, j % 2, :],
                                        ow[:, cc, 128 * dc:128 * (dc + 1)],
                                        aoT[:, cc, 512 * tc:512 * (tc + 1)],
                                        start=(cc == 0), stop=(cc == 3))
                    TA.inc(ins)
            for tc in range(4):
                t_.wait_ge(SC.h, sc_op(31))
                if tc >= 2:
                    t_.wait_ge(SC.h, sc_rt(tc - 2))
                for dc in range(8):
                    ins = t_.matmul(psR[0:8, tc % 2, :], rw[:, dc, :],
                                    mlaT[:, dc, 512 * tc:512 * (tc + 1)],
                                    start=(dc == 0), stop=(dc == 7))
                TA.inc(ins)
            for tt in range(NT):
                t_.wait_ge(SC.h, sc_rt(3))
                if tt >= 2:
                    t_.wait_ge(VC.h, vc0 + 10 * (tt - 2) + 1)
                TA.inc(t_.transpose(psM[:, tt % 2, 0:8], lgT[0:8, 128 * tt:128 * (tt + 1)],
                                    identf[0:8, 0:8]))
            for tt in range(NT):
                t_.wait_ge(VC.h, vc0 + 10 * tt + 10)
                if tt >= 2:
                    t_.wait_ge(SC.h, sc_ct(tt - 2))
                TA.inc(t_.transpose(psR[0:8, tt % 2, 0:128], comb[:, tt, :],
                                    identf[:, :]))

        @blk.scalar
        def _(s_):
            s = CE(s_, SC)
            for dc in range(8):
                for tc in range(4):
                    j = dc * 4 + tc
                    s.w(TA, ta_op(j))
                    r = s.activation(mlaT[:, dc, 512 * tc:512 * (tc + 1)],
                                     psM[:, j % 2, :], AF.Identity,
                                     bias=bcol_fo[:, dc:dc + 1])
                    assert r == sc_op(j)
            for tc in range(4):
                s.w(TA, ta_rt(tc))
                r = s.copy(lgT[0:8, 512 * tc:512 * (tc + 1)], psR[0:8, tc % 2, :])
                assert r == sc_rt(tc)
            for tt in range(NT):
                s.w(VC, vc0 + 10 * tt + 1)
                r = s.activation(E8[:, tt % 2, :], lg2[:, tt % 2, :], AF.Exp,
                                 accum_out=rsc[:, (tt % 2):(tt % 2) + 1])
                assert r == sc_e8(tt)
            for tt in range(NT):
                s.w(TA, ta_ct(tt))
                r = s.copy(combT[0:8, 128 * tt:128 * (tt + 1)], psR[0:8, tt % 2, 0:128])
                assert r == sc_ct(tt)

        @blk.vector
        def _(v_):
            v = CE(v_, VC)
            for tt in range(NT):
                v.w(TA, ta_tr(tt))
                if tt >= 2:
                    v.w(SC, sc_e8(tt - 2))
                r = v.tensor_add(lg2[:, tt % 2, :], psM[:, tt % 2, 0:8], rbrep[:, :])
                assert r == vc0 + 10 * tt + 1
                v.w(SC, sc_e8(tt))
                v.reduce_max(rsc[:, 2:3], E8[:, tt % 2, :], axis=AX.X)
                v.scalar_tensor_tensor(lg2[:, tt % 2, :], E8[:, tt % 2, :], rsc[:, 2:3],
                                       E8[:, tt % 2, :], ALU.is_lt, ALU.mult)
                v.reduce_max(rsc[:, 3:4], lg2[:, tt % 2, :], axis=AX.X)
                v.tensor_scalar(rsc[:, 4:5], rsc[:, (tt % 2):(tt % 2) + 1], 1e-8,
                                rsc[:, 2:3], ALU.mult, ALU.add)
                v.tensor_add(rsc[:, 4:5], rsc[:, 4:5], rsc[:, 3:4])
                v.reciprocal(rsc[:, 5:6], rsc[:, 4:5])
                v.tensor_scalar(rsc[:, 6:7], rsc[:, 5:6], 0.3, None, ALU.mult)
                v.scalar_tensor_tensor(lg2[:, tt % 2, :], E8[:, tt % 2, :], rsc[:, 3:4],
                                       E8[:, tt % 2, :], ALU.is_ge, ALU.mult)
                r = v.tensor_scalar(comb[:, tt, :], lg2[:, tt % 2, :], rsc[:, 6:7],
                                    None, ALU.mult)
                assert r == vc0 + 10 * tt + 10

        fence(blk, [(TA, TA.n), (SC, SC.n), (VC, VC.n)])
        c4.close()
        cAO.close()

        c5 = c.enter_context(ExitStack())
        xful = c5.enter_context(nc.sbuf_tensor("f_xful_" + tag, [128, NT, DM], F32))
        w1sb = c5.enter_context(nc.sbuf_tensor("f_w1_" + tag, [128, 8, 2048], BF16))
        w2sb = c5.enter_context(nc.sbuf_tensor("f_w2_" + tag, [128, 16, 1024], BF16))
        g1T = c5.enter_context(nc.sbuf_tensor("f_g1_" + tag, [128, 16, 512], BF16))
        psH = c5.enter_context(nc.psum_tensor("f_psH_" + tag, [128, 2, 512], F32))
        psG = c5.enter_context(nc.psum_tensor("f_psG_" + tag, [128, 2, 512], F32))

        ta0 = TA.n
        sc0 = SC.n
        vc0 = VC.n
        fq0 = [q.n for q in FQ]
        f3_done = [(TA, TA.n), (SC, SC.n), (VC, VC.n)]

        ta_b2 = lambda j: ta0 + j + 1
        vc_b2 = lambda j: vc0 + j + 1
        def ta_h1(e, tc, fb):
            return ta0 + 32 + e * (4 * (16 + 8)) + tc * (16 + 8) + fb + 1
        def ta_h2(e, tc, j):
            return ta0 + 32 + e * 96 + tc * 24 + 16 + j + 1
        def sc_g1(e, tc, fb):
            return sc0 + e * 64 + tc * 16 + fb + 1
        def vc_h2(e, tc, j):
            return vc0 + 32 + e * 32 + tc * 8 + j + 1
        w1v = [0, 0]
        w2v = [0, 0]
        w1tot = {}
        w2tot = {}

        @blk.sync
        def _(y):
            for sem, val in f3_done:
                y.wait_ge(sem.h, val)
            for fr in ln_slotfree:
                for sem, val in fr:
                    y.wait_ge(sem.h, val)
            for q in range(4):
                src = xmid[512 * q:512 * (q + 1), :].rearrange("(t p) d -> p t d", p=128)
                FQ[q].inc(y.dma_start(out=xful[:, 4 * q:4 * (q + 1), :], in_=src), 16)

        @blk.gpsimd
        def _(g):
            for e in range(8):
                if e == 0:
                    for sem, val in f3_done:
                        g.wait_ge(sem.h, val)
                else:
                    g.wait_ge(TA.h, ta_h1(e - 1, 3, 15))
                for dc in range(8):
                    W1[e % 2].inc(g.dma_start(
                        out=w1sb[:, dc, :],
                        in_=dr["fw_1"][e, 128 * dc:128 * (dc + 1), :]), 16)
                w1tot[e] = W1[e % 2].n
                if e > 0:
                    g.wait_ge(TA.h, ta_h2(e - 1, 3, 7))
                for fb in range(16):
                    W2[e % 2].inc(g.dma_start(
                        out=w2sb[:, fb, :],
                        in_=dr["fw_2"][e, 128 * fb:128 * (fb + 1), :]), 16)
                w2tot[e] = W2[e % 2].n

        @blk.tensor
        def _(t_):
            t_.wait_ge(SC.h, sc0)
            for tt in range(NT):
                for dh in range(2):
                    j = tt * 2 + dh
                    if j >= 2:
                        t_.wait_ge(VC.h, vc_b2(j - 2))
                    TA.inc(t_.matmul(psG[:, j % 2, :], combT[0:8, 128 * tt:128 * (tt + 1)],
                                     fb2n[0:8, 512 * dh:512 * (dh + 1)],
                                     start=True, stop=True))
            for e in range(8):
                t_.wait_ge(W1[e % 2].h, w1tot[e])
                for tc in range(4):
                    for fb in range(16):
                        prev = (e, tc, fb - 2)
                        if fb < 2:
                            prev = (e, tc - 1, 14 + fb) if tc > 0 else \
                                   ((e - 1, 3, 14 + fb) if e > 0 else None)
                        if prev is not None:
                            t_.wait_ge(SC.h, sc_g1(*prev))
                        for dc in range(8):
                            ins = t_.matmul(psH[:, fb % 2, :],
                                            w1sb[:, dc, 128 * fb:128 * (fb + 1)],
                                            mlaT[:, dc, 512 * tc:512 * (tc + 1)],
                                            start=(dc == 0), stop=(dc == 7))
                        TA.inc(ins)
                    if e == 0 and tc == 0:
                        t_.wait_ge(W2[0].h, w2tot[0])
                    elif tc == 0:
                        t_.wait_ge(W2[e % 2].h, w2tot[e])
                    t_.wait_ge(SC.h, sc_g1(e, tc, 15))
                    for tt in range(4):
                        for dh in range(2):
                            j = tt * 2 + dh
                            prev = (e, tc, j - 2)
                            if j < 2:
                                prev = (e, tc - 1, 6 + j) if tc > 0 else \
                                       ((e - 1, 3, 6 + j) if e > 0 else None)
                            if prev is not None:
                                t_.wait_ge(VC.h, vc_h2(*prev))
                            elif j < 2:
                                t_.wait_ge(VC.h, vc_b2(30 + j))
                            for fb in range(16):
                                ins = t_.matmul(psG[:, j % 2, :],
                                                g1T[:, fb, 128 * tt:128 * (tt + 1)],
                                                w2sb[:, fb, 512 * dh:512 * (dh + 1)],
                                                start=(fb == 0), stop=(fb == 15))
                            TA.inc(ins)

        @blk.scalar
        def _(s_):
            s = CE(s_, SC)
            for e in range(8):
                for tc in range(4):
                    for fb in range(16):
                        s.w(TA, ta_h1(e, tc, fb))
                        r = s.activation(g1T[:, fb, :], psH[:, fb % 2, :], AF.Gelu,
                                         bias=bcol_f1[:, e, fb:fb + 1])
                        assert r == sc_g1(e, tc, fb)

        @blk.vector
        def _(v_):
            v = CE(v_, VC)
            for tt in range(NT):
                for dh in range(2):
                    j = tt * 2 + dh
                    v.w(TA, ta_b2(j))
                    v.w(FQ[tt // 4], fq0[tt // 4] + 16)
                    r = v.scalar_tensor_tensor(
                        xful[:, tt, 512 * dh:512 * (dh + 1)],
                        psG[:, j % 2, :], 1.0,
                        xful[:, tt, 512 * dh:512 * (dh + 1)],
                        ALU.mult, ALU.add)
                    assert r == vc_b2(j)
            for e in range(8):
                for tc in range(4):
                    for tt in range(4):
                        for dh in range(2):
                            j = tt * 2 + dh
                            v.w(TA, ta_h2(e, tc, j))
                            t_g = 4 * tc + tt
                            r = v.scalar_tensor_tensor(
                                xful[:, t_g, 512 * dh:512 * (dh + 1)],
                                psG[:, j % 2, :], comb[:, t_g, e:e + 1],
                                xful[:, t_g, 512 * dh:512 * (dh + 1)],
                                ALU.mult, ALU.add)
                            assert r == vc_h2(e, tc, j)

        xo_val = [0] * NT
        sfree = [None] * NS

        @blk.sync
        def _(y):
            for t in range(NT):
                y.wait_ge(VC.h, vc_h2(7, t // 4, (t % 4) * 2 + 1))
                xo_val[t] = XO[t % NS].inc(
                    y.dma_start(out=xout[128 * t:128 * (t + 1), :], in_=xful[:, t, :]), 16)
        for t in range(NT):
            sfree[t % NS] = [(XO[t % NS], xo_val[t])]
        state["slot_free"] = sfree
        xow = [(XO[t % NS], xo_val[t]) for t in range(NT - NS, NT)]
        state["phase_done"] = [(TA, TA.n), (SC, SC.n), (VC, VC.n)] + xow
        fence(blk, [(TA, TA.n), (SC, SC.n), (VC, VC.n)] + xow,
              engines=("tensor", "scalar", "vector", "gpsimd"))


def emit_attn_b(p, blk, dr, cn, bcol_bq, bcol_bk, bcol_bv2, borep_b,
                state, xin, xmid, xout):
    nc = p.nc
    SC, VC, GC, TM, TA = p.sem("SC"), p.sem("VC"), p.sem("GC"), p.sem("TM"), p.sem("TA")
    XI, XO = state["XI"], state["XO"]
    xsl = cn["xsl"]
    onesf2, identb = cn["onesf2"], cn["identb"]
    CB = p.sem("CB")
    WK = p.sem("WK")
    WJ = [p.sem("WJA"), p.sem("WJB")]
    WR = [p.sem("WRA"), p.sem("WRB")]
    WM = [p.sem("WMA"), p.sem("WMB"), p.sem("WMC"), p.sem("WMD")]
    pdone = state["phase_done"]

    with ExitStack() as c:
        sb = lambda n, sh, dt=F32: c.enter_context(nc.sbuf_tensor(n, sh, dt))
        cn["wstage"] = sb("b_wstage", [128, 2, 2048], BF16)
        wstage = cn["wstage"]
        cosb = sb("b_cos", [128, TOK], F32)
        sinb = sb("b_sinm", [128, TOK], F32)
        maskb = sb("b_mask", [128, 4, 512], BF16)
        aoTb = sb("b_aoT", [128, 8, TOK], BF16)
        lnT = sb("b_lnT", [128, 8, TOK], BF16)

        @blk.sync
        def _(y):
            CB.inc(y.dma_start(out=cosb[:, :], in_=dr["c_cos"]), 16)
            CB.inc(y.dma_start(out=sinb[:, :], in_=dr["c_sinm"]), 16)
        cb_tab = CB.n

        @blk.gpsimd
        def _(g):
            for sem, val in pdone:
                g.wait_ge(sem.h, val)
            for o in range(4):
                WK.inc(g.dma_start(out=maskb[:, o, :], in_=dr["c_mask"][o, :, :]), 16)
        wk_msk = WK.n

        ln_all, _, ln_done = emit_ln(p, blk, cn, state, lnT, "b", xin, None,
                                     dup=0, add_hs=False)
        fence(blk, ln_done)

        cw = c.enter_context(ExitStack())
        wpT = cw.enter_context(nc.sbuf_tensor("b_wpT", [128, 3, 8, 128], BF16))
        Qf = cw.enter_context(nc.sbuf_tensor("b_Qf", [128, TOK], F32))
        Kf = cw.enter_context(nc.sbuf_tensor("b_Kf", [128, TOK], F32))
        rot = cw.enter_context(nc.sbuf_tensor("b_rot", [128, TOK], F32))
        rtm = cw.enter_context(nc.sbuf_tensor("b_rtm", [128, TOK], F32))
        Qr = cw.enter_context(nc.sbuf_tensor("b_Qr", [128, TOK], BF16))
        Kr = cw.enter_context(nc.sbuf_tensor("b_Kr", [128, TOK], BF16))
        Vb = cw.enter_context(nc.sbuf_tensor("b_Vb", [128, 16, 130], BF16))
        Eb = cw.enter_context(nc.sbuf_tensor("b_Eb", [128, 2, 16, 512], BF16))
        recRb = cw.enter_context(nc.sbuf_tensor("b_recR", [128, 512], F32))
        o64 = cw.enter_context(nc.sbuf_tensor("b_o64", [128, 2, 512], BF16))
        aostg = cw.enter_context(nc.sbuf_tensor("b_aostg", [128, 4, 512], BF16))
        psW = cw.enter_context(nc.psum_tensor("b_psW", [128, 2, DM], BF16))
        psP = cw.enter_context(nc.psum_tensor("b_psP", [128, 2, 512], F32))
        psS = cw.enter_context(nc.psum_tensor("b_psS", [128, 2, 512], F32))
        psA = cw.enter_context(nc.psum_tensor("b_psA", [128, 512], F32))
        psN = cw.enter_context(nc.psum_tensor("b_psN", [128, 512], F32))

        NKT = lambda qc: 4 * qc + 4
        ta0, sc0, vc0, gc0 = TA.n, SC.n, VC.n, GC.n
        wj0 = [WJ[0].n, WJ[1].n]
        wr0 = [WR[0].n, WR[1].n]
        wm0 = [WM[0].n, WM[1].n]

        TPP = 3 + 12 + 2 * sum(NKT(qc) + 2 for qc in range(4))
        def ta_tr(p_, w3):
            return ta0 + TPP * p_ + w3 + 1
        def ta_qk(p_, qk, tc):
            return ta0 + TPP * p_ + 3 + 4 * qk + tc + 1
        def ta_v(p_, sl):
            return ta0 + TPP * p_ + 11 + sl + 1
        def ta_att0(p_, w, qc):
            base = ta0 + TPP * p_ + 15
            for ww in range(2):
                for q2 in range(4):
                    if (ww, q2) == (w, qc):
                        return base
                    base += NKT(q2) + 2
            return base
        def ta_sc(p_, w, qc, kt):
            return ta_att0(p_, w, qc) + kt + 1
        def ta_av(p_, w, qc):
            return ta_att0(p_, w, qc) + NKT(qc) + 1
        def ta_bc(p_, w, qc):
            return ta_att0(p_, w, qc) + NKT(qc) + 2
        SPP = 3 + 8 + 2 * sum(NKT(qc) + 1 for qc in range(4))
        def sc_w(p_, w3):
            return sc0 + SPP * p_ + w3 + 1
        def sc_qk(p_, qk, tc):
            return sc0 + SPP * p_ + 3 + 4 * qk + tc + 1
        def sc_att0(p_, w, qc):
            base = sc0 + SPP * p_ + 11
            for ww in range(2):
                for q2 in range(4):
                    if (ww, q2) == (w, qc):
                        return base
                    base += NKT(q2) + 1
            return base
        def sc_exp(p_, w, qc, kt):
            return sc_att0(p_, w, qc) + kt + 1
        def sc_o64(p_, w, qc):
            return sc_att0(p_, w, qc) + NKT(qc) + 1
        VOFF = 2
        VPP = 16 + 4 + 2 * sum(4 + 3 for qc in range(4))
        def vc_vev(p_, sl, tt):
            return vc0 + VOFF + VPP * p_ + 4 * sl + tt + 1
        def vc_rope(p_, qk, ph):
            return vc0 + VOFF + VPP * p_ + 16 + 2 * qk + ph + 1
        def vc_att0(p_, w, qc):
            return vc0 + VOFF + VPP * p_ + 20 + 7 * (4 * w + qc)
        def vc_msk(p_, w, qc, j):
            return vc_att0(p_, w, qc) + j + 1
        def vc_rec(p_, w, qc):
            return vc_att0(p_, w, qc) + 5
        def vc_mul(p_, w, qc):
            return vc_att0(p_, w, qc) + 7
        GPP = 2
        def gc_rp(p_, qk):
            return gc0 + GPP * p_ + qk + 1
        wr_val = {}
        wm_val = {}
        wrn = [WR[0].n, WR[1].n]
        wmn = [sem.n for sem in WM]
        wm_prev = {}
        for p_ in range(8):
            for qk in range(2):
                wrn[qk] += 64
                wr_val[(p_, qk)] = wrn[qk]
            for qc in range(4):
                i = p_ * 4 + qc
                wm_prev[i] = wm_val.get((p_ - 1) * 4 + qc) if p_ > 0 else None
                wmn[qc] += 16
                wm_val[i] = wmn[qc]

        @blk.gpsimd
        def _(g_):
            g = CE(g_, GC)
            for p_ in range(8):
                for mp in ([p_ - 1] if p_ > 0 else []):
                    for qc in range(4):
                        i = mp * 4 + qc
                        g_.wait_ge(VC.h, vc_mul(mp, 1, qc))
                        vv = WM[qc].inc(
                            g_.dma_start(out=aoTb[64:128, mp, 512 * qc:512 * (qc + 1)],
                                         in_=aostg[0:64, qc, :]), 16)
                        assert vv == wm_val[i]
                if p_ == 0:
                    g_.wait_ge(VC.h, ln_done[0][1])
                else:
                    g_.wait_ge(TA.h, ta_tr(p_ - 1, 2))
                WJ[0].inc(g_.dma_start(out=wstage[:, 0, 0:1024],
                                       in_=dr["bw_q"][128 * p_:128 * (p_ + 1), :]), 16)
                WJ[0].inc(g_.dma_start(out=wstage[:, 0, 1024:2048],
                                       in_=dr["bw_k"][128 * p_:128 * (p_ + 1), :]), 16)
                WJ[1].inc(g_.dma_start(out=wstage[:, 1, 0:1024],
                                       in_=dr["bw_v"][128 * p_:128 * (p_ + 1), :]), 16)
                for qk in range(2):
                    src = Qf if qk == 0 else Kf
                    rr = rot if qk == 0 else rtm
                    g_.wait_ge(SC.h, sc_qk(p_, qk, 3))
                    if p_ > 0:
                        g_.wait_ge(VC.h, vc_rope(p_ - 1, qk, 1))
                    for piece in range(4):
                        sb_, db_ = [32, 0, 96, 64][piece], [0, 32, 64, 96][piece]
                        WR[qk % 2].inc(g_.dma_start(out=rr[db_:db_ + 32, :],
                                                    in_=src[sb_:sb_ + 32, :]), 16)
                    g.w(WR[qk % 2], wr_val[(p_, qk)])
                    g.w(CB, cb_tab)
                    r = g.tensor_mul(src[:, :], src[:, :], cosb[:, :])
                    assert r == gc_rp(p_, qk)
            for qc in range(4):
                i = 7 * 4 + qc
                g_.wait_ge(VC.h, vc_mul(7, 1, qc))
                vv = WM[qc].inc(
                    g_.dma_start(out=aoTb[64:128, 7, 512 * qc:512 * (qc + 1)],
                                 in_=aostg[0:64, qc, :]), 16)
                assert vv == wm_val[i]

        @blk.tensor
        def _(t_):
            for p_ in range(8):
                for w3 in range(3):
                    sl = 0 if w3 < 2 else 1
                    off = 1024 * (w3 % 2) if w3 < 2 else 0
                    t_.wait_ge(WJ[sl].h, wj0[sl] + (2 if sl == 0 else 1) * 16 * (p_ + 1))
                    if p_ >= 1 or w3 >= 2:
                        pk, pw = (p_, w3 - 2) if w3 >= 2 else (p_ - 1, w3 + 1)
                        t_.wait_ge(SC.h, sc_w(pk, pw))
                    for cc in range(8):
                        ins = t_.transpose(psW[:, w3 % 2, 128 * cc:128 * (cc + 1)],
                                           wstage[:, sl, off + 128 * cc:off + 128 * (cc + 1)],
                                           identb[:, :])
                    TA.inc(ins)
                for qk in range(2):
                    t_.wait_ge(SC.h, sc_w(p_, qk))
                    for tc in range(4):
                        j = 4 * qk + tc
                        if j >= 2:
                            t_.wait_ge(SC.h, sc_qk(p_, (j - 2) // 4, (j - 2) % 4))
                        for dc in range(8):
                            ins = t_.matmul(psP[:, j % 2, :],
                                            wpT[:, qk, dc, :],
                                            lnT[:, dc, 512 * tc:512 * (tc + 1)],
                                            start=(dc == 0), stop=(dc == 7))
                        TA.inc(ins)
                t_.wait_ge(SC.h, sc_w(p_, 2))
                for sl in range(4):
                    if sl >= 2:
                        t_.wait_ge(VC.h, vc_vev(p_, sl - 2, 3))
                    else:
                        if p_ > 0:
                            t_.wait_ge(VC.h, vc_vev(p_ - 1, 2 + sl, 3))
                        t_.wait_ge(SC.h, sc_qk(p_, 1, 2 + sl))
                    for tt in range(4):
                        kt = 4 * sl + tt
                        for dc in range(8):
                            ins = t_.matmul(psP[:, (8 + sl) % 2, 128 * tt:128 * (tt + 1)],
                                            lnT[:, dc, 128 * kt:128 * (kt + 1)],
                                            wpT[:, 2, dc, :],
                                            start=(dc == 0), stop=(dc == 7))
                    TA.inc(ins)
                for w in range(2):
                    b = 64 * w
                    for qc in range(4):
                        nkt = NKT(qc)
                        for kt in range(nkt):
                            pr = None
                            if kt >= 2:
                                pr = sc_exp(p_, w, qc, kt - 2)
                            elif (w, qc) != (0, 0):
                                pw, pq = (w, qc - 1) if qc > 0 else (0, 3)
                                pkt = NKT(pq) - 2 + kt
                                pr = sc_exp(p_, pw, pq, pkt)
                            elif p_ > 0:
                                pr = sc_exp(p_ - 1, 1, 3, 14 + kt)
                            if pr is not None:
                                t_.wait_ge(SC.h, pr)
                            if kt == 0:
                                t_.wait_ge(VC.h, vc_rope(p_, 1, 1))
                            ins = t_.matmul(psS[:, kt % 2, :],
                                            Kr[b:b + 64, 128 * kt:128 * (kt + 1)],
                                            Qr[b:b + 64, 512 * qc:512 * (qc + 1)],
                                            start=True, stop=True)
                            TA.inc(ins)
                        t_.wait_ge(SC.h, sc_exp(p_, w, qc, nkt - 1))
                        t_.wait_ge(VC.h, vc_msk(p_, w, qc, 3))
                        i = p_ * 4 + qc
                        if (w, qc) != (0, 0) or p_ > 0:
                            if qc > 0:
                                pp, pw, pq = p_, w, qc - 1
                            elif w == 1:
                                pp, pw, pq = p_, 0, 3
                            else:
                                pp, pw, pq = p_ - 1, 1, 3
                            t_.wait_ge(SC.h, sc_o64(pp, pw, pq))
                            t_.wait_ge(VC.h, vc_rec(pp, pw, pq))
                        ib = (8 * p_ + 4 * w + qc) % 2
                        for kt in range(nkt):
                            ins = t_.matmul(psA[0:65, :],
                                            Vb[:, kt, 65 * w:65 * w + 65],
                                            Eb[:, ib, kt, :],
                                            start=(kt == 0), stop=(kt == nkt - 1))
                        TA.inc(ins)
                        t_.wait_ge(VC.h, vc_rec(p_, w, qc))
                        TA.inc(t_.matmul(psN[0:64, :], onesf2[64:65, 0:64],
                                         recRb[64:65, :], start=True, stop=True))

        @blk.scalar
        def _(s_):
            s = CE(s_, SC)
            for p_ in range(8):
                for w3 in range(3):
                    s.w(TA, ta_tr(p_, w3))
                    r = s.copy(wpT[:, w3, :, :],
                               psW[:, w3 % 2, :].rearrange("p (c i) -> p c i", i=128))
                    assert r == sc_w(p_, w3)
                for qk in range(2):
                    dst = Qf if qk == 0 else Kf
                    bcol = bcol_bq if qk == 0 else bcol_bk
                    for tc in range(4):
                        s.w(TA, ta_qk(p_, qk, tc))
                        if tc == 0 and p_ > 0:
                            s.w(WR[qk % 2], wr_val[(p_ - 1, qk)])
                            s.w(GC, gc_rp(p_ - 1, qk))
                            s.w(VC, vc_rope(p_ - 1, qk, 1))
                        r = s.activation(dst[:, 512 * tc:512 * (tc + 1)],
                                         psP[:, (4 * qk + tc) % 2, :], AF.Identity,
                                         bias=bcol[:, p_:p_ + 1])
                        assert r == sc_qk(p_, qk, tc)
                for w in range(2):
                    for qc in range(4):
                        nkt = NKT(qc)
                        ib = (8 * p_ + 4 * w + qc) % 2
                        for kt in range(nkt):
                            s.w(TA, ta_sc(p_, w, qc, kt))
                            if kt == 0:
                                G = 8 * p_ + 4 * w + qc
                                if G >= 2:
                                    Gp = G - 2
                                    s.w(TA, ta_av(Gp // 8, (Gp % 8) // 4, Gp % 4))
                            r = s.activation(Eb[:, ib, kt, :], psS[:, kt % 2, :],
                                             AF.Exp, scale=ISR64)
                            assert r == sc_exp(p_, w, qc, kt)
                        s.w(TA, ta_av(p_, w, qc))
                        r = s.copy(o64[0:64, (4 * w + qc) % 2, :], psA[0:64, :])
                        assert r == sc_o64(p_, w, qc)

        @blk.vector
        def _(v_):
            v = CE(v_, VC)
            v.memset(Vb[:, :, 64:65], 1.0)
            v.memset(Vb[:, :, 129:130], 1.0)
            for p_ in range(8):
                for sl in range(4):
                    for tt in range(4):
                        kt = 4 * sl + tt
                        v.w(TA, ta_v(p_, sl))
                        r = v.tensor_copy(
                            Vb[:, kt, :].rearrange("p (w u) -> p w u", u=65)[:, :, 0:64],
                            psP[:, (8 + sl) % 2, 128 * tt:128 * (tt + 1)]
                            .rearrange("p (w u) -> p w u", u=64))
                        assert r == vc_vev(p_, sl, tt)
                for qk in range(2):
                    src = Qf if qk == 0 else Kf
                    rr = rot if qk == 0 else rtm
                    dst = Qr if qk == 0 else Kr
                    v.w(WR[qk % 2], wr_val[(p_, qk)])
                    v.w(GC, gc_rp(p_, qk))
                    r = v.tensor_mul(rr[:, :], rr[:, :], sinb[:, :])
                    assert r == vc_rope(p_, qk, 0)
                    r = v.tensor_add(dst[:, :], src[:, :], rr[:, :])
                    assert r == vc_rope(p_, qk, 1)
                for w in range(2):
                    for qc in range(4):
                        i = p_ * 4 + qc
                        ib = (8 * p_ + 4 * w + qc) % 2
                        for j in range(4):
                            kt = 4 * qc + j
                            v.w(SC, sc_exp(p_, w, qc, kt))
                            v.w(WK, wk_msk)
                            r = v.tensor_mul(Eb[:, ib, kt, :], Eb[:, ib, kt, :],
                                             maskb[:, j, :])
                            assert r == vc_msk(p_, w, qc, j)
                        v.w(TA, ta_av(p_, w, qc))
                        r = v.reciprocal(recRb[64:65, :], psA[64:65, :])
                        assert r == vc_rec(p_, w, qc)
                        v.w(TA, ta_bc(p_, w, qc))
                        v.w(SC, sc_o64(p_, w, qc))
                        if w == 0:
                            v.tensor_mul(aoTb[0:64, p_, 512 * qc:512 * (qc + 1)],
                                         o64[0:64, (4 * w + qc) % 2, :],
                                         psN[0:64, :])
                            r = v.tensor_scalar_add(
                                aoTb[0:64, p_, 512 * qc:512 * (qc + 1)],
                                aoTb[0:64, p_, 512 * qc:512 * (qc + 1)],
                                bcol_bv2[0][0:64, p_:p_ + 1])
                        else:
                            if wm_prev.get(i) is not None:
                                v.w(WM[qc], wm_prev[i])
                            v.tensor_mul(aostg[0:64, qc, :],
                                         o64[0:64, (4 * w + qc) % 2, :],
                                         psN[0:64, :])
                            r = v.tensor_scalar_add(aostg[0:64, qc, :],
                                                    aostg[0:64, qc, :],
                                                    bcol_bv2[1][0:64, p_:p_ + 1])
                        assert r == vc_mul(p_, w, qc)

        fence(blk, [(TA, TA.n), (SC, SC.n), (VC, VC.n), (GC, GC.n)] +
              [(sem, sem.n) for sem in WM])
        cw.close()

        with ExitStack() as co:
            WoT = co.enter_context(nc.sbuf_tensor("b_WoT", [128, 8, DM], BF16))
            psW2 = co.enter_context(nc.psum_tensor("b_psWo", [128, 2, DM], BF16))
            psY = co.enter_context(nc.psum_tensor("b_psY", [128, 2, 512], F32))

            state["wfree"] = None
            state["phase_done"] = [(TA, TA.n), (SC, SC.n), (VC, VC.n)]
            ta_w0, scw = jit_w(p, blk, dr["bw_o"], 0, 8, cn, state, psW2, WoT)

            ta0b = TA.n
            vc0b = VC.n
            gc0b = GC.n
            vc_stt = lambda t: vc0b + 2 * (t + 1)
            gc_add = lambda t: gc0b + t + 1
            xi_val, xo_val, gate = plan_stream(state)

            @blk.sync
            def _(y):
                for t in range(NT):
                    sl = t % NS
                    for sem, val in gate[t]:
                        y.wait_ge(sem.h, val)
                    vv = XI[sl].inc(y.dma_start(out=xsl[:, sl, :],
                                                in_=xin[128 * t:128 * (t + 1), :]), 16)
                    assert vv == xi_val[t]

            @blk.tensor
            def _(t_):
                t_.wait_ge(SC.h, scw(7))
                i = 0
                for t in range(NT):
                    for vch in range(2):
                        if i >= 2:
                            t_.wait_ge(VC.h, vc0b + (i - 1))
                        for hp in range(8):
                            ins = t_.matmul(psY[:, i % 2, :],
                                            aoTb[:, hp, 128 * t:128 * (t + 1)],
                                            WoT[:, hp, 512 * vch:512 * (vch + 1)],
                                            start=(hp == 0), stop=(hp == 7))
                        TA.inc(ins)
                        i += 1

            @blk.vector
            def _(v_):
                v = CE(v_, VC)
                i = 0
                for t in range(NT):
                    v.w(XI[t % NS], xi_val[t])
                    for vch in range(2):
                        v.w(TA, ta0b + i + 1)
                        r = v.scalar_tensor_tensor(
                            xsl[:, t % NS, 512 * vch:512 * (vch + 1)],
                            psY[:, i % 2, :], 1.0,
                            xsl[:, t % NS, 512 * vch:512 * (vch + 1)],
                            ALU.mult, ALU.add)
                        i += 1
                    assert r == vc_stt(t)

            @blk.gpsimd
            def _(g_):
                g = CE(g_, GC)
                for t in range(NT):
                    g.w(VC, vc_stt(t))
                    r = g.tensor_add(xsl[:, t % NS, :], xsl[:, t % NS, :],
                                     borep_b[:, :])
                    assert r == gc_add(t)

            @blk.scalar
            def _(s_):
                for t in range(NT):
                    s_.wait_ge(GC.h, gc_add(t))
                    vv = XO[t % NS].inc(s_.dma_start(out=xout[128 * t:128 * (t + 1), :],
                                                     in_=xsl[:, t % NS, :]), 16)
                    assert vv == xo_val[t]

            xow = [(XO[t % NS], xo_val[t]) for t in range(NT - NS, NT)]
            state["phase_done"] = [(TA, TA.n), (VC, VC.n), (GC, GC.n), (SC, SC.n)] + xow
            fence(blk, [(TA, TA.n), (VC, VC.n), (GC, GC.n), (SC, SC.n)] + xow,
                  engines=("tensor", "scalar", "vector", "gpsimd"))


def make_consts():
    ident = np.eye(128, dtype=np.float32)
    hd = 64
    theta = 1.0 / (10000.0 ** (np.arange(0, hd, 2, dtype=np.float64) / hd))
    pos = np.arange(1, TOK + 1, dtype=np.float64)
    ang = pos[None, :] * theta[:, None]
    cos = np.cos(ang).astype(np.float32)
    sin = np.sin(ang).astype(np.float32)
    cos64 = np.concatenate([cos, cos], axis=0)
    sinm64 = np.concatenate([-sin, sin], axis=0)
    c_cos = np.concatenate([cos64, cos64], axis=0)
    c_sinm = np.concatenate([sinm64, sinm64], axis=0)
    masks = np.zeros((4, 128, 512), dtype=np.float32)
    for o in range(4):
        kk = np.arange(128)[:, None] + 128 * o
        qq = np.arange(512)[None, :]
        masks[o] = (kk <= qq).astype(np.float32)
    sel4 = np.zeros((4, 128), dtype=np.float32)
    for k in range(4):
        sel4[k, 32 * k:32 * k + 32] = 1.0
    return ident, c_cos, c_sinm, masks, sel4


def kernel(L0_attn_a_x, L0_attn_a_in_proj_w, L0_attn_a_in_proj_b, L0_attn_a_out_w,
           L0_attn_a_out_b, L0_attn_b_Wq, L0_attn_b_bq, L0_attn_b_Wk, L0_attn_b_bk,
           L0_attn_b_Wv, L0_attn_b_bv, L0_attn_b_Wo, L0_attn_b_bo, L0_ffn_q_w,
           L0_ffn_q_b, L0_ffn_k_w, L0_ffn_k_b, L0_ffn_v_w, L0_ffn_v_b, L0_ffn_o_w,
           L0_ffn_o_b, L0_ffn_router_w, L0_ffn_router_b, L0_ffn_w1, L0_ffn_b1,
           L0_ffn_w2, L0_ffn_b2, _upto=4):
    x = np.ascontiguousarray(np.asarray(L0_attn_a_x, dtype=np.float32))
    ident, c_cos, c_sinm, masks, sel4 = make_consts()
    f32c = lambda a: np.ascontiguousarray(np.asarray(a, dtype=np.float32))
    row = lambda a: np.ascontiguousarray(np.asarray(a, dtype=np.float32).reshape(1, -1))
    shared = {
        "aw_inproj": f32c(L0_attn_a_in_proj_w), "aw_inb": row(L0_attn_a_in_proj_b),
        "aw_outw": f32c(L0_attn_a_out_w), "aw_outb": row(L0_attn_a_out_b),
        "bw_q": f32c(L0_attn_b_Wq), "bw_k": f32c(L0_attn_b_Wk),
        "bw_v": f32c(L0_attn_b_Wv), "bw_o": f32c(L0_attn_b_Wo),
        "bb_q": row(L0_attn_b_bq), "bb_k": row(L0_attn_b_bk),
        "bb_v": row(L0_attn_b_bv), "bb_o": row(L0_attn_b_bo),
        "fw_q": f32c(L0_ffn_q_w), "fw_k": f32c(L0_ffn_k_w),
        "fw_v": f32c(L0_ffn_v_w), "fw_o": f32c(L0_ffn_o_w),
        "fb_q": row(L0_ffn_q_b), "fb_k": row(L0_ffn_k_b),
        "fb_v": row(L0_ffn_v_b), "fb_o": row(L0_ffn_o_b),
        "fw_r": f32c(L0_ffn_router_w), "fb_r": row(L0_ffn_router_b),
        "fw_1": f32c(L0_ffn_w1), "fb_1": f32c(L0_ffn_b1),
        "fw_2": f32c(L0_ffn_w2), "fb_2": f32c(L0_ffn_b2),
        "c_ident": ident, "c_cos": c_cos, "c_sinm": c_sinm, "c_mask": masks,
        "c_sel4": sel4,
    }
    nc = build_program(upto=_upto)
    in_maps = []
    for i in range(NC8):
        m = dict(shared)
        m["x"] = np.ascontiguousarray(x[i])
        in_maps.append(m)
    res = run_bass_kernel_spmd(nc, in_maps, core_ids=list(range(NC8)))
    return np.stack([np.asarray(r["out"], dtype=np.float32) for r in res.results], axis=0)
